# version 02
# baseline (speedup 1.0000x reference)
import math
import contextlib
import numpy as np
import ml_dtypes
import concourse.bass as bass
import concourse.mybir as mybir
from concourse.bass_utils import run_bass_kernel_spmd

F32 = mybir.dt.float32
BF16 = mybir.dt.bfloat16
AF = mybir.ActivationFunctionType
ALU = mybir.AluOpType
AX = mybir.AxisListType

NCORES = 8
T = 2048
D = 1024
NT = T // 512
ND = D // 128
BF = ml_dtypes.bfloat16

CONST_VALS = [0.0, 1.0, -1.0, 0.5, 7.0, -7.0, 1e-5,
              float(np.float32(np.finfo(np.float32).eps)), 1e-12, 1.0 / 1024.0]
CCOL = {v: i for i, v in enumerate(CONST_VALS)}


def host_constants():
    c = {}
    c["c_ident"] = np.ascontiguousarray(np.eye(128, dtype=np.float32).astype(BF))
    c["c_ones"] = np.ascontiguousarray(np.ones((128, 128), dtype=np.float32).astype(BF))
    i = np.arange(128)
    c["c_trimask"] = np.where(i[None, :] <= i[:, None], 0.0, -1e30).astype(np.float32)
    c["c_identf"] = np.eye(128, dtype=np.float32)
    c["c_cols"] = np.tile(np.array(CONST_VALS, dtype=np.float32)[None, :], (128, 1))
    return c


def body_sem_clear_range(nc):
    kr = bass.get_kernel_semaphore_range()
    skip = 2
    if nc._bir_kernel_barrier_sem is not None:
        skip += 1
    skip += len(nc._monotonic_sems)
    return range(kr.start + skip, kr.stop)


class Sem:
    def __init__(self, ctx, name):
        self.h = ctx.es.enter_context(ctx.nc.semaphore(name))
        self.v = 0


class Ctx:

    def __init__(self, nc):
        self.nc = nc
        self.es = contextlib.ExitStack()
        self.clear_rng = body_sem_clear_range(nc)
        self.boot = Sem(self, "boot")
        self.wdma = Sem(self, "wdma")
        self.sio = Sem(self, "sio")
        self.tdr = Sem(self, "tdr")
        self.vdr = Sem(self, "vdr")
        self.sdr = Sem(self, "sdr")
        self.gdr = Sem(self, "gdr")
        self.vch = Sem(self, "vch")
        self.sch = Sem(self, "sch")
        self.evk = Sem(self, "evk")
        self.st = Sem(self, "st")
        self.aux = Sem(self, "aux")
        self.aux2 = Sem(self, "aux2")

    def sb(self, name, shape, dtype):
        return self.es.enter_context(self.nc.sbuf_tensor(name, shape, dtype))

    def psum(self, name, shape, dtype=F32):
        return self.es.enter_context(self.nc.psum_tensor(name, shape, dtype))


def emit_boot(ctx, block):
    @block.gpsimd
    def _(g):
        g.sem_clear(ctx.clear_rng)
        g.nop().then_inc(ctx.boot.h, 1)
    ctx.boot.v = 1
    block.tensor(lambda e: e.wait_ge(ctx.boot.h, 1))
    block.vector(lambda e: e.wait_ge(ctx.boot.h, 1))
    block.scalar(lambda e: e.wait_ge(ctx.boot.h, 1))
    block.sync(lambda e: e.wait_ge(ctx.boot.h, 1))



def emit_linear_fm(ctx, block, C, F, w_ap, src_fn, evac_fn, wslots, pbanks,
                   Tloc=T, fb_per_dma=2, extra_tensor_wait=None):
    ncc = C // 128
    nfb = F // 128
    ntc = (Tloc + 511) // 512
    tcs = min(512, Tloc)
    nslot = len(wslots)
    ngrp = (nfb + fb_per_dma - 1) // fb_per_dma
    assert len(pbanks) >= 2 * ntc

    wbase = ctx.wdma.v
    tbase = ctx.tdr.v
    ebase = ctx.evk.v

    @block.gpsimd
    def _(g):
        for gi in range(ngrp):
            f0 = gi * fb_per_dma * 128
            fw = min(fb_per_dma * 128, F - f0)
            slot = wslots[gi % nslot]
            if gi >= nslot:
                g.wait_ge(ctx.tdr.h, tbase + (gi - nslot + 1))
            g.dma_start(
                out=slot[:, 0:ncc * fw].rearrange("c (k m) -> c k m", m=fw),
                in_=w_ap[:, f0:f0 + fw].rearrange("(k c) m -> c k m", c=128),
            ).then_inc(ctx.wdma.h, 16)
    ctx.wdma.v = wbase + 16 * ngrp

    @block.tensor
    def _(t):
        if extra_tensor_wait is not None:
            extra_tensor_wait(t)
        for gi in range(ngrp):
            t.wait_ge(ctx.wdma.h, wbase + 16 * (gi + 1))
            f0 = gi * fb_per_dma * 128
            fw = min(fb_per_dma * 128, F - f0)
            slot = wslots[gi % nslot]
            for fb in range(gi * fb_per_dma, min((gi + 1) * fb_per_dma, nfb)):
                foff = fb * 128 - f0
                pb0 = (fb % 2) * ntc
                if fb >= 2:
                    t.wait_ge(ctx.evk.h, ebase + (fb - 2) + 1)
                for ti in range(ntc):
                    psb = pbanks[pb0 + ti]
                    for cc in range(ncc):
                        t.matmul(psb[:, 0:tcs],
                                 slot[:, cc * fw + foff: cc * fw + foff + 128],
                                 src_fn(cc)[:, ti * tcs:(ti + 1) * tcs],
                                 start=(cc == 0), stop=(cc == ncc - 1))
            ctx.tdr.v += 1
            t.drain().then_inc(ctx.tdr.h, 1)

    @block.scalar
    def _(s):
        for gi in range(ngrp):
            s.wait_ge(ctx.tdr.h, tbase + gi + 1)
            for fb in range(gi * fb_per_dma, min((gi + 1) * fb_per_dma, nfb)):
                pb0 = (fb % 2) * ntc
                for ti in range(ntc):
                    evac_fn(s, fb, ti, pbanks[pb0 + ti][:, 0:tcs])
                ctx.evk.v += 1
                s.drain().then_inc(ctx.evk.h, 1)



def emit_norm(ctx, block, stream, xhat, sq_scratch, rowbuf, ab_rows, bcast,
              pbanks, ones_bf, ccols, kind="ln", gamma_cols=None, src_bf=None):
    is_ln = kind == "ln"

    if is_ln:
        @block.vector
        def _(v):
            for k in range(ND):
                v.tensor_copy(xhat[:, k * T:(k + 1) * T], stream[:, k * T:(k + 1) * T])
            ctx.vdr.v += 1
            v.drain().then_inc(ctx.vdr.h, 1)
        xsrc = xhat
    else:
        xsrc = src_bf
    x_rdy = ctx.vdr.v

    @block.scalar
    def _(s):
        if is_ln:
            s.wait_ge(ctx.vdr.h, x_rdy)
        for k in range(ND):
            s.activation(sq_scratch[:, k * T:(k + 1) * T], xsrc[:, k * T:(k + 1) * T], AF.Square)
        ctx.sdr.v += 1
        s.drain().then_inc(ctx.sdr.h, 1)
    sq_rdy = ctx.sdr.v

    @block.tensor
    def _(t):
        t.wait_ge(ctx.sdr.h, sq_rdy)
        srcs = ((0, xsrc), (1, sq_scratch)) if is_ln else ((1, sq_scratch),)
        for which, buf in srcs:
            for ti in range(NT):
                pb = pbanks[which * NT + ti]
                for k in range(ND):
                    t.matmul(pb[0:1, 0:512], ones_bf[:, 0:1],
                             buf[:, k * T + ti * 512: k * T + ti * 512 + 512],
                             start=(k == 0), stop=(k == ND - 1))
        ctx.tdr.v += 1
        t.drain().then_inc(ctx.tdr.h, 1)
    sums_rdy = ctx.tdr.v

    @block.vector
    def _(v):
        vb = ctx.vch
        def hs(inst):
            vb.v += 1
            inst.then_inc(vb.h, 1)
            v.wait_ge(vb.h, vb.v)
        v.wait_ge(ctx.tdr.h, sums_rdy)
        for ti in range(NT):
            if is_ln:
                hs(v.tensor_copy(rowbuf[0:1, ti * 512:(ti + 1) * 512], pbanks[ti][0:1, 0:512]))
            hs(v.tensor_copy(rowbuf[0:1, T + ti * 512: T + (ti + 1) * 512],
                             pbanks[NT + ti][0:1, 0:512]))
        if is_ln:
            hs(v.tensor_scalar_mul(rowbuf[0:1, 2 * T:3 * T], rowbuf[0:1, 0:T], 1.0 / 1024.0))
            hs(v.tensor_scalar_mul(rowbuf[0:1, 3 * T:4 * T], rowbuf[0:1, T:2 * T], 1.0 / 1024.0))
            hs(v.tensor_mul(rowbuf[0:1, 0:T], rowbuf[0:1, 2 * T:3 * T], rowbuf[0:1, 2 * T:3 * T]))
            hs(v.tensor_sub(rowbuf[0:1, 3 * T:4 * T], rowbuf[0:1, 3 * T:4 * T], rowbuf[0:1, 0:T]))
        else:
            hs(v.tensor_scalar_mul(rowbuf[0:1, 3 * T:4 * T], rowbuf[0:1, T:2 * T], 1.0 / 1024.0))
        ctx.vdr.v += 1
        v.drain().then_inc(ctx.vdr.h, 1)
    var_rdy = ctx.vdr.v

    eps_col = ccols[:, CCOL[1e-5]:CCOL[1e-5] + 1] if is_ln else \
        ccols[:, CCOL[float(np.float32(np.finfo(np.float32).eps))]:
              CCOL[float(np.float32(np.finfo(np.float32).eps))] + 1]

    @block.scalar
    def _(s):
        sc = ctx.sch
        def hs(inst):
            sc.v += 1
            inst.then_inc(sc.h, 1)
            s.wait_ge(sc.h, sc.v)
        s.wait_ge(ctx.vdr.h, var_rdy)
        hs(s.activation(rowbuf[0:1, T:2 * T], rowbuf[0:1, 3 * T:4 * T], AF.Ln,
                        bias=eps_col[0:1, :]))
        hs(s.activation(rowbuf[0:1, 3 * T:4 * T], rowbuf[0:1, T:2 * T], AF.Exp, scale=-0.5))
        hs(s.activation(ab_rows[0:1, 0:T], rowbuf[0:1, 3 * T:4 * T], AF.Copy))
        ctx.sdr.v += 1
        s.drain().then_inc(ctx.sdr.h, 1)
    rstd_rdy = ctx.sdr.v

    if is_ln:
        @block.vector
        def _(v):
            v.wait_ge(ctx.sdr.h, rstd_rdy)
            v.tensor_mul(ab_rows[0:1, T:2 * T], rowbuf[0:1, 2 * T:3 * T],
                         rowbuf[0:1, 3 * T:4 * T])
            ctx.vdr.v += 1
            v.drain().then_inc(ctx.vdr.h, 1)
    ab_rdy = ctx.vdr.v if is_ln else None

    @block.tensor
    def _(t):
        t.wait_ge(ctx.sdr.h, rstd_rdy)
        if is_ln:
            t.wait_ge(ctx.vdr.h, ab_rdy)
        nrow = 2 if is_ln else 1
        for which in range(nrow):
            for ti in range(NT):
                pb = pbanks[which * NT + ti]
                t.matmul(pb[:, 0:512], ones_bf[0:1, 0:128],
                         ab_rows[0:1, which * T + ti * 512: which * T + ti * 512 + 512],
                         start=True, stop=True)
        ctx.tdr.v += 1
        t.drain().then_inc(ctx.tdr.h, 1)
    bc_rdy = ctx.tdr.v

    @block.vector
    def _(v):
        v.wait_ge(ctx.tdr.h, bc_rdy)
        nrow = 2 if is_ln else 1
        for which in range(nrow):
            for ti in range(NT):
                v.tensor_copy(bcast[:, which * T + ti * 512: which * T + ti * 512 + 512],
                              pbanks[which * NT + ti][:, 0:512])
        ctx.vdr.v += 1
        v.drain().then_inc(ctx.vdr.h, 1)

        v.wait_ge(ctx.vdr.h, ctx.vdr.v)
        for k in range(ND):
            xk = xhat[:, k * T:(k + 1) * T]
            sk = xsrc[:, k * T:(k + 1) * T]
            v.tensor_mul(xk, sk, bcast[:, 0:T])
            if is_ln:
                v.tensor_sub(xk, xk, bcast[:, T:2 * T])
            else:
                v.tensor_scalar_mul(xk, xk, gamma_cols[:, k:k + 1])
        ctx.vdr.v += 1
        v.drain().then_inc(ctx.vdr.h, 1)
    return ctx.vdr.v



def fence(ctx, block, engines=("tensor", "vector", "scalar", "gpsimd", "sync")):
    tv, vv, sv, ev = ctx.tdr.v, ctx.vdr.v, ctx.sdr.v, ctx.evk.v
    gv, iov, stv = ctx.gdr.v, ctx.sio.v, ctx.st.v

    def mk(e):
        if tv: e.wait_ge(ctx.tdr.h, tv)
        if vv: e.wait_ge(ctx.vdr.h, vv)
        if sv: e.wait_ge(ctx.sdr.h, sv)
        if ev: e.wait_ge(ctx.evk.h, ev)
        if gv: e.wait_ge(ctx.gdr.h, gv)
        if iov: e.wait_ge(ctx.sio.h, iov)
        if stv: e.wait_ge(ctx.st.h, stv)
    for name in engines:
        getattr(block, name)(mk)



def emit_stream_ln(ctx, block, stream_hbm, schunk, xhat, sq_scratch, rowbuf,
                   ab_rows, bcast, pbanks, ones_bf, ccols):
    sio_base = ctx.sio.v
    vdr_base = ctx.vdr.v

    @block.sync
    def _(sy):
        for k in range(ND):
            if k >= 1:
                sy.wait_ge(ctx.vdr.h, vdr_base + k)
            sy.dma_start(out=schunk[:, 0:T],
                         in_=stream_hbm[k * 128:(k + 1) * 128, :]).then_inc(ctx.sio.h, 16)
    ctx.sio.v = sio_base + 16 * ND

    @block.vector
    def _(v):
        for k in range(ND):
            v.wait_ge(ctx.sio.h, sio_base + 16 * (k + 1))
            v.tensor_copy(xhat[:, k * T:(k + 1) * T], schunk[:, 0:T])
            ctx.vdr.v += 1
            v.drain().then_inc(ctx.vdr.h, 1)

    sdr_base = ctx.sdr.v
    @block.scalar
    def _(s):
        for k in range(ND):
            s.wait_ge(ctx.vdr.h, vdr_base + (k + 1))
            s.activation(sq_scratch[:, 0:T], xhat[:, k * T:(k + 1) * T], AF.Square)
            ctx.sdr.v += 1
            s.drain().then_inc(ctx.sdr.h, 1)

    @block.tensor
    def _(t):
        for k in range(ND):
            t.wait_ge(ctx.vdr.h, vdr_base + (k + 1))
            for ti in range(NT):
                t.matmul(pbanks[ti][0:1, 0:512], ones_bf[:, 0:1],
                         xhat[:, k * T + ti * 512: k * T + ti * 512 + 512],
                         start=(k == 0), stop=(k == ND - 1))
            t.wait_ge(ctx.sdr.h, sdr_base + (k + 1))
            for ti in range(NT):
                t.matmul(pbanks[NT + ti][0:1, 0:512], ones_bf[:, 0:1],
                         sq_scratch[:, ti * 512: ti * 512 + 512],
                         start=(k == 0), stop=(k == ND - 1))
        ctx.tdr.v += 1
        t.drain().then_inc(ctx.tdr.h, 1)
    sums_rdy = ctx.tdr.v

    _emit_ln_tail(ctx, block, xhat, rowbuf, ab_rows, bcast, pbanks, ones_bf, ccols,
                  sums_rdy, is_ln=True, xsrc=xhat)
    return ctx.vdr.v


def _emit_ln_tail(ctx, block, xhat, rowbuf, ab_rows, bcast, pbanks, ones_bf, ccols,
                  sums_rdy, is_ln=True, xsrc=None, gamma_cols=None):
    eps_val = 1e-5 if is_ln else float(np.float32(np.finfo(np.float32).eps))
    s0 = rowbuf[0:1, 0:512]
    s1 = rowbuf[0:1, 512:1024]
    t2 = rowbuf[0:1, 1024:1536]

    for ti in range(NT):
        @block.vector
        def _(v, ti=ti):
            vb = ctx.vch
            def hs(inst):
                vb.v += 1
                inst.then_inc(vb.h, 1)
                v.wait_ge(vb.h, vb.v)
            if ti == 0:
                v.wait_ge(ctx.tdr.h, sums_rdy)
            if is_ln:
                hs(v.tensor_copy(s0, pbanks[ti][0:1, 0:512]))
            hs(v.tensor_copy(s1, pbanks[NT + ti][0:1, 0:512]))
            if is_ln:
                hs(v.tensor_scalar_mul(s0, s0, 1.0 / 1024.0))
                hs(v.tensor_mul(t2, s0, s0))
                hs(v.tensor_scalar_mul(s1, s1, 1.0 / 1024.0))
                hs(v.tensor_sub(s1, s1, t2))
            else:
                hs(v.tensor_scalar_mul(s1, s1, 1.0 / 1024.0))
            ctx.vdr.v += 1
            v.drain().then_inc(ctx.vdr.h, 1)
        var_rdy = ctx.vdr.v

        @block.scalar
        def _(s, ti=ti, var_rdy=var_rdy):
            sc = ctx.sch
            def hs(inst):
                sc.v += 1
                inst.then_inc(sc.h, 1)
                s.wait_ge(sc.h, sc.v)
            s.wait_ge(ctx.vdr.h, var_rdy)
            hs(s.activation(s1, s1, AF.Ln, bias=ccols[0:1, CCOL[eps_val]:CCOL[eps_val] + 1]))
            hs(s.activation(s1, s1, AF.Exp, scale=-0.5))
            hs(s.activation(ab_rows[0:1, ti * 512:(ti + 1) * 512], s1, AF.Copy))
            ctx.sdr.v += 1
            s.drain().then_inc(ctx.sdr.h, 1)
        rstd_rdy = ctx.sdr.v

        @block.vector
        def _(v, ti=ti, rstd_rdy=rstd_rdy):
            v.wait_ge(ctx.sdr.h, rstd_rdy)
            if is_ln:
                v.tensor_mul(ab_rows[0:1, T + ti * 512: T + (ti + 1) * 512], s0, s1)
            ctx.vdr.v += 1
            v.drain().then_inc(ctx.vdr.h, 1)
    ab_rdy = ctx.vdr.v

    @block.tensor
    def _(t):
        t.wait_ge(ctx.vdr.h, ab_rdy)
        nrow = 2 if is_ln else 1
        for which in range(nrow):
            for ti in range(NT):
                t.matmul(pbanks[which * NT + ti][:, 0:512], ones_bf[0:1, 0:128],
                         ab_rows[0:1, which * T + ti * 512: which * T + ti * 512 + 512],
                         start=True, stop=True)
        ctx.tdr.v += 1
        t.drain().then_inc(ctx.tdr.h, 1)
    bc_rdy = ctx.tdr.v

    @block.vector
    def _(v):
        v.wait_ge(ctx.tdr.h, bc_rdy)
        nrow = 2 if is_ln else 1
        for which in range(nrow):
            for ti in range(NT):
                v.tensor_copy(bcast[:, which * T + ti * 512: which * T + ti * 512 + 512],
                              pbanks[which * NT + ti][:, 0:512])
        ctx.vdr.v += 1
        v.drain().then_inc(ctx.vdr.h, 1)
        v.wait_ge(ctx.vdr.h, ctx.vdr.v)
        for k in range(ND):
            xk = xhat[:, k * T:(k + 1) * T]
            sk = xsrc[:, k * T:(k + 1) * T]
            v.tensor_mul(xk, sk, bcast[:, 0:T])
            if is_ln:
                v.tensor_sub(xk, xk, bcast[:, T:2 * T])
            else:
                v.tensor_scalar_mul(xk, xk, gamma_cols[:, k:k + 1])
        ctx.vdr.v += 1
        v.drain().then_inc(ctx.vdr.h, 1)



def emit_l0_ffn(ctx, block, xhat, wu_ap, h1_ap, h2_ap, w3_ap, scale_cols, u_tmp,
                bufA, bufB, wslots_small, wslots_big, pbanks, stream_hbm, schunk_res):
    Tc = 1024
    for tchunk in range(2):
        t0 = tchunk * Tc

        def src_x(cc, t0=t0):
            return xhat[:, cc * T + t0: cc * T + t0 + Tc]

        fence(ctx, block)
        _emit_u_stage(ctx, block, wu_ap, src_x, scale_cols, u_tmp, bufA,
                      wslots_small, pbanks, Tc)

        fence(ctx, block)
        def evac_h1(s, fb, ti, ps):
            s.activation(bufB[:, fb * Tc + ti * 512: fb * Tc + ti * 512 + 512],
                         ps, AF.Gelu, scale=scale_cols[:, 64 + fb:64 + fb + 1])
        emit_linear_fm(ctx, block, 4096, 4096, h1_ap,
                       lambda cc: bufA[:, cc * Tc:(cc + 1) * Tc],
                       evac_h1, wslots_big, pbanks[:4], Tloc=Tc, fb_per_dma=1)

        fence(ctx, block)
        def evac_h2(s, fb, ti, ps):
            s.activation(bufA[:, fb * Tc + ti * 512: fb * Tc + ti * 512 + 512],
                         ps, AF.Gelu, scale=scale_cols[:, 96 + fb:96 + fb + 1])
        emit_linear_fm(ctx, block, 4096, 4096, h2_ap,
                       lambda cc: bufB[:, cc * Tc:(cc + 1) * Tc],
                       evac_h2, wslots_big, pbanks[:4], Tloc=Tc, fb_per_dma=1)

        fence(ctx, block)
        _emit_w3_residual(ctx, block, w3_ap, bufA, scale_cols, wslots_big, pbanks,
                          stream_hbm, schunk_res, Tc, t0)
    fence(ctx, block)


def _emit_u_stage(ctx, block, wu_ap, src_x, scale_cols, u_tmp, out_u, wslots, pbanks, Tc):
    ncc, nfb = 8, 64
    ntc = Tc // 512
    nslot = len(wslots)
    ngrp = nfb // 2
    wbase, tbase, ebase, sbase = ctx.wdma.v, ctx.tdr.v, ctx.evk.v, ctx.sdr.v

    @block.gpsimd
    def _(g):
        for gi in range(ngrp):
            f0 = gi * 256
            slot = wslots[gi % nslot]
            if gi >= nslot:
                g.wait_ge(ctx.tdr.h, tbase + (gi - nslot + 1))
            g.dma_start(out=slot[:, 0:ncc * 256].rearrange("c (k m) -> c k m", m=256),
                        in_=wu_ap[:, f0:f0 + 256].rearrange("(k c) m -> c k m", c=128),
                        ).then_inc(ctx.wdma.h, 16)
    ctx.wdma.v = wbase + 16 * ngrp

    @block.tensor
    def _(t):
        for gi in range(ngrp):
            t.wait_ge(ctx.wdma.h, wbase + 16 * (gi + 1))
            slot = wslots[gi % nslot]
            if gi >= 1:
                t.wait_ge(ctx.evk.h, ebase + gi)
            for half in range(2):
                foff = half * 128
                for ti in range(ntc):
                    psb = pbanks[half * ntc + ti]
                    for cc in range(ncc):
                        t.matmul(psb[:, 0:512],
                                 slot[:, cc * 256 + foff: cc * 256 + foff + 128],
                                 src_x(cc)[:, ti * 512:(ti + 1) * 512],
                                 start=(cc == 0), stop=(cc == ncc - 1))
            ctx.tdr.v += 1
            t.drain().then_inc(ctx.tdr.h, 1)

    @block.scalar
    def _(s):
        for gi in range(ngrp):
            s.wait_ge(ctx.tdr.h, tbase + gi + 1)
            for ti in range(ntc):
                s.activation(u_tmp[:, ti * 512:(ti + 1) * 512], pbanks[ti][:, 0:512],
                             AF.Silu, scale=scale_cols[:, gi:gi + 1])
            ctx.sdr.v += 1
            s.drain().then_inc(ctx.sdr.h, 1)

    @block.vector
    def _(v):
        for gi in range(ngrp):
            v.wait_ge(ctx.sdr.h, sbase + gi + 1)
            for ti in range(ntc):
                v.scalar_tensor_tensor(
                    out_u[:, gi * Tc + ti * 512: gi * Tc + ti * 512 + 512],
                    pbanks[ntc + ti][:, 0:512], scale_cols[:, 32 + gi:32 + gi + 1],
                    u_tmp[:, ti * 512:(ti + 1) * 512], ALU.mult, ALU.mult)
            ctx.evk.v += 1
            v.drain().then_inc(ctx.evk.h, 1)


def _emit_w3_residual(ctx, block, w3_ap, src_buf, scale_cols, wslots, pbanks,
                      stream_hbm, schunk_res, Tc, t0):
    ncc, nfb = 32, 8
    ntc = Tc // 512
    nslot = len(wslots)
    fbpd = 1
    ngrp = nfb // fbpd
    wbase, tbase, ebase = ctx.wdma.v, ctx.tdr.v, ctx.evk.v
    sio_base = ctx.sio.v
    st_base = ctx.st.v

    @block.sync
    def _(sy):
        for fb in range(nfb):
            if fb >= 2:
                sy.wait_ge(ctx.st.h, st_base + 16 * (fb - 1))
            sy.dma_start(out=schunk_res[:, (fb % 2) * Tc:(fb % 2) * Tc + Tc],
                         in_=stream_hbm[fb * 128:(fb + 1) * 128, t0:t0 + Tc],
                         ).then_inc(ctx.sio.h, 16)
    ctx.sio.v = sio_base + 16 * nfb

    @block.gpsimd
    def _(g):
        for gi in range(ngrp):
            f0 = gi * fbpd * 128
            fw = fbpd * 128
            slot = wslots[gi % nslot]
            if gi >= nslot:
                g.wait_ge(ctx.tdr.h, tbase + (gi - nslot + 1))
            g.dma_start(out=slot[:, 0:ncc * fw].rearrange("c (k m) -> c k m", m=fw),
                        in_=w3_ap[:, f0:f0 + fw].rearrange("(k c) m -> c k m", c=128),
                        ).then_inc(ctx.wdma.h, 16)
    ctx.wdma.v = wbase + 16 * ngrp

    @block.tensor
    def _(t):
        for gi in range(ngrp):
            t.wait_ge(ctx.wdma.h, wbase + 16 * (gi + 1))
            slot = wslots[gi % nslot]
            for fb in range(gi * fbpd, (gi + 1) * fbpd):
                foff = (fb - gi * fbpd) * 128
                pb0 = (fb % 2) * ntc
                if fb >= 2:
                    t.wait_ge(ctx.evk.h, ebase + (fb - 2) + 1)
                for ti in range(ntc):
                    psb = pbanks[pb0 + ti]
                    for cc in range(ncc):
                        t.matmul(psb[:, 0:512],
                                 slot[:, cc * (fbpd * 128) + foff: cc * (fbpd * 128) + foff + 128],
                                 src_buf[:, cc * Tc + ti * 512: cc * Tc + ti * 512 + 512],
                                 start=(cc == 0), stop=(cc == ncc - 1))
            ctx.tdr.v += 1
            t.drain().then_inc(ctx.tdr.h, 1)

    @block.vector
    def _(v):
        for fb in range(nfb):
            gi = fb // fbpd
            v.wait_ge(ctx.tdr.h, tbase + gi + 1)
            v.wait_ge(ctx.sio.h, sio_base + 16 * (fb + 1))
            pb0 = (fb % 2) * ntc
            sc = schunk_res[:, (fb % 2) * Tc:(fb % 2) * Tc + Tc]
            for ti in range(ntc):
                v.scalar_tensor_tensor(
                    sc[:, ti * 512:(ti + 1) * 512], pbanks[pb0 + ti][:, 0:512],
                    scale_cols[:, 128 + fb:128 + fb + 1],
                    sc[:, ti * 512:(ti + 1) * 512], ALU.mult, ALU.add)
            ctx.evk.v += 1
            v.drain().then_inc(ctx.evk.h, 1)

    @block.scalar
    def _(s):
        for fb in range(nfb):
            s.wait_ge(ctx.evk.h, ebase + fb + 1)
            s.dma_start(out=stream_hbm[fb * 128:(fb + 1) * 128, t0:t0 + Tc],
                        in_=schunk_res[:, (fb % 2) * Tc:(fb % 2) * Tc + Tc],
                        ).then_inc(ctx.st.h, 16)
    ctx.st.v = st_base + 16 * nfb



def emit_l0_attn(ctx, block, xhat, qkv_w_ap, qkv_b_cols, wlat, lat_b_cols,
                 rope_cs, trimask, ident_f32, qfk, krope, vo, sec_stage,
                 lat_s1, lat_s2, attT, pbuf, ytmp, fptmp, y_tm, y_fm, ps_all,
                 wslots, stream_hbm, schunk_res, cproj_w_ap, cproj_b_cols, l2ob_col):
    SC = 1.0 / math.sqrt(32.0)
    banks = [ps_all[:, i * 512:(i + 1) * 512] for i in range(8)]

    for sec in range(6):
        f0 = sec * 512
        kind = ("q", "q", "k", "k", "v", "v")[sec]
        fence(ctx, block)

        def evac_qkv(s, fb, ti, ps, sec=sec):
            gfb = sec * 4 + fb
            s.activation(sec_stage[:, fb * T + ti * 512: fb * T + ti * 512 + 512],
                         ps, AF.Identity, bias=qkv_b_cols[:, gfb:gfb + 1])
        emit_linear_fm(ctx, block, 1024, 512, qkv_w_ap[:, f0:f0 + 512],
                       lambda cc: xhat[:, cc * T:(cc + 1) * T],
                       evac_qkv, wslots, banks, Tloc=T, fb_per_dma=2)
        fence(ctx, block)

        for fb in range(4):
            pt = (sec % 2) * 4 + fb
            stage = sec_stage[:, fb * T:(fb + 1) * T]
            _emit_lat_pair(ctx, block, kind, pt, stage, wlat, lat_b_cols,
                           rope_cs, qfk, krope, vo, lat_s1, lat_s2, banks)
            fence(ctx, block, engines=("tensor", "vector", "scalar"))
        fence(ctx, block)

    sc_ps = ps_all[:, 0:2048]
    tr_ps = banks[4]
    y_ps = banks[5][:, 0:64]
    fence(ctx, block)
    prev_yevac = ctx.sdr.v

    for h in range(16):
        for qt in range(16):
            q0 = qt * 128
            klen = q0 + 128
            nkc = (klen + 511) // 512
            nkt = klen // 128
            b = (h % 2) * 64
            qt_tile = qfk[:, (h // 2) * T:(h // 2) * T + T]
            kt_tile = krope[:, (h // 2) * T:(h // 2) * T + T]

            @block.tensor
            def _(t, qt_tile=qt_tile, kt_tile=kt_tile, b=b, q0=q0, klen=klen,
                  nkc=nkc, prev=prev_yevac):
                t.wait_ge(ctx.sdr.h, prev)
                for kc in range(nkc):
                    k0 = kc * 512
                    kw = min(512, klen - k0)
                    t.matmul(sc_ps[:, k0:k0 + kw], qt_tile[b:b + 32, q0:q0 + 128],
                             kt_tile[b:b + 32, k0:k0 + kw], start=True, stop=True)
                ctx.tdr.v += 1
                t.drain().then_inc(ctx.tdr.h, 1)
            sc_rdy = ctx.tdr.v

            @block.vector
            def _(v, q0=q0, klen=klen, sc_rdy=sc_rdy):
                vb = ctx.vch
                def hs(inst):
                    vb.v += 1
                    inst.then_inc(vb.h, 1)
                    v.wait_ge(vb.h, vb.v)
                v.wait_ge(ctx.tdr.h, sc_rdy)
                hs(v.tensor_add(sc_ps[:, q0:q0 + 128], sc_ps[:, q0:q0 + 128], trimask[:]))
                hs(v.reduce_max(ytmp[:, 0:1], sc_ps[:, 0:klen], axis=AX.X))
                hs(v.tensor_scalar_mul(ytmp[:, 1:2], ytmp[:, 0:1], -SC))
                ctx.vdr.v += 1
                v.drain().then_inc(ctx.vdr.h, 1)
            mask_rdy = ctx.vdr.v

            @block.scalar
            def _(s, klen=klen, nkc=nkc, mask_rdy=mask_rdy):
                s.wait_ge(ctx.vdr.h, mask_rdy)
                for kc in range(nkc):
                    k0 = kc * 512
                    kw = min(512, klen - k0)
                    s.activation(pbuf[:, k0:k0 + kw], sc_ps[:, k0:k0 + kw], AF.Exp,
                                 bias=ytmp[:, 1:2], scale=SC,
                                 accum_out=ytmp[:, 2 + kc:3 + kc])
                ctx.sdr.v += 1
                s.drain().then_inc(ctx.sdr.h, 1)
            exp_rdy = ctx.sdr.v

            @block.vector
            def _(v, nkc=nkc, exp_rdy=exp_rdy):
                vb = ctx.vch
                def hs(inst):
                    vb.v += 1
                    inst.then_inc(vb.h, 1)
                    v.wait_ge(vb.h, vb.v)
                v.wait_ge(ctx.sdr.h, exp_rdy)
                hs(v.reduce_sum(ytmp[:, 8:9], ytmp[:, 2:2 + nkc], axis=AX.X))
                hs(v.reciprocal(ytmp[:, 9:10], ytmp[:, 8:9]))
                ctx.vdr.v += 1
                v.drain().then_inc(ctx.vdr.h, 1)
            sum_rdy = ctx.vdr.v

            ngrp = (nkt + 3) // 4
            for grp in range(ngrp):
                n = min(4, nkt - grp * 4)

                @block.tensor
                def _(t, grp=grp, n=n, exp_rdy=exp_rdy, evprev=ctx.vdr.v):
                    t.wait_ge(ctx.sdr.h, exp_rdy)
                    if grp > 0:
                        t.wait_ge(ctx.vdr.h, evprev)
                    for j in range(n):
                        t.transpose(tr_ps[:, j * 128:(j + 1) * 128],
                                    pbuf[:, (grp * 4 + j) * 128:(grp * 4 + j + 1) * 128],
                                    ident_f32[:])
                    ctx.tdr.v += 1
                    t.drain().then_inc(ctx.tdr.h, 1)
                trd = ctx.tdr.v

                @block.vector
                def _(v, grp=grp, n=n, trd=trd):
                    v.wait_ge(ctx.tdr.h, trd)
                    v.tensor_copy(attT[:, grp * 4 * 128: grp * 4 * 128 + n * 128],
                                  tr_ps[:, 0:n * 128])
                    ctx.vdr.v += 1
                    v.drain().then_inc(ctx.vdr.h, 1)
            attT_rdy = ctx.vdr.v

            @block.tensor
            def _(t, h=h, nkt=nkt, attT_rdy=attT_rdy):
                t.wait_ge(ctx.vdr.h, attT_rdy)
                for kt in range(nkt):
                    t.matmul(y_ps, attT[:, kt * 128:(kt + 1) * 128],
                             vo[:, h * 1024 + kt * 64: h * 1024 + kt * 64 + 64],
                             start=(kt == 0), stop=(kt == nkt - 1))
                ctx.tdr.v += 1
                t.drain().then_inc(ctx.tdr.h, 1)
            pv_rdy = ctx.tdr.v

            @block.scalar
            def _(s, h=h, qt=qt, pv_rdy=pv_rdy, sum_rdy=sum_rdy):
                s.wait_ge(ctx.tdr.h, pv_rdy)
                s.wait_ge(ctx.vdr.h, sum_rdy)
                s.activation(y_tm[:, qt * 1024 + h * 64: qt * 1024 + h * 64 + 64],
                             y_ps, AF.Copy, scale=ytmp[:, 9:10])
                ctx.sdr.v += 1
                s.drain().then_inc(ctx.sdr.h, 1)
            prev_yevac = ctx.sdr.v

    fence(ctx, block)
    for qt in range(16):
        @block.vector
        def _(v, qt=qt, prev=ctx.sdr.v):
            if qt > 0:
                v.wait_ge(ctx.sdr.h, prev)
            v.tensor_copy(fptmp[:, 0:1024], y_tm[:, qt * 1024:(qt + 1) * 1024])
            ctx.vdr.v += 1
            v.drain().then_inc(ctx.vdr.h, 1)
        up_rdy = ctx.vdr.v
        for half in range(2):
            @block.tensor
            def _(t, half=half, up_rdy=up_rdy, prevs=ctx.sdr.v):
                t.wait_ge(ctx.vdr.h, up_rdy)
                if half == 1:
                    t.wait_ge(ctx.sdr.h, prevs)
                for j in range(4):
                    t.transpose(tr_ps[:, j * 128:(j + 1) * 128],
                                fptmp[:, (half * 4 + j) * 128:(half * 4 + j + 1) * 128],
                                ident_f32[:])
                ctx.tdr.v += 1
                t.drain().then_inc(ctx.tdr.h, 1)
            trd = ctx.tdr.v

            @block.scalar
            def _(s, qt=qt, half=half, trd=trd):
                s.wait_ge(ctx.tdr.h, trd)
                for j in range(4):
                    fc = half * 4 + j
                    s.activation(y_fm[:, fc * T + qt * 128: fc * T + qt * 128 + 128],
                                 tr_ps[:, j * 128:(j + 1) * 128], AF.Identity,
                                 bias=l2ob_col[:, 0:1])
                ctx.sdr.v += 1
                s.drain().then_inc(ctx.sdr.h, 1)

    fence(ctx, block)
    _emit_out_residual(ctx, block, cproj_w_ap, y_fm, wslots, banks,
                       stream_hbm, schunk_res, C=1024, bias_cols=cproj_b_cols,
                       scale_cols=None, t0=0, Tc=T)
    fence(ctx, block)


def _emit_lat_pair(ctx, block, kind, pt, stage, wlat, lat_b_cols, rope_cs,
                   qfk, krope, vo, lat_s1, lat_s2, banks):
    proj_col = {"q": 0, "k": 32, "v": 32}[kind]
    bias_col = {"q": 0, "k": 1, "v": 1}[kind]

    @block.tensor
    def _(t, prev_ev=ctx.sdr.v):
        t.wait_ge(ctx.sdr.h, prev_ev)
        for ti in range(NT):
            for b in (0, 64):
                t.matmul(banks[ti][b:b + 32, 0:512],
                         wlat[b:b + 64, proj_col:proj_col + 32],
                         stage[b:b + 64, ti * 512:(ti + 1) * 512],
                         start=True, stop=True)
        ctx.tdr.v += 1
        t.drain().then_inc(ctx.tdr.h, 1)
    lat_rdy = ctx.tdr.v

    @block.scalar
    def _(s, lat_rdy=lat_rdy):
        s.wait_ge(ctx.tdr.h, lat_rdy)
        for ti in range(NT):
            s.activation(lat_s1[:, ti * 512:(ti + 1) * 512], banks[ti][:, 0:512],
                         AF.Identity, bias=lat_b_cols[:, bias_col:bias_col + 1])
        ctx.sdr.v += 1
        s.drain().then_inc(ctx.sdr.h, 1)
    ev1 = ctx.sdr.v

    if kind in ("q", "k"):
        @block.tensor
        def _(t, ev1=ev1):
            t.wait_ge(ctx.sdr.h, ev1)
            for ti in range(NT):
                for b in (0, 64):
                    t.matmul(banks[4 + ti][b:b + 32, 0:512],
                             wlat[b:b + 32, 96:128],
                             lat_s1[b:b + 32, ti * 512:(ti + 1) * 512],
                             start=True, stop=True)
            ctx.tdr.v += 1
            t.drain().then_inc(ctx.tdr.h, 1)
        rot_rdy = ctx.tdr.v

        @block.vector
        def _(v, rot_rdy=rot_rdy):
            vb = ctx.vch
            def hs(inst):
                vb.v += 1
                inst.then_inc(vb.h, 1)
                v.wait_ge(vb.h, vb.v)
            v.wait_ge(ctx.tdr.h, rot_rdy)
            for ti in range(NT):
                sl = slice(ti * 512, (ti + 1) * 512)
                hs(v.tensor_mul(lat_s2[:, sl], lat_s1[:, sl], rope_cs[:, 0:T][:, sl]))
                hs(v.tensor_mul(lat_s1[:, sl], banks[4 + ti][:, 0:512], rope_cs[:, T:2 * T][:, sl]))
                hs(v.tensor_add(lat_s2[:, sl], lat_s2[:, sl], lat_s1[:, sl]))
            ctx.vdr.v += 1
            v.drain().then_inc(ctx.vdr.h, 1)
        rope_rdy = ctx.vdr.v

        if kind == "k":
            @block.vector
            def _(v, pt=pt, rope_rdy=rope_rdy):
                v.wait_ge(ctx.vdr.h, rope_rdy)
                v.tensor_copy(krope[:, pt * T:(pt + 1) * T], lat_s2[:, 0:T])
                ctx.vdr.v += 1
                v.drain().then_inc(ctx.vdr.h, 1)
        else:
            @block.tensor
            def _(t, rope_rdy=rope_rdy):
                t.wait_ge(ctx.vdr.h, rope_rdy)
                for ti in range(NT):
                    for b in (0, 64):
                        t.matmul(banks[ti][b:b + 32, 0:512],
                                 wlat[b:b + 32, 64:96],
                                 lat_s2[b:b + 32, ti * 512:(ti + 1) * 512],
                                 start=True, stop=True)
                ctx.tdr.v += 1
                t.drain().then_inc(ctx.tdr.h, 1)
            wqk_rdy = ctx.tdr.v

            @block.scalar
            def _(s, pt=pt, wqk_rdy=wqk_rdy):
                s.wait_ge(ctx.tdr.h, wqk_rdy)
                for ti in range(NT):
                    s.activation(qfk[:, pt * T + ti * 512: pt * T + ti * 512 + 512],
                                 banks[ti][:, 0:512], AF.Identity,
                                 bias=lat_b_cols[:, 2:3])
                ctx.sdr.v += 1
                s.drain().then_inc(ctx.sdr.h, 1)
    else:
        @block.tensor
        def _(t, ev1=ev1):
            t.wait_ge(ctx.sdr.h, ev1)
            for hh in range(2):
                b = hh * 64
                for tt in range(16):
                    bank = banks[4 + 2 * hh + tt // 8]
                    t.matmul(bank[:, (tt % 8) * 64:(tt % 8) * 64 + 64],
                             lat_s1[b:b + 32, tt * 128:(tt + 1) * 128],
                             wlat[b:b + 32, 128:192],
                             start=True, stop=True)
            ctx.tdr.v += 1
            t.drain().then_inc(ctx.tdr.h, 1)
        vo_rdy = ctx.tdr.v

        @block.vector
        def _(v, pt=pt, vo_rdy=vo_rdy):
            v.wait_ge(ctx.tdr.h, vo_rdy)
            for hh in range(2):
                h = pt * 2 + hh
                v.tensor_copy(vo[:, h * 1024: h * 1024 + 512], banks[4 + 2 * hh][:, 0:512])
                v.tensor_copy(vo[:, h * 1024 + 512: h * 1024 + 1024],
                              banks[5 + 2 * hh][:, 0:512])
            ctx.vdr.v += 1
            v.drain().then_inc(ctx.vdr.h, 1)


def _emit_out_residual(ctx, block, w_ap, src_fm, wslots, banks, stream_hbm,
                       schunk_res, C, bias_cols=None, scale_cols=None, t0=0, Tc=T,
                       factor_col=None):
    ncc, nfb = C // 128, 8
    ntc = Tc // 512
    nslot = len(wslots)
    ngrp = nfb
    wbase, tbase, ebase = ctx.wdma.v, ctx.tdr.v, ctx.evk.v
    sio_base = ctx.sio.v
    st_base = ctx.st.v

    dbuf = Tc <= 1024

    @block.sync
    def _(sy):
        for fb in range(nfb):
            if dbuf and fb >= 2:
                sy.wait_ge(ctx.st.h, st_base + 16 * (fb - 1))
            if not dbuf and fb >= 1:
                sy.wait_ge(ctx.st.h, st_base + 16 * fb)
            sy.dma_start(out=schunk_res[:, (fb % 2) * Tc:(fb % 2) * Tc + Tc]
                         if dbuf else schunk_res[:, 0:Tc],
                         in_=stream_hbm[fb * 128:(fb + 1) * 128, t0:t0 + Tc],
                         ).then_inc(ctx.sio.h, 16)
    ctx.sio.v = sio_base + 16 * nfb

    @block.gpsimd
    def _(g):
        for gi in range(ngrp):
            f0 = gi * 128
            slot = wslots[gi % nslot]
            if gi >= nslot:
                g.wait_ge(ctx.tdr.h, tbase + (gi - nslot + 1))
            g.dma_start(out=slot[:, 0:ncc * 128].rearrange("c (k m) -> c k m", m=128),
                        in_=w_ap[:, f0:f0 + 128].rearrange("(k c) m -> c k m", c=128),
                        ).then_inc(ctx.wdma.h, 16)
    ctx.wdma.v = wbase + 16 * ngrp

    @block.tensor
    def _(t):
        for fb in range(nfb):
            t.wait_ge(ctx.wdma.h, wbase + 16 * (fb + 1))
            slot = wslots[fb % nslot]
            pb0 = (fb % 2) * ntc
            if fb >= 2:
                t.wait_ge(ctx.evk.h, ebase + (fb - 2) + 1)
            for ti in range(ntc):
                psb = banks[pb0 + ti]
                for cc in range(ncc):
                    t.matmul(psb[:, 0:512],
                             slot[:, cc * 128: cc * 128 + 128],
                             src_fm[:, cc * Tc + ti * 512: cc * Tc + ti * 512 + 512]
                             if Tc != T else src_fm[:, cc * T + t0 + ti * 512: cc * T + t0 + ti * 512 + 512],
                             start=(cc == 0), stop=(cc == ncc - 1))
            ctx.tdr.v += 1
            t.drain().then_inc(ctx.tdr.h, 1)

    @block.vector
    def _(v):
        for fb in range(nfb):
            v.wait_ge(ctx.tdr.h, tbase + fb + 1)
            v.wait_ge(ctx.sio.h, sio_base + 16 * (fb + 1))
            pb0 = (fb % 2) * ntc
            sc = schunk_res[:, (fb % 2) * Tc:(fb % 2) * Tc + Tc] if dbuf \
                else schunk_res[:, 0:Tc]
            for ti in range(ntc):
                if factor_col is not None:
                    v.scalar_tensor_tensor(sc[:, ti * 512:(ti + 1) * 512],
                                           banks[pb0 + ti][:, 0:512],
                                           factor_col[:, fb:fb + 1],
                                           sc[:, ti * 512:(ti + 1) * 512],
                                           ALU.mult, ALU.add)
                else:
                    v.scalar_tensor_tensor(sc[:, ti * 512:(ti + 1) * 512],
                                           banks[pb0 + ti][:, 0:512], 1.0,
                                           sc[:, ti * 512:(ti + 1) * 512],
                                           ALU.mult, ALU.add)
            if bias_cols is not None:
                v.tensor_scalar_add(sc, sc, bias_cols[:, fb:fb + 1])
            ctx.evk.v += 1
            v.drain().then_inc(ctx.evk.h, 1)

    @block.scalar
    def _(s):
        for fb in range(nfb):
            s.wait_ge(ctx.evk.h, ebase + fb + 1)
            s.dma_start(out=stream_hbm[fb * 128:(fb + 1) * 128, t0:t0 + Tc],
                        in_=schunk_res[:, (fb % 2) * Tc:(fb % 2) * Tc + Tc]
                        if dbuf else schunk_res[:, 0:Tc],
                        ).then_inc(ctx.st.h, 16)
    ctx.st.v = st_base + 16 * nfb



def build_layer0_program(parts=("ffn1", "attn", "ffn2")):
    nc = bass.Bass("TRN2", target_bir_lowering=False)
    stream_in = nc.dram_tensor("stream_in", [D, T], F32, kind="ExternalInput").ap()
    stream_out = nc.dram_tensor("stream_out", [D, T], F32, kind="ExternalOutput").ap()
    wu = nc.dram_tensor("wu", [1024, 8192], F32, kind="ExternalInput").ap()
    h1 = nc.dram_tensor("h1", [4096, 4096], F32, kind="ExternalInput").ap()
    h2 = nc.dram_tensor("h2", [4096, 4096], F32, kind="ExternalInput").ap()
    w3 = nc.dram_tensor("w3", [4096, 1024], F32, kind="ExternalInput").ap()
    scl_in = nc.dram_tensor("scl", [128, 144], F32, kind="ExternalInput").ap()
    qkvw = nc.dram_tensor("qkvw", [1024, 3072], F32, kind="ExternalInput").ap()
    cprojw = nc.dram_tensor("cprojw", [1024, 1024], F32, kind="ExternalInput").ap()
    qkvb_in = nc.dram_tensor("qkvb", [128, 24], F32, kind="ExternalInput").ap()
    wlat_in = nc.dram_tensor("wlat", [128, 192], F32, kind="ExternalInput").ap()
    latb_in = nc.dram_tensor("latb", [128, 4], F32, kind="ExternalInput").ap()
    cpb_in = nc.dram_tensor("cpb", [128, 8], F32, kind="ExternalInput").ap()
    rope_in = nc.dram_tensor("rope", [128, 2 * T], F32, kind="ExternalInput").ap()
    tri_in = nc.dram_tensor("tri", [128, 128], F32, kind="ExternalInput").ap()
    identf_in = nc.dram_tensor("identf", [128, 128], F32, kind="ExternalInput").ap()
    ones_in = nc.dram_tensor("ones", [128, 128], BF16, kind="ExternalInput").ap()
    ccols_in = nc.dram_tensor("ccols", [128, len(CONST_VALS)], F32, kind="ExternalInput").ap()

    ctx = Ctx(nc)
    bfa = ctx.sb("bfa", [128, 81920], BF16)
    fpa = ctx.sb("fpa", [128, 4640], F32)
    wsmall = [ctx.sb(f"wsm{i}", [128, 2048], BF16) for i in range(2)]
    wbig = [ctx.sb(f"wbg{i}", [128, 4096], BF16) for i in range(2)]
    ones_bf = ctx.sb("ones_bf", [128, 128], BF16)
    ccols = ctx.sb("ccols_sb", [128, len(CONST_VALS)], F32)
    scl = ctx.sb("scl_sb", [128, 144], F32)
    qkvb = ctx.sb("qkvb_sb", [128, 24], F32)
    wlat = ctx.sb("wlat_sb", [128, 192], BF16)
    latb = ctx.sb("latb_sb", [128, 4], F32)
    cpb = ctx.sb("cpb_sb", [128, 8], F32)
    trimask = ctx.sb("tri_sb", [128, 128], F32)
    identf = ctx.sb("identf_sb", [128, 128], F32)
    ps_all = ctx.psum("ps_all", [128, 4096], F32)
    pbanks = [ps_all[:, i * 512:(i + 1) * 512] for i in range(8)]

    xhat = bfa[:, 0:16384]
    bufA = bfa[:, 16384:49152]
    bufB = bfa[:, 49152:81920]
    sq_scr = bufA[:, 0:2048]
    bcast = bufA[:, 2048:6144]
    ab_rows = bufA[:, 6144:10240]
    qfk = bfa[:, 16384:32768]
    krope = bfa[:, 32768:49152]
    vo = bfa[:, 49152:65536]
    sec_stage = bfa[:, 65536:73728]
    lat_s1 = bfa[:, 73728:75776]
    lat_s2 = bfa[:, 75776:77824]
    attT = bfa[:, 65536:67584]
    rope_cs = bfa[:, 77824:81920]
    y_tm = bfa[:, 0:16384]
    y_fm = bfa[:, 16384:32768]
    schunk = fpa[:, 0:2048]
    rowbuf = fpa[:, 2048:3584]
    u_tmp = fpa[:, 3584:4608]
    fptmp = fpa[:, 3584:4608]
    schunk_res = fpa[:, 0:2048]
    pbuf = fpa[:, 0:2048]
    ytmp = fpa[:, 4608:4640]

    with nc.Block() as block:
        emit_boot(ctx, block)

        @block.gpsimd
        def _(g):
            g.dma_start(out=ones_bf[:], in_=ones_in).then_inc(ctx.sio.h, 16)
            g.dma_start(out=ccols[:], in_=ccols_in).then_inc(ctx.sio.h, 16)
            g.dma_start(out=scl[:], in_=scl_in).then_inc(ctx.sio.h, 16)
            g.dma_start(out=qkvb[:], in_=qkvb_in).then_inc(ctx.sio.h, 16)
            g.dma_start(out=wlat[:], in_=wlat_in).then_inc(ctx.sio.h, 16)
            g.dma_start(out=latb[:], in_=latb_in).then_inc(ctx.sio.h, 16)
            g.dma_start(out=cpb[:], in_=cpb_in).then_inc(ctx.sio.h, 16)
            g.dma_start(out=trimask[:], in_=tri_in).then_inc(ctx.sio.h, 16)
            g.dma_start(out=identf[:], in_=identf_in).then_inc(ctx.sio.h, 16)
        ctx.sio.v += 144
        pro_io = ctx.sio.v

        @block.vector
        def _(v):
            v.wait_ge(ctx.sio.h, pro_io)
            v.tensor_scalar_mul(scl[:, 128:136], scl[:, 128:136], 0.5)
            ctx.vdr.v += 1
            v.drain().then_inc(ctx.vdr.h, 1)
        fence(ctx, block)

        @block.sync
        def _(sy):
            sy.dma_start(out=stream_out, in_=stream_in).then_inc(ctx.sio.h, 16)
        ctx.sio.v += 16
        fence(ctx, block)

        if "ffn1" in parts:
            emit_stream_ln(ctx, block, stream_out, schunk, xhat, sq_scr, rowbuf,
                           ab_rows, bcast, pbanks, ones_bf, ccols)
            fence(ctx, block)
            emit_l0_ffn(ctx, block, xhat, wu, h1, h2, w3, scl, u_tmp,
                        bufA, bufB, wsmall, wbig, pbanks, stream_out, schunk_res)

        if "attn" in parts:
            emit_stream_ln(ctx, block, stream_out, schunk, xhat, sq_scr, rowbuf,
                           ab_rows, bcast, pbanks, ones_bf, ccols)
            fence(ctx, block)

            @block.gpsimd
            def _(g):
                g.dma_start(out=rope_cs[:], in_=rope_in).then_inc(ctx.sio.h, 16)
            ctx.sio.v += 16
            fence(ctx, block)
            emit_l0_attn(ctx, block, xhat, qkvw, qkvb, wlat, latb, rope_cs,
                         trimask, identf, qfk, krope, vo, sec_stage, lat_s1,
                         lat_s2, attT, pbuf, ytmp, fptmp, y_tm, y_fm, ps_all,
                         wsmall, stream_out, schunk_res, cprojw, cpb, latb[:, 3:4])

        if "ffn2" in parts:
            emit_stream_ln(ctx, block, stream_out, schunk, xhat, sq_scr, rowbuf,
                           ab_rows, bcast, pbanks, ones_bf, ccols)
            fence(ctx, block)
            emit_l0_ffn(ctx, block, xhat, wu, h1, h2, w3, scl, u_tmp,
                        bufA, bufB, wsmall, wbig, pbanks, stream_out, schunk_res)

        stv = ctx.st.v
        @block.sync
        def _(sy):
            sy.wait_ge(ctx.st.h, stv)

    ctx.es.close()
    nc.gpsimd.sem_clear(ctx.clear_rng)
    return nc


def prep_layer0_inputs(ip):
    out = {}
    w1 = ip["L0_ffn_w1_w"]; w2 = ip["L0_ffn_w2_w"]
    wu = np.empty((1024, 8192), np.float32)
    w1t = np.ascontiguousarray(w1.T); w2t = np.ascontiguousarray(w2.T)
    for i in range(32):
        wu[:, (2 * i) * 128:(2 * i) * 128 + 128] = w1t[:, i * 128:(i + 1) * 128]
        wu[:, (2 * i + 1) * 128:(2 * i + 1) * 128 + 128] = w2t[:, i * 128:(i + 1) * 128]
    out["wu"] = np.ascontiguousarray(wu)
    out["h1"] = np.ascontiguousarray(ip["L0_ffn_h1_w"].T)
    out["h2"] = np.ascontiguousarray(ip["L0_ffn_h2_w"].T)
    out["w3"] = np.ascontiguousarray(ip["L0_ffn_w3_w"].T)
    scl = np.zeros((128, 144), np.float32)
    scl[:, 0:32] = ip["L0_ffn_w1_s"].reshape(32, 128).T
    scl[:, 32:64] = ip["L0_ffn_w2_s"].reshape(32, 128).T
    scl[:, 64:96] = ip["L0_ffn_h1_s"].reshape(32, 128).T
    scl[:, 96:128] = ip["L0_ffn_h2_s"].reshape(32, 128).T
    scl[:, 128:136] = ip["L0_ffn_w3_s"].reshape(8, 128).T
    out["scl"] = np.ascontiguousarray(scl)

    out["qkvw"] = np.ascontiguousarray(ip["L0_attn_c_attn_w"].T)
    out["cprojw"] = np.ascontiguousarray(ip["L0_attn_cproj_w"].T)
    out["qkvb"] = np.ascontiguousarray(ip["L0_attn_c_attn_b"].reshape(24, 128).T)
    wlat = np.zeros((128, 192), np.float32)
    q2lT = ip["L0_attn_q2l_w"].T
    kv2lT = ip["L0_attn_kv2l_w"].T
    wqkT = ip["L0_attn_wqk_w"].T
    l2oT = ip["L0_attn_l2o_w"].T
    P = np.zeros((32, 32), np.float32)
    for i in range(16):
        P[2 * i, 2 * i + 1] = -1.0
        P[2 * i + 1, 2 * i] = 1.0
    PT = P.T
    for b in (0, 64):
        wlat[b:b + 64, 0:32] = q2lT
        wlat[b:b + 64, 32:64] = kv2lT
        wlat[b:b + 32, 64:96] = wqkT
        wlat[b:b + 32, 96:128] = PT
        wlat[b:b + 32, 128:192] = l2oT
    out["wlat"] = wlat
    latb = np.zeros((128, 4), np.float32)
    for b in (0, 64):
        latb[b:b + 32, 0] = ip["L0_attn_q2l_b"]
        latb[b:b + 32, 1] = ip["L0_attn_kv2l_b"]
        latb[b:b + 32, 2] = ip["L0_attn_wqk_b"]
    latb[0:64, 3] = ip["L0_attn_l2o_b"]
    latb[64:128, 3] = ip["L0_attn_l2o_b"]
    out["latb"] = latb
    out["cpb"] = np.ascontiguousarray(ip["L0_attn_cproj_b"].reshape(8, 128).T)
    L = 32
    inv_freq = (1.0 / 10000.0 ** (np.arange(0, L, 2, dtype=np.float32) / L)).astype(np.float32)
    tpos = np.arange(T, dtype=np.float32)
    freqs = np.outer(tpos, inv_freq).astype(np.float32)
    cos = np.repeat(np.cos(freqs), 2, axis=-1)[:, :L].astype(np.float32)
    sin = np.repeat(np.sin(freqs), 2, axis=-1)[:, :L].astype(np.float32)
    rope = np.zeros((128, 2 * T), np.float32)
    for b in (0, 64):
        rope[b:b + 32, 0:T] = cos.T
        rope[b:b + 32, T:2 * T] = sin.T
    out["rope"] = rope
    return out


_NLAYERS = 4
_L3PARTS = ("ffn1", "attn", "ffn2")
_L1PARTS = ("ffn1", "attn", "ffn2")
_L2PARTS = ("ffn1", "attn", "ffn2")


def kernel(**inputs):
    consts = host_constants()
    x = np.ascontiguousarray(inputs["L0_attn_x"])
    streams = [np.ascontiguousarray(x[b].T) for b in range(NCORES)]
    base = {"ones": consts["c_ones"], "ccols": consts["c_cols"],
            "tri": consts["c_trimask"], "identf": consts["c_identf"]}

    l0 = prep_layer0_inputs(inputs)
    nc = build_layer0_program(parts=("ffn1", "attn", "ffn2"))
    in_maps = [dict(base, stream_in=streams[b], **l0) for b in range(NCORES)]
    res = run_bass_kernel_spmd(nc, in_maps, core_ids=list(range(NCORES)))
    streams = [np.ascontiguousarray(res.results[b]["stream_out"]) for b in range(NCORES)]

    if _NLAYERS >= 2:
        l1 = prep_layer1_inputs(inputs)
        l1.update({k: base[k] for k in ("ones", "ccols", "identf", "tri")})
        nc = build_layer1_program(parts=_L1PARTS)
        in_maps = [dict(l1, stream_in=streams[b]) for b in range(NCORES)]
        res = run_bass_kernel_spmd(nc, in_maps, core_ids=list(range(NCORES)))
        streams = [np.ascontiguousarray(res.results[b]["stream_out"]) for b in range(NCORES)]

    if _NLAYERS >= 3:
        l2 = prep_layer2_inputs(inputs)
        l2.update({k: base[k] for k in ("ones", "ccols", "identf", "tri")})
        nc = build_layer2_program(parts=_L2PARTS)
        in_maps = [dict(l2, stream_in=streams[b]) for b in range(NCORES)]
        res = run_bass_kernel_spmd(nc, in_maps, core_ids=list(range(NCORES)))
        streams = [np.ascontiguousarray(res.results[b]["stream_out"]) for b in range(NCORES)]

    if _NLAYERS >= 4:
        l3 = prep_layer3_inputs(inputs)
        l3.update({k: base[k] for k in ("ones", "ccols", "identf", "tri")})
        nc = build_layer3_program(parts=_L3PARTS)
        in_maps = [dict(l3, stream_in=streams[b]) for b in range(NCORES)]
        res = run_bass_kernel_spmd(nc, in_maps, core_ids=list(range(NCORES)))
        streams = [np.ascontiguousarray(res.results[b]["stream_out"]) for b in range(NCORES)]

    out = np.stack([np.ascontiguousarray(s.T) for s in streams], axis=0)
    return out.astype(np.float32)



def emit_linear_fm2(ctx, block, C, F, w_ap, src_fn, evac_fn, wslots, pbanks,
                    Tloc=T, fb_per_dma=2, evac_engine="scalar"):
    ncc = C // 128
    nfb = F // 128
    ntc = (Tloc + 511) // 512
    tcs = min(512, Tloc)
    nslot = len(wslots)
    ngrp = (nfb + fb_per_dma - 1) // fb_per_dma
    wbase, tbase, ebase = ctx.wdma.v, ctx.tdr.v, ctx.evk.v

    @block.gpsimd
    def _(g):
        for gi in range(ngrp):
            f0 = gi * fb_per_dma * 128
            fw = min(fb_per_dma * 128, F - f0)
            slot = wslots[gi % nslot]
            if gi >= nslot:
                g.wait_ge(ctx.tdr.h, tbase + (gi - nslot + 1))
            g.dma_start(out=slot[:, 0:ncc * fw].rearrange("c (k m) -> c k m", m=fw),
                        in_=w_ap[:, f0:f0 + fw].rearrange("(k c) m -> c k m", c=128),
                        ).then_inc(ctx.wdma.h, 16)
    ctx.wdma.v = wbase + 16 * ngrp

    @block.tensor
    def _(t):
        for gi in range(ngrp):
            t.wait_ge(ctx.wdma.h, wbase + 16 * (gi + 1))
            f0 = gi * fb_per_dma * 128
            fw = min(fb_per_dma * 128, F - f0)
            slot = wslots[gi % nslot]
            for fb in range(gi * fb_per_dma, min((gi + 1) * fb_per_dma, nfb)):
                foff = fb * 128 - f0
                pb0 = (fb % 2) * ntc
                if fb >= 2:
                    t.wait_ge(ctx.evk.h, ebase + (fb - 2) + 1)
                for ti in range(ntc):
                    for cc in range(ncc):
                        t.matmul(pbanks[pb0 + ti][:, 0:tcs],
                                 slot[:, cc * fw + foff: cc * fw + foff + 128],
                                 src_fn(cc)[:, ti * tcs:(ti + 1) * tcs],
                                 start=(cc == 0), stop=(cc == ncc - 1))
            ctx.tdr.v += 1
            t.drain().then_inc(ctx.tdr.h, 1)

    def evac_body(e):
        for gi in range(ngrp):
            e.wait_ge(ctx.tdr.h, tbase + gi + 1)
            for fb in range(gi * fb_per_dma, min((gi + 1) * fb_per_dma, nfb)):
                pb0 = (fb % 2) * ntc
                for ti in range(ntc):
                    evac_fn(e, fb, ti, pbanks[pb0 + ti][:, 0:tcs])
                ctx.evk.v += 1
                e.drain().then_inc(ctx.evk.h, 1)
    (block.scalar if evac_engine == "scalar" else block.vector)(evac_body)


def emit_linear_tm(ctx, block, C, F, w_ap, src_fn, evac_fn, wslots, pbanks,
                   Tloc=T, extra_mm=None):
    ncc = C // 128
    nfc = (F + 511) // 512
    ntt = Tloc // 128
    nslot = len(wslots)
    wbase, tbase, ebase = ctx.wdma.v, ctx.tdr.v, ctx.evk.v

    @block.gpsimd
    def _(g):
        for fc in range(nfc):
            f0 = fc * 512
            fw = min(512, F - f0)
            slot = wslots[fc % nslot]
            if fc >= nslot:
                g.wait_ge(ctx.tdr.h, tbase + (fc - nslot) * ntt + ntt)
            g.dma_start(out=slot[:, 0:ncc * fw].rearrange("c (k m) -> c k m", m=fw),
                        in_=w_ap[:, f0:f0 + fw].rearrange("(k c) m -> c k m", c=128),
                        ).then_inc(ctx.wdma.h, 16)
    ctx.wdma.v = wbase + 16 * nfc

    @block.tensor
    def _(t):
        for fc in range(nfc):
            t.wait_ge(ctx.wdma.h, wbase + 16 * (fc + 1))
            f0 = fc * 512
            fw = min(512, F - f0)
            slot = wslots[fc % nslot]
            for tt in range(ntt):
                pb = pbanks[tt % 2]
                if fc > 0 or tt >= 2:
                    ctx_evk_target = ebase + (fc * ntt + tt - 2) + 1
                    t.wait_ge(ctx.evk.h, ctx_evk_target)
                for cc in range(ncc):
                    t.matmul(pb[:, 0:fw],
                             src_fn(cc)[:, tt * 128:(tt + 1) * 128],
                             slot[:, cc * fw: cc * fw + fw],
                             start=(cc == 0), stop=(cc == ncc - 1 and extra_mm is None))
                if extra_mm is not None:
                    extra_mm(t, tt, fc, pb[:, 0:fw])
                ctx.tdr.v += 1
                t.drain().then_inc(ctx.tdr.h, 1)

    @block.vector
    def _(v):
        for fc in range(nfc):
            fw = min(512, F - fc * 512)
            for tt in range(ntt):
                v.wait_ge(ctx.tdr.h, tbase + fc * ntt + tt + 1)
                evac_fn(v, tt, fc, pbanks[tt % 2][:, 0:fw])
                ctx.evk.v += 1
                v.drain().then_inc(ctx.evk.h, 1)



def emit_topk_gating(ctx, block, h_src_fn, gatewT_ap, gateb_row, ones_bf, c_all,
                     cT_bf, stats, wslot, bank, tr_bank, identf, topk, softmax_all):
    wbase = ctx.wdma.v

    @block.gpsimd
    def _(g):
        g.dma_start(out=wslot[:, 0:64].rearrange("c (k m) -> c k m", m=8),
                    in_=gatewT_ap.rearrange("(k c) m -> c k m", c=128),
                    ).then_inc(ctx.wdma.h, 16)
    ctx.wdma.v = wbase + 16

    @block.tensor
    def _(t):
        t.wait_ge(ctx.wdma.h, wbase + 16)
        for tt in range(16):
            for cc in range(8):
                t.matmul(bank[:, tt * 8:(tt + 1) * 8],
                         h_src_fn(cc)[:, tt * 128:(tt + 1) * 128],
                         wslot[:, cc * 8: cc * 8 + 8],
                         start=(cc == 0), stop=(cc == 7 and gateb_row is None))
            if gateb_row is not None:
                t.matmul(bank[:, tt * 8:(tt + 1) * 8], ones_bf[0:1, 0:128],
                         gateb_row[0:1, 0:8], start=False, stop=True)
        ctx.tdr.v += 1
        t.drain().then_inc(ctx.tdr.h, 1)
    logits_rdy = ctx.tdr.v

    @block.vector
    def _(v):
        vb = ctx.vch
        def hs(inst):
            vb.v += 1
            inst.then_inc(vb.h, 1)
            v.wait_ge(vb.h, vb.v)
        v.wait_ge(ctx.tdr.h, logits_rdy)
        for tt in range(16):
            lg = bank[:, tt * 8:(tt + 1) * 8]
            st4 = 4 * tt
            msk = stats[:, 64:72]
            hs(v.reduce_max(stats[:, st4:st4 + 1], lg, axis=AX.X))
            hs(v.tensor_scalar(msk, lg, stats[:, st4:st4 + 1], -1e30, ALU.is_ge, ALU.mult))
            hs(v.tensor_add(msk, msk, lg))
            for k in range(1, topk):
                hs(v.reduce_max(stats[:, st4 + 1:st4 + 2], msk, axis=AX.X))
                if k < topk - 1:
                    hs(v.tensor_scalar(stats[:, 72:80], msk, stats[:, st4 + 1:st4 + 2],
                                       -1e30, ALU.is_ge, ALU.mult))
                    hs(v.tensor_add(msk, msk, stats[:, 72:80]))
        ctx.vdr.v += 1
        v.drain().then_inc(ctx.vdr.h, 1)
    mk_rdy = ctx.vdr.v

    @block.scalar
    def _(s):
        s.wait_ge(ctx.vdr.h, mk_rdy)
        for tt in range(16):
            lg = bank[:, tt * 8:(tt + 1) * 8]
            st4 = 4 * tt
            s.activation(stats[:, st4 + 2:st4 + 3], stats[:, st4:st4 + 1],
                         AF.Identity, scale=-1.0)
        s.drain().then_inc(ctx.sdr.h, 1)
        ctx.sdr.v += 1
        s.wait_ge(ctx.sch.h, ctx.sch.v)
        for tt in range(16):
            lg = bank[:, tt * 8:(tt + 1) * 8]
            st4 = 4 * tt
            s.activation(c_all[:, tt * 8:(tt + 1) * 8], lg, AF.Exp,
                         bias=stats[:, st4 + 2:st4 + 3])
        ctx.sdr.v += 1
        s.drain().then_inc(ctx.sdr.h, 1)
    exp_rdy = ctx.sdr.v

    @block.vector
    def _(v):
        vb = ctx.vch
        def hs(inst):
            vb.v += 1
            inst.then_inc(vb.h, 1)
            v.wait_ge(vb.h, vb.v)
        v.wait_ge(ctx.sdr.h, exp_rdy)
        for tt in range(16):
            lg = bank[:, tt * 8:(tt + 1) * 8]
            st4 = 4 * tt
            ca = c_all[:, tt * 8:(tt + 1) * 8]
            msk = stats[:, 64:72]
            hs(v.tensor_scalar(msk, lg, stats[:, st4 + 1:st4 + 2], None, ALU.is_ge))
            hs(v.tensor_mul(ca, ca, msk))
            hs(v.reduce_sum(stats[:, st4 + 3:st4 + 4], ca, axis=AX.X))
            hs(v.reciprocal(stats[:, st4 + 2:st4 + 3], stats[:, st4 + 3:st4 + 4]))
            hs(v.tensor_scalar(ca, ca, stats[:, st4 + 2:st4 + 3], None, ALU.mult))
        ctx.vdr.v += 1
        v.drain().then_inc(ctx.vdr.h, 1)
    c_rdy = ctx.vdr.v

    @block.tensor
    def _(t):
        t.wait_ge(ctx.vdr.h, c_rdy)
        for tt in range(16):
            t.transpose(tr_bank[:, (tt % 4) * 128:(tt % 4) * 128 + 128][0:8, :],
                        c_all[:, tt * 8:(tt + 1) * 8], identf[:])
            if tt % 4 == 3:
                ctx.tdr.v += 1
                t.drain().then_inc(ctx.tdr.h, 1)
                t.wait_ge(ctx.vdr.h, ctx.vdr.v + (tt // 4) + 1)
    trb = ctx.tdr.v - 4

    @block.vector
    def _(v):
        for grp in range(4):
            v.wait_ge(ctx.tdr.h, trb + grp + 1)
            v.tensor_copy(cT_bf[0:8, grp * 512:(grp + 1) * 512],
                          tr_bank[0:8, 0:512])
            ctx.vdr.v += 1
            v.drain().then_inc(ctx.vdr.h, 1)


def emit_l1_ffn(ctx, block, xhat, h, normw_cols, gatewT_ap, gateb_row, mlp1_ap,
                b1_cols, mlp2_ap, b2row_bf, c_all, cT_bf, stats, t1buf, actbuf,
                acc, wsmall, wbig, pbanks, ps_all, identf, ones_bf, ccols,
                rowbuf, ab_rows, bcast, sq_scr, stream_hbm, schunk_res, fptmp):
    sdr0 = ctx.sdr.v
    tdr0 = ctx.tdr.v
    @block.scalar
    def _(s):
        for k in range(ND):
            if k >= 1:
                s.wait_ge(ctx.tdr.h, tdr0 + k)
            s.activation(sq_scr[:, 0:T], xhat[:, k * T:(k + 1) * T], AF.Square)
            ctx.sdr.v += 1
            s.drain().then_inc(ctx.sdr.h, 1)

    @block.tensor
    def _(t):
        for k in range(ND):
            t.wait_ge(ctx.sdr.h, sdr0 + k + 1)
            for ti in range(NT):
                t.matmul(pbanks[NT + ti][0:1, 0:512], ones_bf[:, 0:1],
                         sq_scr[:, ti * 512: ti * 512 + 512],
                         start=(k == 0), stop=(k == ND - 1))
            ctx.tdr.v += 1
            t.drain().then_inc(ctx.tdr.h, 1)
    sums_rdy = ctx.tdr.v
    _emit_ln_tail(ctx, block, h, rowbuf, ab_rows, bcast, pbanks, ones_bf, ccols,
                  sums_rdy, is_ln=False, xsrc=xhat, gamma_cols=normw_cols)
    fence(ctx, block)

    emit_topk_gating(ctx, block, lambda cc: h[:, cc * T:(cc + 1) * T], gatewT_ap,
                     gateb_row, ones_bf, c_all, cT_bf, stats, wsmall[0],
                     pbanks[0], pbanks[1], identf, topk=2, softmax_all=False)
    fence(ctx, block)

    Tc = 1024
    for tchunk in range(2):
        t0 = tchunk * Tc
        tb0 = ctx.tdr.v
        ev0 = ctx.evk.v

        @block.tensor
        def _(t, tchunk=tchunk, tb0=tb0, ev0=ev0):
            n = 0
            for tt in range(8):
                gt = tchunk * 8 + tt
                for fc in range(2):
                    pb = pbanks[4 + n % 2]
                    if n >= 2:
                        t.wait_ge(ctx.evk.h, ev0 + (n - 2) + 1)
                    t.matmul(pb[:, 0:512],
                             cT_bf[0:8, gt * 128:(gt + 1) * 128],
                             b2row_bf[0:8, fc * 512:(fc + 1) * 512],
                             start=True, stop=True)
                    ctx.tdr.v += 1
                    t.drain().then_inc(ctx.tdr.h, 1)
                    n += 1

        @block.vector
        def _(v, tchunk=tchunk, tb0=tb0):
            n = 0
            for tt in range(8):
                for fc in range(2):
                    pb = pbanks[4 + n % 2]
                    n += 1
                    v.wait_ge(ctx.tdr.h, tb0 + n)
                    v.tensor_copy(acc[:, tt * 1024 + fc * 512: tt * 1024 + fc * 512 + 512],
                                  pb[:, 0:512])
                    ctx.evk.v += 1
                    v.drain().then_inc(ctx.evk.h, 1)
        fence(ctx, block)
        for e in range(8):
            fence(ctx, block, engines=("tensor", "vector", "scalar", "gpsimd"))

            def evac_t1(v, fb, ti, ps, e=e):
                dst = t1buf[:, fb * Tc + ti * 512: fb * Tc + ti * 512 + 512]
                bc = b1_cols[:, e * 16 + fb: e * 16 + fb + 1]
                if fb < 8:
                    v.tensor_scalar(dst, ps, bc, 7.0, ALU.add, ALU.min)
                else:
                    ctx.vch.v += 1
                    v.tensor_scalar(dst, ps, bc, 7.0, ALU.add, ALU.min
                                    ).then_inc(ctx.vch.h, 1)
                    v.wait_ge(ctx.vch.h, ctx.vch.v)
                    v.tensor_scalar(dst, dst, -7.0, None, ALU.max)
            emit_linear_fm2(ctx, block, 1024, 2048,
                            mlp1_ap[e], lambda cc: xh_src(h, cc, t0, Tc),
                            evac_t1, wbig, pbanks[:4], Tloc=Tc, fb_per_dma=2,
                            evac_engine="vector")
            fence(ctx, block, engines=("tensor", "vector", "scalar", "gpsimd"))

            sd0 = ctx.sdr.v
            vd0 = ctx.vdr.v
            @block.scalar
            def _(s, e=e, vd0=vd0):
                for fb in range(8):
                    if fb >= 1:
                        s.wait_ge(ctx.vdr.h, vd0 + fb)
                    s.activation(fptmp[:, 0:Tc], t1buf[:, fb * Tc:(fb + 1) * Tc],
                                 AF.Sigmoid, scale=1.702)
                    ctx.sdr.v += 1
                    s.drain().then_inc(ctx.sdr.h, 1)

            @block.vector
            def _(v, sd0=sd0, e=e):
                for fb in range(8):
                    v.wait_ge(ctx.sdr.h, sd0 + fb + 1)
                    v.tensor_mul(fptmp[:, 0:Tc], fptmp[:, 0:Tc], t1buf[:, fb * Tc:(fb + 1) * Tc])
                    v.scalar_tensor_tensor(actbuf[:, fb * Tc:(fb + 1) * Tc],
                                           t1buf[:, (8 + fb) * Tc:(9 + fb) * Tc], 1.0,
                                           fptmp[:, 0:Tc], ALU.add, ALU.mult)
                    ctx.vdr.v += 1
                    v.drain().then_inc(ctx.vdr.h, 1)
            fence(ctx, block, engines=("tensor", "vector", "scalar"))

            def evac_y(v, tt, fc, ps, e=e, tchunk=tchunk):
                gt = tchunk * 8 + tt
                dst = acc[:, tt * 1024 + fc * 512: tt * 1024 + fc * 512 + 512]
                v.scalar_tensor_tensor(dst, ps, c_all[:, gt * 8 + e: gt * 8 + e + 1],
                                       dst, ALU.mult, ALU.add)
            emit_linear_tm(ctx, block, 1024, 1024, mlp2_ap[e],
                           lambda cc: actbuf[:, cc * Tc:(cc + 1) * Tc],
                           evac_y, wbig, pbanks[4:6], Tloc=Tc)
        fence(ctx, block)

        _emit_l1_residual(ctx, block, xhat, acc, stream_hbm, schunk_res, identf,
                          ps_all, t0, Tc)
        fence(ctx, block)


def _nv(ctx):
    return ctx.vdr.v


def xh_src(h, cc, t0, Tc):
    return h[:, cc * T + t0: cc * T + t0 + Tc]


def _emit_l1_residual(ctx, block, xhat, acc, stream_hbm, schunk_res, identf,
                      ps_all, t0, Tc, xfac=0.5, afac=0.5, acc_is_tm=True):
    tr_bank = ps_all[:, 2048:2560]
    sio_base = ctx.sio.v
    st_base = ctx.st.v
    ntt = Tc // 128

    @block.sync
    def _(sy):
        for fb in range(ND):
            if fb >= 1:
                sy.wait_ge(ctx.st.h, st_base + 16 * fb)
            sy.dma_start(out=schunk_res[:, 0:Tc],
                         in_=stream_hbm[fb * 128:(fb + 1) * 128, t0:t0 + Tc],
                         ).then_inc(ctx.sio.h, 16)
    ctx.sio.v = sio_base + 16 * ND

    for fb in range(ND):
        @block.tensor
        def _(t, fb=fb, evtarget=ctx.evk.v, ev_at=ctx.evk.v):
            if fb > 0:
                t.wait_ge(ctx.evk.h, evtarget)
            for tt in range(ntt):
                t.transpose(tr_bank[:, (tt % 4) * 128:(tt % 4) * 128 + 128],
                            acc[:, tt * 1024 + fb * 128: tt * 1024 + fb * 128 + 128],
                            identf[:])
                if tt % 4 == 3:
                    ctx.tdr.v += 1
                    t.drain().then_inc(ctx.tdr.h, 1)
                    if tt < ntt - 1:
                        t.wait_ge(ctx.evk.h, ev_at + tt // 4 + 1)
        trb = ctx.tdr.v - 2

        @block.vector
        def _(v, fb=fb, trb=trb):
            v.wait_ge(ctx.sio.h, sio_base + 16 * (fb + 1))
            v.scalar_tensor_tensor(schunk_res[:, 0:Tc],
                                   xhat[:, fb * T + t0: fb * T + t0 + Tc], xfac,
                                   schunk_res[:, 0:Tc], ALU.mult, ALU.add)
            for g2 in range(2):
                v.wait_ge(ctx.tdr.h, trb + g2 + 1)
                v.scalar_tensor_tensor(schunk_res[:, g2 * 512:(g2 + 1) * 512],
                                       tr_bank[:, 0:512], afac,
                                       schunk_res[:, g2 * 512:(g2 + 1) * 512],
                                       ALU.mult, ALU.add)
                ctx.evk.v += 1
                v.drain().then_inc(ctx.evk.h, 1)

        @block.scalar
        def _(s, fb=fb, evt=None):
            s.wait_ge(ctx.evk.h, ctx.evk.v if evt is None else evt)
            s.dma_start(out=stream_hbm[fb * 128:(fb + 1) * 128, t0:t0 + Tc],
                        in_=schunk_res[:, 0:Tc]).then_inc(ctx.st.h, 16)
        ctx.st.v += 16


def emit_l1_attn(ctx, block, xhat, qkw_ap, qkb_cols, vw_ap, gains_row_bf,
                 rowrecip, tri01, trimask, identf, q_fm, k_fm, vtm, gains_bc,
                 pbuf, attT, ytmp, fptmp, y_tm, y_fm, ps_all, wsmall, wbig,
                 ones_bf, stream_hbm, schunk_res, cprojw_ap):
    SC = 0.125
    banks = [ps_all[:, i * 512:(i + 1) * 512] for i in range(8)]

    @block.tensor
    def _(t, prev=ctx.tdr.v):
        t.matmul(banks[0][:, 0:48], ones_bf[0:1, 0:128], gains_row_bf[0:1, 0:48],
                 start=True, stop=True)
        ctx.tdr.v += 1
        t.drain().then_inc(ctx.tdr.h, 1)
    gb = ctx.tdr.v

    @block.vector
    def _(v, gb=gb):
        v.wait_ge(ctx.tdr.h, gb)
        v.tensor_copy(gains_bc[:], banks[0][:, 0:48])
        ctx.vdr.v += 1
        v.drain().then_inc(ctx.vdr.h, 1)
    fence(ctx, block)

    def evac_q(s, fb, ti, ps):
        s.activation(q_fm[:, fb * T + ti * 512: fb * T + ti * 512 + 512], ps,
                     AF.Identity, bias=qkb_cols[:, fb:fb + 1])
    emit_linear_fm2(ctx, block, 1024, 1024, qkw_ap[:, 0:1024],
                    lambda cc: xhat[:, cc * T:(cc + 1) * T], evac_q,
                    wsmall, banks, Tloc=T, fb_per_dma=2)
    fence(ctx, block)

    def evac_k(s, fb, ti, ps):
        s.activation(k_fm[:, fb * T + ti * 512: fb * T + ti * 512 + 512], ps,
                     AF.Identity, bias=qkb_cols[:, 8 + fb:9 + fb])
    emit_linear_fm2(ctx, block, 1024, 1024, qkw_ap[:, 1024:2048],
                    lambda cc: xhat[:, cc * T:(cc + 1) * T], evac_k,
                    wsmall, banks, Tloc=T, fb_per_dma=2)
    fence(ctx, block)

    def evac_v(v, tt, fc, ps):
        v.tensor_copy(vtm[:, tt * 1024 + fc * 512: tt * 1024 + fc * 512 + 512], ps)
    emit_linear_tm(ctx, block, 1024, 1024, vw_ap,
                   lambda cc: xhat[:, cc * T:(cc + 1) * T], evac_v,
                   wbig, banks[4:6], Tloc=T)
    fence(ctx, block)

    sc_ps = ps_all[:, 0:2048]
    tr_ps = banks[4]
    y_ps = banks[5][0:64, 0:128]
    prev_yevac = ctx.sdr.v
    for h in range(16):
        for qt in range(16):
            q0 = qt * 128
            klen = q0 + 128
            nkc = (klen + 511) // 512
            nkt = klen // 128
            b = (h % 2) * 64
            qtile = q_fm[:, (h // 2) * T:(h // 2) * T + T]
            ktile = k_fm[:, (h // 2) * T:(h // 2) * T + T]

            @block.tensor
            def _(t, qtile=qtile, ktile=ktile, b=b, q0=q0, klen=klen, nkc=nkc,
                  prev=prev_yevac):
                t.wait_ge(ctx.sdr.h, prev)
                for kc in range(nkc):
                    k0 = kc * 512
                    kw = min(512, klen - k0)
                    t.matmul(sc_ps[:, k0:k0 + kw], qtile[b:b + 64, q0:q0 + 128],
                             ktile[b:b + 64, k0:k0 + kw], start=True, stop=True)
                ctx.tdr.v += 1
                t.drain().then_inc(ctx.tdr.h, 1)
            sc_rdy = ctx.tdr.v

            @block.vector
            def _(v, q0=q0, klen=klen, sc_rdy=sc_rdy):
                vb = ctx.vch
                def hs(inst):
                    vb.v += 1
                    inst.then_inc(vb.h, 1)
                    v.wait_ge(vb.h, vb.v)
                v.wait_ge(ctx.tdr.h, sc_rdy)
                hs(v.tensor_add(sc_ps[:, q0:q0 + 128], sc_ps[:, q0:q0 + 128], trimask[:]))
                hs(v.reduce_max(ytmp[:, 0:1], sc_ps[:, 0:klen], axis=AX.X))
                hs(v.tensor_scalar_mul(ytmp[:, 1:2], ytmp[:, 0:1], -SC))
                ctx.vdr.v += 1
                v.drain().then_inc(ctx.vdr.h, 1)
            mask_rdy = ctx.vdr.v

            @block.scalar
            def _(s, klen=klen, nkc=nkc, mask_rdy=mask_rdy):
                s.wait_ge(ctx.vdr.h, mask_rdy)
                for kc in range(nkc):
                    k0 = kc * 512
                    kw = min(512, klen - k0)
                    s.activation(pbuf[:, k0:k0 + kw], sc_ps[:, k0:k0 + kw], AF.Exp,
                                 bias=ytmp[:, 1:2], scale=SC,
                                 accum_out=ytmp[:, 2 + kc:3 + kc])
                ctx.sdr.v += 1
                s.drain().then_inc(ctx.sdr.h, 1)
            exp_rdy = ctx.sdr.v

            @block.vector
            def _(v, h=h, qt=qt, q0=q0, klen=klen, nkc=nkc, exp_rdy=exp_rdy):
                vb = ctx.vch
                def hs(inst):
                    vb.v += 1
                    inst.then_inc(vb.h, 1)
                    v.wait_ge(vb.h, vb.v)
                v.wait_ge(ctx.sdr.h, exp_rdy)
                hs(v.reduce_sum(ytmp[:, 8:9], ytmp[:, 2:2 + nkc], axis=AX.X))
                hs(v.reciprocal(ytmp[:, 9:10], ytmp[:, 8:9]))
                hs(v.tensor_mul(ytmp[:, 10:11], gains_bc[:, h:h + 1], ytmp[:, 9:10]))
                for kc in range(nkc):
                    k0 = kc * 512
                    kw = min(512, klen - k0)
                    hs(v.tensor_scalar(pbuf[:, k0:k0 + kw], pbuf[:, k0:k0 + kw],
                                       ytmp[:, 10:11], None, ALU.mult))
                hs(v.scalar_tensor_tensor(pbuf[:, q0:q0 + 128], identf[:],
                                          gains_bc[:, 16 + h:17 + h],
                                          pbuf[:, q0:q0 + 128], ALU.mult, ALU.add))
                hs(v.tensor_mul(ytmp[:, 11:12], gains_bc[:, 32 + h:33 + h],
                                rowrecip[:, qt:qt + 1]))
                hs(v.tensor_scalar_mul(ytmp[:, 12:13], ytmp[:, 11:12], -1.0))
                if q0 > 0:
                    hs(v.tensor_scalar(pbuf[:, 0:q0], pbuf[:, 0:q0],
                                       ytmp[:, 12:13], None, ALU.add))
                hs(v.scalar_tensor_tensor(pbuf[:, q0:q0 + 128], tri01[:],
                                          ytmp[:, 12:13], pbuf[:, q0:q0 + 128],
                                          ALU.mult, ALU.add))
                ctx.vdr.v += 1
                v.drain().then_inc(ctx.vdr.h, 1)
            mods_rdy = ctx.vdr.v

            ngrp = (nkt + 3) // 4
            for grp in range(ngrp):
                n = min(4, nkt - grp * 4)

                @block.tensor
                def _(t, grp=grp, n=n, mods_rdy=mods_rdy, evprev=ctx.vdr.v):
                    t.wait_ge(ctx.vdr.h, mods_rdy)
                    if grp > 0:
                        t.wait_ge(ctx.vdr.h, evprev)
                    for j in range(n):
                        t.transpose(tr_ps[:, j * 128:(j + 1) * 128],
                                    pbuf[:, (grp * 4 + j) * 128:(grp * 4 + j + 1) * 128],
                                    identf[:])
                    ctx.tdr.v += 1
                    t.drain().then_inc(ctx.tdr.h, 1)
                trd = ctx.tdr.v

                @block.vector
                def _(v, grp=grp, n=n, trd=trd):
                    v.wait_ge(ctx.tdr.h, trd)
                    v.tensor_copy(attT[:, grp * 4 * 128: grp * 4 * 128 + n * 128],
                                  tr_ps[:, 0:n * 128])
                    ctx.vdr.v += 1
                    v.drain().then_inc(ctx.vdr.h, 1)
            attT_rdy = ctx.vdr.v

            @block.tensor
            def _(t, h=h, nkt=nkt, attT_rdy=attT_rdy):
                t.wait_ge(ctx.vdr.h, attT_rdy)
                for kt in range(nkt):
                    t.matmul(y_ps, vtm[:, kt * 1024 + h * 64: kt * 1024 + h * 64 + 64],
                             attT[:, kt * 128:(kt + 1) * 128],
                             start=(kt == 0), stop=(kt == nkt - 1))
                ctx.tdr.v += 1
                t.drain().then_inc(ctx.tdr.h, 1)
            pv_rdy = ctx.tdr.v

            @block.scalar
            def _(s, h=h, qt=qt, q0=q0, pv_rdy=pv_rdy):
                s.wait_ge(ctx.tdr.h, pv_rdy)
                b2 = (h % 2) * 64
                fc = h // 2
                s.activation(y_fm[b2:b2 + 64, fc * T + q0: fc * T + q0 + 128],
                             y_ps, AF.Copy)
                ctx.sdr.v += 1
                s.drain().then_inc(ctx.sdr.h, 1)
            prev_yevac = ctx.sdr.v

    fence(ctx, block)
    _emit_out_residual(ctx, block, cprojw_ap, y_fm, wsmall, banks,
                       stream_hbm, schunk_res, C=1024, bias_cols=None,
                       scale_cols=None, t0=0, Tc=T)
    fence(ctx, block)


def build_layer1_program(parts=("ffn1", "attn", "ffn2")):
    nc = bass.Bass("TRN2", target_bir_lowering=False)
    stream_in = nc.dram_tensor("stream_in", [D, T], F32, kind="ExternalInput").ap()
    stream_out = nc.dram_tensor("stream_out", [D, T], F32, kind="ExternalOutput").ap()
    gatewT = nc.dram_tensor("gatewT", [1024, 8], F32, kind="ExternalInput").ap()
    gateb = nc.dram_tensor("gateb", [1, 8], F32, kind="ExternalInput").ap()
    mlp1 = nc.dram_tensor("mlp1", [8, 1024, 2048], F32, kind="ExternalInput")
    mlp2 = nc.dram_tensor("mlp2", [8, 1024, 1024], F32, kind="ExternalInput")
    b1c_in = nc.dram_tensor("b1c", [128, 128], F32, kind="ExternalInput").ap()
    b2r_in = nc.dram_tensor("b2r", [8, 1024], F32, kind="ExternalInput").ap()
    normw_in = nc.dram_tensor("normw", [128, 8], F32, kind="ExternalInput").ap()
    qkw = nc.dram_tensor("qkw", [1024, 2048], F32, kind="ExternalInput").ap()
    qkb_in = nc.dram_tensor("qkb", [128, 16], F32, kind="ExternalInput").ap()
    vw = nc.dram_tensor("vw", [1024, 1024], F32, kind="ExternalInput").ap()
    cprojw = nc.dram_tensor("cprojw", [1024, 1024], F32, kind="ExternalInput").ap()
    gains_in = nc.dram_tensor("gains", [1, 48], F32, kind="ExternalInput").ap()
    rr_in = nc.dram_tensor("rowrecip", [128, 16], F32, kind="ExternalInput").ap()
    tri_in = nc.dram_tensor("tri", [128, 128], F32, kind="ExternalInput").ap()
    tri01_in = nc.dram_tensor("tri01", [128, 128], F32, kind="ExternalInput").ap()
    identf_in = nc.dram_tensor("identf", [128, 128], F32, kind="ExternalInput").ap()
    ones_in = nc.dram_tensor("ones", [128, 128], BF16, kind="ExternalInput").ap()
    ccols_in = nc.dram_tensor("ccols", [128, len(CONST_VALS)], F32, kind="ExternalInput").ap()

    ctx = Ctx(nc)
    bfa = ctx.sb("bfa", [128, 68608], BF16)
    fpa = ctx.sb("fpa", [128, 13104], F32)
    wbig = [ctx.sb(f"wbg{i}", [128, 4096], BF16) for i in range(2)]
    ones_bf = ctx.sb("ones_bf", [128, 128], BF16)
    ccols = ctx.sb("ccols_sb", [128, len(CONST_VALS)], F32)
    identf = ctx.sb("identf_sb", [128, 128], F32)
    trimask = ctx.sb("tri_sb", [128, 128], F32)
    tri01 = ctx.sb("tri01_sb", [128, 128], F32)
    rowrecip = ctx.sb("rr_sb", [128, 16], F32)
    normw = ctx.sb("normw_sb", [128, 8], F32)
    qkb = ctx.sb("qkb_sb", [128, 16], F32)
    b1c = ctx.sb("b1c_sb", [128, 128], F32)
    gateb_row = ctx.sb("gateb_sb", [1, 8], BF16)
    gains_row = ctx.sb("gains_sb", [1, 48], BF16)
    ps_all = ctx.psum("ps_all", [128, 4096], F32)
    pbanks = [ps_all[:, i * 512:(i + 1) * 512] for i in range(8)]

    xhat = bfa[:, 0:16384]
    h = bfa[:, 16384:32768]
    sq_scr = bfa[:, 32768:34816]
    bcast = bfa[:, 34816:38912]
    ab_rows = bfa[:, 38912:43008]
    t1buf = bfa[:, 32768:49152]
    actbuf = bfa[:, 49152:57344]
    cT_bf = bfa[:, 57344:59392]
    b2row_bf = bfa[:, 67584:68608]
    q_fm = bfa[:, 16384:32768]
    k_fm = bfa[:, 32768:49152]
    vtm = bfa[:, 49152:65536]
    y_fm = bfa[:, 0:16384]
    attT = bfa[:, 65536:67584]
    schunk = fpa[:, 0:2048]
    schunk_res = fpa[:, 0:2048]
    rowbuf = fpa[:, 2048:3584]
    fptmp = fpa[:, 3584:4608]
    ytmp = fpa[:, 4608:4640]
    acc = fpa[:, 4640:12832]
    pbuf = acc[:, 0:2048]
    c_all = fpa[:, 12832:12960]
    stats = fpa[:, 12960:13040]
    gains_bc = fpa[:, 13040:13088]

    with nc.Block() as block:
        emit_boot(ctx, block)

        @block.gpsimd
        def _(g):
            g.dma_start(out=ones_bf[:], in_=ones_in).then_inc(ctx.sio.h, 16)
            g.dma_start(out=ccols[:], in_=ccols_in).then_inc(ctx.sio.h, 16)
            g.dma_start(out=identf[:], in_=identf_in).then_inc(ctx.sio.h, 16)
            g.dma_start(out=trimask[:], in_=tri_in).then_inc(ctx.sio.h, 16)
            g.dma_start(out=tri01[:], in_=tri01_in).then_inc(ctx.sio.h, 16)
            g.dma_start(out=rowrecip[:], in_=rr_in).then_inc(ctx.sio.h, 16)
            g.dma_start(out=normw[:], in_=normw_in).then_inc(ctx.sio.h, 16)
            g.dma_start(out=qkb[:], in_=qkb_in).then_inc(ctx.sio.h, 16)
            g.dma_start(out=b1c[:], in_=b1c_in).then_inc(ctx.sio.h, 16)
            g.dma_start(out=gateb_row[:], in_=gateb).then_inc(ctx.sio.h, 16)
            g.dma_start(out=gains_row[:], in_=gains_in).then_inc(ctx.sio.h, 16)
            g.dma_start(out=b2row_bf[0:8, :], in_=b2r_in).then_inc(ctx.sio.h, 16)
        ctx.sio.v += 192
        pro_io = ctx.sio.v
        for en in ("tensor", "vector", "scalar", "sync"):
            getattr(block, en)(lambda e: e.wait_ge(ctx.sio.h, pro_io))

        @block.sync
        def _(sy):
            sy.dma_start(out=stream_out, in_=stream_in).then_inc(ctx.sio.h, 16)
        ctx.sio.v += 16
        fence(ctx, block)

        def do_ffn():
            emit_stream_ln(ctx, block, stream_out, schunk, xhat, sq_scr, rowbuf,
                           ab_rows, bcast, pbanks, ones_bf, ccols)
            fence(ctx, block)
            emit_l1_ffn(ctx, block, xhat, h, normw, gatewT, gateb_row,
                        [mlp1.ap()[e] for e in range(8)], b1c,
                        [mlp2.ap()[e] for e in range(8)], b2row_bf, c_all, cT_bf,
                        stats, t1buf, actbuf, acc, wbig, wbig, pbanks, ps_all,
                        identf, ones_bf, ccols, rowbuf, ab_rows, bcast, sq_scr,
                        stream_out, schunk_res, fptmp)
            fence(ctx, block)

        if "ffn1" in parts:
            do_ffn()
        if "attn" in parts:
            emit_stream_ln(ctx, block, stream_out, schunk, xhat, sq_scr, rowbuf,
                           ab_rows, bcast, pbanks, ones_bf, ccols)
            fence(ctx, block)
            emit_l1_attn(ctx, block, xhat, qkw, qkb, vw, gains_row, rowrecip, tri01,
                         trimask, identf, q_fm, k_fm, vtm, gains_bc, pbuf, attT,
                         ytmp, fptmp, None, y_fm, ps_all, wbig, wbig, ones_bf,
                         stream_out, schunk_res, cprojw)
            fence(ctx, block)
        if "ffn2" in parts:
            do_ffn()

        stv = ctx.st.v
        @block.sync
        def _(sy):
            sy.wait_ge(ctx.st.h, stv)

    ctx.es.close()
    nc.gpsimd.sem_clear(ctx.clear_rng)
    return nc


def prep_layer1_inputs(ip):
    out = {}
    out["gatewT"] = np.ascontiguousarray(ip["L1_ffn_gate_w"].T)
    out["gateb"] = np.ascontiguousarray(ip["L1_ffn_gate_b"].reshape(1, 8))
    m1 = ip["L1_ffn_mlp1_w"]
    m1d = np.empty((8, 1024, 2048), np.float32)
    b1 = ip["L1_ffn_mlp1_b"]
    b1c = np.zeros((128, 128), np.float32)
    for e in range(8):
        de = np.concatenate([m1[e, 0::2, :], m1[e, 1::2, :]], axis=0)
        m1d[e] = de.T
        bde = np.concatenate([b1[e, 0::2], b1[e, 1::2]])
        b1c[:, e * 16:(e + 1) * 16] = bde.reshape(16, 128).T
    out["mlp1"] = np.ascontiguousarray(m1d)
    out["b1c"] = b1c
    m2 = ip["L1_ffn_mlp2_w"]
    m2t = np.stack([np.ascontiguousarray(m2[e].T) for e in range(8)], axis=0)
    out["mlp2"] = np.ascontiguousarray(m2t)
    out["b2r"] = np.ascontiguousarray(ip["L1_ffn_mlp2_b"])
    out["normw"] = np.ascontiguousarray(ip["L1_ffn_norm_w"].reshape(8, 128).T)
    out["qkw"] = np.ascontiguousarray(ip["L1_attn_qk_w"])
    out["qkb"] = np.ascontiguousarray(ip["L1_attn_qk_b"].reshape(16, 128).T)
    out["vw"] = np.ascontiguousarray(ip["L1_attn_v_w"])
    out["cprojw"] = np.ascontiguousarray(ip["L1_attn_cproj_w"])
    gains = np.concatenate([ip["L1_attn_resid_gain"].reshape(16),
                            ip["L1_attn_skip_gain"].reshape(16),
                            ip["L1_attn_centre_gain"].reshape(16)]).reshape(1, 48)
    out["gains"] = np.ascontiguousarray(gains.astype(np.float32))
    rr = np.zeros((128, 16), np.float32)
    for qt in range(16):
        rr[:, qt] = 1.0 / (qt * 128 + np.arange(128) + 1.0)
    out["rowrecip"] = rr
    i = np.arange(128)
    out["tri01"] = (i[None, :] <= i[:, None]).astype(np.float32)
    return out



def emit_swiglu_pair(ctx, block, w_ap, b1col, b2col, src_fn, out_fn, utmp,
                     wslots, pbanks, Tc, F2):
    ncc = 8
    nfb = 2 * F2 // 128
    ntc = Tc // 512
    nslot = len(wslots)
    ngrp = nfb // 2
    wbase, tbase, ebase, sbase = ctx.wdma.v, ctx.tdr.v, ctx.evk.v, ctx.sdr.v

    @block.gpsimd
    def _(g):
        for gi in range(ngrp):
            f0 = gi * 256
            slot = wslots[gi % nslot]
            if gi >= nslot:
                g.wait_ge(ctx.tdr.h, tbase + (gi - nslot + 1))
            g.dma_start(out=slot[:, 0:ncc * 256].rearrange("c (k m) -> c k m", m=256),
                        in_=w_ap[:, f0:f0 + 256].rearrange("(k c) m -> c k m", c=128),
                        ).then_inc(ctx.wdma.h, 16)
    ctx.wdma.v = wbase + 16 * ngrp

    @block.tensor
    def _(t):
        for gi in range(ngrp):
            t.wait_ge(ctx.wdma.h, wbase + 16 * (gi + 1))
            slot = wslots[gi % nslot]
            if gi >= 1:
                t.wait_ge(ctx.evk.h, ebase + gi)
            for half in range(2):
                foff = half * 128
                for ti in range(ntc):
                    psb = pbanks[half * ntc + ti]
                    for cc in range(ncc):
                        t.matmul(psb[:, 0:512],
                                 slot[:, cc * 256 + foff: cc * 256 + foff + 128],
                                 src_fn(cc)[:, ti * 512:(ti + 1) * 512],
                                 start=(cc == 0), stop=(cc == ncc - 1))
            ctx.tdr.v += 1
            t.drain().then_inc(ctx.tdr.h, 1)

    @block.scalar
    def _(s):
        for gi in range(ngrp):
            s.wait_ge(ctx.tdr.h, tbase + gi + 1)
            for ti in range(ntc):
                if b1col is not None:
                    s.activation(utmp[:, ti * 512:(ti + 1) * 512], pbanks[ti][:, 0:512],
                                 AF.Silu, bias=b1col[:, gi:gi + 1])
                else:
                    s.activation(utmp[:, ti * 512:(ti + 1) * 512], pbanks[ti][:, 0:512],
                                 AF.Silu)
            ctx.sdr.v += 1
            s.drain().then_inc(ctx.sdr.h, 1)

    @block.vector
    def _(v):
        for gi in range(ngrp):
            v.wait_ge(ctx.sdr.h, sbase + gi + 1)
            for ti in range(ntc):
                dst = out_fn(gi, ti)
                if b2col is not None:
                    v.scalar_tensor_tensor(dst, pbanks[ntc + ti][:, 0:512],
                                           b2col[:, gi:gi + 1],
                                           utmp[:, ti * 512:(ti + 1) * 512],
                                           ALU.add, ALU.mult)
                else:
                    v.tensor_mul(dst, pbanks[ntc + ti][:, 0:512],
                                 utmp[:, ti * 512:(ti + 1) * 512])
            ctx.evk.v += 1
            v.drain().then_inc(ctx.evk.h, 1)


def emit_plain_residual(ctx, block, src_fm, stream_hbm, schunk_res, factor=1.0,
                        bias_cols=None):
    sio_base = ctx.sio.v
    st_base = ctx.st.v
    ev0 = ctx.evk.v

    @block.sync
    def _(sy):
        for fb in range(ND):
            if fb >= 1:
                sy.wait_ge(ctx.st.h, st_base + 16 * fb)
            sy.dma_start(out=schunk_res[:, 0:T],
                         in_=stream_hbm[fb * 128:(fb + 1) * 128, :],
                         ).then_inc(ctx.sio.h, 16)
    ctx.sio.v = sio_base + 16 * ND

    @block.vector
    def _(v):
        for fb in range(ND):
            v.wait_ge(ctx.sio.h, sio_base + 16 * (fb + 1))
            v.scalar_tensor_tensor(schunk_res[:, 0:T], src_fm[:, fb * T:(fb + 1) * T],
                                   factor, schunk_res[:, 0:T], ALU.mult, ALU.add)
            if bias_cols is not None:
                ctx.vch.v += 1
                v.nop().then_inc(ctx.vch.h, 1)
                v.wait_ge(ctx.vch.h, ctx.vch.v)
                v.tensor_scalar_add(schunk_res[:, 0:T], schunk_res[:, 0:T],
                                    bias_cols[:, fb:fb + 1])
            ctx.evk.v += 1
            v.drain().then_inc(ctx.evk.h, 1)

    @block.scalar
    def _(s):
        for fb in range(ND):
            s.wait_ge(ctx.evk.h, ev0 + fb + 1)
            s.dma_start(out=stream_hbm[fb * 128:(fb + 1) * 128, :],
                        in_=schunk_res[:, 0:T]).then_inc(ctx.st.h, 16)
    ctx.st.v = st_base + 16 * ND


_L2STAGES = 99


def emit_l2_ffn(ctx, block, xhat, h0, h1, actbuf, shh, moe_fm, acc,
                lin0_ap, lin0b, swi1_ap, swi1b1, swi1b2, gatewT_ap,
                expw13, expw2, sh13_ap, sh2_ap, lin1_ap, lin1b,
                swi2_ap, swi2b1, swi2b2, c_all, cT_bf, stats,
                wbig, pbanks, ps_all, identf, ones_bf, ccols, utmp,
                stream_hbm, schunk_res):
    banks = pbanks
    def evac_lin0(s, fb, ti, ps):
        s.activation(h0[:, fb * T + ti * 512: fb * T + ti * 512 + 512], ps,
                     AF.Identity, bias=lin0b[:, fb:fb + 1])
    emit_linear_fm2(ctx, block, 1024, 1024, lin0_ap,
                    lambda cc: xhat[:, cc * T:(cc + 1) * T], evac_lin0,
                    wbig, banks, Tloc=T, fb_per_dma=2)
    fence(ctx, block)
    if _L2STAGES < 2:
        return
    for tch in range(2):
        t0 = tch * 1024
        emit_swiglu_pair(ctx, block, swi1_ap, swi1b1, swi1b2,
                         lambda cc, t0=t0: h0[:, cc * T + t0: cc * T + t0 + 1024],
                         lambda gi, ti, t0=t0: h1[:, gi * T + t0 + ti * 512:
                                                  gi * T + t0 + ti * 512 + 512],
                         utmp, wbig, banks, 1024, 1024)
        fence(ctx, block, engines=("tensor", "vector", "scalar", "gpsimd"))
    fence(ctx, block)

    if _L2STAGES < 3:
        return
    emit_topk_gating(ctx, block, lambda cc: h1[:, cc * T:(cc + 1) * T], gatewT_ap,
                     None, ones_bf, c_all, cT_bf, stats, wbig[0],
                     banks[0], banks[1], identf, topk=4, softmax_all=True)
    fence(ctx, block)

    if _L2STAGES < 4:
        return
    Tc = 1024
    for tchunk in range(2):
        t0 = tchunk * Tc
        @block.vector
        def _(v):
            v.memset(acc[:, 0:8192], 0.0)
            ctx.vdr.v += 1
            v.drain().then_inc(ctx.vdr.h, 1)
        fence(ctx, block, engines=("tensor", "vector", "scalar"))

        for e in range(8):
            emit_swiglu_pair(ctx, block, expw13[e], None, None,
                             lambda cc, t0=t0: h1[:, cc * T + t0: cc * T + t0 + Tc],
                             lambda gi, ti: actbuf[:, gi * Tc + ti * 512:
                                                   gi * Tc + ti * 512 + 512],
                             utmp, wbig, banks, Tc, 1024)
            fence(ctx, block, engines=("tensor", "vector", "scalar", "gpsimd"))
            def evac_y(v, tt, fc, ps, e=e, tchunk=tchunk):
                gt = tchunk * 8 + tt
                dst = acc[:, tt * 1024 + fc * 512: tt * 1024 + fc * 512 + 512]
                v.scalar_tensor_tensor(dst, ps, c_all[:, gt * 8 + e: gt * 8 + e + 1],
                                       dst, ALU.mult, ALU.add)
            emit_linear_tm(ctx, block, 1024, 1024, expw2[e],
                           lambda cc: actbuf[:, cc * Tc:(cc + 1) * Tc],
                           evac_y, wbig, banks[4:6], Tloc=Tc)
            fence(ctx, block, engines=("tensor", "vector", "scalar", "gpsimd"))

        if _L2STAGES < 5:
            continue
        emit_swiglu_pair(ctx, block, sh13_ap, None, None,
                         lambda cc, t0=t0: h1[:, cc * T + t0: cc * T + t0 + Tc],
                         lambda gi, ti: shh[:, gi * Tc + ti * 512: gi * Tc + ti * 512 + 512],
                         utmp, wbig, banks, Tc, 2048)
        fence(ctx, block, engines=("tensor", "vector", "scalar", "gpsimd"))

        def evac_sh(v, tt, fc, ps):
            dst = acc[:, tt * 1024 + fc * 512: tt * 1024 + fc * 512 + 512]
            v.scalar_tensor_tensor(dst, ps, 1.0, dst, ALU.mult, ALU.add)
        for chalf in range(2):
            emit_linear_tm(ctx, block, 1024, 1024, sh2_ap[chalf * 1024:(chalf + 1) * 1024],
                           lambda cc, chalf=chalf: shh[:, (chalf * 8 + cc) * Tc:
                                                       (chalf * 8 + cc + 1) * Tc],
                           evac_sh, wbig, banks[4:6], Tloc=Tc)
            fence(ctx, block, engines=("tensor", "vector", "scalar", "gpsimd"))
        fence(ctx, block)

        if _L2STAGES < 6:
            continue
        tr_bank = ps_all[:, 2048:2560]
        for fb in range(ND):
            @block.tensor
            def _(t, fb=fb, ev_at=ctx.vdr.v):
                if fb > 0:
                    t.wait_ge(ctx.vdr.h, ev_at)
                for tt in range(8):
                    t.transpose(tr_bank[:, (tt % 4) * 128:(tt % 4) * 128 + 128],
                                acc[:, tt * 1024 + fb * 128: tt * 1024 + fb * 128 + 128],
                                identf[:])
                    if tt % 4 == 3:
                        ctx.tdr.v += 1
                        t.drain().then_inc(ctx.tdr.h, 1)
                        if tt < 7:
                            t.wait_ge(ctx.vdr.h, ev_at + tt // 4 + 1)
            trb = ctx.tdr.v - 2

            @block.vector
            def _(v, fb=fb, trb=trb, t0=t0):
                for g2 in range(2):
                    v.wait_ge(ctx.tdr.h, trb + g2 + 1)
                    v.tensor_copy(moe_fm[:, fb * T + t0 + g2 * 512:
                                         fb * T + t0 + g2 * 512 + 512],
                                  tr_bank[:, 0:512])
                    ctx.vdr.v += 1
                    v.drain().then_inc(ctx.vdr.h, 1)
        fence(ctx, block)

    if _L2STAGES < 7:
        return
    def evac_lin1(s, fb, ti, ps):
        s.activation(h0[:, fb * T + ti * 512: fb * T + ti * 512 + 512], ps,
                     AF.Identity, bias=lin1b[:, fb:fb + 1])
    emit_linear_fm2(ctx, block, 1024, 1024, lin1_ap,
                    lambda cc: moe_fm[:, cc * T:(cc + 1) * T], evac_lin1,
                    wbig, banks, Tloc=T, fb_per_dma=2)
    fence(ctx, block)
    for tch in range(2):
        t0 = tch * 1024
        emit_swiglu_pair(ctx, block, swi2_ap, swi2b1, swi2b2,
                         lambda cc, t0=t0: h0[:, cc * T + t0: cc * T + t0 + 1024],
                         lambda gi, ti, t0=t0: h1[:, gi * T + t0 + ti * 512:
                                                  gi * T + t0 + ti * 512 + 512],
                         utmp, wbig, banks, 1024, 1024)
        fence(ctx, block, engines=("tensor", "vector", "scalar", "gpsimd"))
    fence(ctx, block)
    emit_plain_residual(ctx, block, h1, stream_hbm, schunk_res, factor=0.5)
    fence(ctx, block)


def emit_l2_attn(ctx, block, xhat, kqvw_ap, kqvb_cols, vbias_row, slopes_row_bf,
                 mtab, trimask, identf, q_fm, k_fm, vtm, slope_bc, pbuf, attT,
                 ytmp, y_fm, ps_all, wbig, ones_bf, stream_hbm, schunk_res,
                 projw_ap, projb_cols):
    SC = 0.125
    banks = [ps_all[:, i * 512:(i + 1) * 512] for i in range(8)]

    @block.tensor
    def _(t):
        t.matmul(banks[0][:, 0:16], ones_bf[0:1, 0:128], slopes_row_bf[0:1, 0:16],
                 start=True, stop=True)
        ctx.tdr.v += 1
        t.drain().then_inc(ctx.tdr.h, 1)
    gb = ctx.tdr.v

    @block.vector
    def _(v, gb=gb):
        v.wait_ge(ctx.tdr.h, gb)
        v.tensor_scalar(slope_bc[:], banks[0][:, 0:16], 8.0, None, ALU.mult)
        ctx.vdr.v += 1
        v.drain().then_inc(ctx.vdr.h, 1)
    fence(ctx, block)

    def evac_k(s, fb, ti, ps):
        s.activation(k_fm[:, fb * T + ti * 512: fb * T + ti * 512 + 512], ps,
                     AF.Identity, bias=kqvb_cols[:, fb:fb + 1])
    emit_linear_fm2(ctx, block, 1024, 1024, kqvw_ap[:, 0:1024],
                    lambda cc: xhat[:, cc * T:(cc + 1) * T], evac_k,
                    wbig, banks, Tloc=T, fb_per_dma=2)
    fence(ctx, block)

    def evac_q(s, fb, ti, ps):
        s.activation(q_fm[:, fb * T + ti * 512: fb * T + ti * 512 + 512], ps,
                     AF.Identity, bias=kqvb_cols[:, 8 + fb:9 + fb])
    emit_linear_fm2(ctx, block, 1024, 1024, kqvw_ap[:, 1024:2048],
                    lambda cc: xhat[:, cc * T:(cc + 1) * T], evac_q,
                    wbig, banks, Tloc=T, fb_per_dma=2)
    fence(ctx, block)

    def extra_vb(t, tt, fc, ps):
        t.matmul(ps, ones_bf[0:1, 0:128], vbias_row[0:1, fc * 512:(fc + 1) * 512],
                 start=False, stop=True)

    def evac_v2(v, tt, fc, ps):
        v.tensor_copy(vtm[:, tt * 1024 + fc * 512: tt * 1024 + fc * 512 + 512], ps)
    emit_linear_tm(ctx, block, 1024, 1024, kqvw_ap[:, 2048:3072],
                   lambda cc: xhat[:, cc * T:(cc + 1) * T], evac_v2,
                   wbig, banks[4:6], Tloc=T, extra_mm=extra_vb)
    fence(ctx, block)

    sc_ps = ps_all[:, 0:2048]
    tr_ps = banks[4]
    y_ps = banks[5][0:64, 0:128]
    prev_yevac = ctx.sdr.v
    for h in range(16):
        for qt in range(16):
            q0 = qt * 128
            klen = q0 + 128
            nkc = (klen + 511) // 512
            nkt = klen // 128
            b = (h % 2) * 64
            qtile = q_fm[:, (h // 2) * T:(h // 2) * T + T]
            ktile = k_fm[:, (h // 2) * T:(h // 2) * T + T]

            @block.tensor
            def _(t, qtile=qtile, ktile=ktile, b=b, q0=q0, klen=klen, nkc=nkc,
                  prev=prev_yevac):
                t.wait_ge(ctx.sdr.h, prev)
                for kc in range(nkc):
                    k0 = kc * 512
                    kw = min(512, klen - k0)
                    t.matmul(sc_ps[:, k0:k0 + kw], qtile[b:b + 64, q0:q0 + 128],
                             ktile[b:b + 64, k0:k0 + kw], start=True, stop=True)
                ctx.tdr.v += 1
                t.drain().then_inc(ctx.tdr.h, 1)
            sc_rdy = ctx.tdr.v

            @block.vector
            def _(v, h=h, q0=q0, klen=klen, nkc=nkc, sc_rdy=sc_rdy):
                vb = ctx.vch
                def hs(inst):
                    vb.v += 1
                    inst.then_inc(vb.h, 1)
                    v.wait_ge(vb.h, vb.v)
                v.wait_ge(ctx.tdr.h, sc_rdy)
                for kc in range(nkc):
                    k0 = kc * 512
                    kw = min(512, klen - k0)
                    hs(v.scalar_tensor_tensor(sc_ps[:, k0:k0 + kw],
                                              mtab[:, 1920 - q0 + k0: 1920 - q0 + k0 + kw],
                                              slope_bc[:, h:h + 1],
                                              sc_ps[:, k0:k0 + kw], ALU.mult, ALU.add))
                hs(v.tensor_add(sc_ps[:, q0:q0 + 128], sc_ps[:, q0:q0 + 128], trimask[:]))
                hs(v.reduce_max(ytmp[:, 0:1], sc_ps[:, 0:klen], axis=AX.X))
                hs(v.tensor_scalar_mul(ytmp[:, 1:2], ytmp[:, 0:1], -SC))
                ctx.vdr.v += 1
                v.drain().then_inc(ctx.vdr.h, 1)
            mask_rdy = ctx.vdr.v

            @block.scalar
            def _(s, klen=klen, nkc=nkc, mask_rdy=mask_rdy):
                s.wait_ge(ctx.vdr.h, mask_rdy)
                for kc in range(nkc):
                    k0 = kc * 512
                    kw = min(512, klen - k0)
                    s.activation(pbuf[:, k0:k0 + kw], sc_ps[:, k0:k0 + kw], AF.Exp,
                                 bias=ytmp[:, 1:2], scale=SC,
                                 accum_out=ytmp[:, 2 + kc:3 + kc])
                ctx.sdr.v += 1
                s.drain().then_inc(ctx.sdr.h, 1)
            exp_rdy = ctx.sdr.v

            @block.vector
            def _(v, klen=klen, nkc=nkc, exp_rdy=exp_rdy):
                vb = ctx.vch
                def hs(inst):
                    vb.v += 1
                    inst.then_inc(vb.h, 1)
                    v.wait_ge(vb.h, vb.v)
                v.wait_ge(ctx.sdr.h, exp_rdy)
                hs(v.reduce_sum(ytmp[:, 8:9], ytmp[:, 2:2 + nkc], axis=AX.X))
                hs(v.reciprocal(ytmp[:, 9:10], ytmp[:, 8:9]))
                for kc in range(nkc):
                    k0 = kc * 512
                    kw = min(512, klen - k0)
                    hs(v.tensor_scalar(pbuf[:, k0:k0 + kw], pbuf[:, k0:k0 + kw],
                                       ytmp[:, 9:10], None, ALU.mult))
                ctx.vdr.v += 1
                v.drain().then_inc(ctx.vdr.h, 1)
            mods_rdy = ctx.vdr.v

            ngrp = (nkt + 3) // 4
            for grp in range(ngrp):
                n = min(4, nkt - grp * 4)

                @block.tensor
                def _(t, grp=grp, n=n, mods_rdy=mods_rdy, evprev=ctx.vdr.v):
                    t.wait_ge(ctx.vdr.h, mods_rdy)
                    if grp > 0:
                        t.wait_ge(ctx.vdr.h, evprev)
                    for j in range(n):
                        t.transpose(tr_ps[:, j * 128:(j + 1) * 128],
                                    pbuf[:, (grp * 4 + j) * 128:(grp * 4 + j + 1) * 128],
                                    identf[:])
                    ctx.tdr.v += 1
                    t.drain().then_inc(ctx.tdr.h, 1)
                trd = ctx.tdr.v

                @block.vector
                def _(v, grp=grp, n=n, trd=trd):
                    v.wait_ge(ctx.tdr.h, trd)
                    v.tensor_copy(attT[:, grp * 4 * 128: grp * 4 * 128 + n * 128],
                                  tr_ps[:, 0:n * 128])
                    ctx.vdr.v += 1
                    v.drain().then_inc(ctx.vdr.h, 1)
            attT_rdy = ctx.vdr.v

            @block.tensor
            def _(t, h=h, nkt=nkt, attT_rdy=attT_rdy):
                t.wait_ge(ctx.vdr.h, attT_rdy)
                for kt in range(nkt):
                    t.matmul(y_ps, vtm[:, kt * 1024 + h * 64: kt * 1024 + h * 64 + 64],
                             attT[:, kt * 128:(kt + 1) * 128],
                             start=(kt == 0), stop=(kt == nkt - 1))
                ctx.tdr.v += 1
                t.drain().then_inc(ctx.tdr.h, 1)
            pv_rdy = ctx.tdr.v

            @block.scalar
            def _(s, h=h, qt=qt, q0=q0, pv_rdy=pv_rdy):
                s.wait_ge(ctx.tdr.h, pv_rdy)
                b2 = (h % 2) * 64
                fc = h // 2
                s.activation(y_fm[b2:b2 + 64, fc * T + q0: fc * T + q0 + 128],
                             y_ps, AF.Copy)
                ctx.sdr.v += 1
                s.drain().then_inc(ctx.sdr.h, 1)
            prev_yevac = ctx.sdr.v

    fence(ctx, block)
    _emit_out_residual(ctx, block, projw_ap, y_fm, wbig, banks,
                       stream_hbm, schunk_res, C=1024, bias_cols=projb_cols,
                       scale_cols=None, t0=0, Tc=T)
    fence(ctx, block)


def build_layer2_program(parts=("ffn1", "attn", "ffn2")):
    nc = bass.Bass("TRN2", target_bir_lowering=False)
    stream_in = nc.dram_tensor("stream_in", [D, T], F32, kind="ExternalInput").ap()
    stream_out = nc.dram_tensor("stream_out", [D, T], F32, kind="ExternalOutput").ap()
    lin0 = nc.dram_tensor("lin0", [1024, 1024], F32, kind="ExternalInput").ap()
    swi1 = nc.dram_tensor("swi1", [1024, 2048], F32, kind="ExternalInput").ap()
    gatewT = nc.dram_tensor("gatewT", [1024, 8], F32, kind="ExternalInput").ap()
    exp13 = nc.dram_tensor("exp13", [8, 1024, 2048], F32, kind="ExternalInput")
    exp2 = nc.dram_tensor("exp2", [8, 1024, 1024], F32, kind="ExternalInput")
    sh13 = nc.dram_tensor("sh13", [1024, 4096], F32, kind="ExternalInput").ap()
    sh2 = nc.dram_tensor("sh2", [2048, 1024], F32, kind="ExternalInput").ap()
    lin1 = nc.dram_tensor("lin1", [1024, 1024], F32, kind="ExternalInput").ap()
    swi2 = nc.dram_tensor("swi2", [1024, 2048], F32, kind="ExternalInput").ap()
    bcols_in = nc.dram_tensor("bcols", [128, 48], F32, kind="ExternalInput").ap()
    kqvw = nc.dram_tensor("kqvw", [1024, 3072], F32, kind="ExternalInput").ap()
    kqvb_in = nc.dram_tensor("kqvb", [128, 16], F32, kind="ExternalInput").ap()
    vb_in = nc.dram_tensor("vbias", [1, 1024], F32, kind="ExternalInput").ap()
    projw = nc.dram_tensor("projw", [1024, 1024], F32, kind="ExternalInput").ap()
    projb_in = nc.dram_tensor("projb", [128, 8], F32, kind="ExternalInput").ap()
    slopes_in = nc.dram_tensor("slopes", [1, 16], F32, kind="ExternalInput").ap()
    mtab_in = nc.dram_tensor("mtab", [128, 2048], F32, kind="ExternalInput").ap()
    tri_in = nc.dram_tensor("tri", [128, 128], F32, kind="ExternalInput").ap()
    identf_in = nc.dram_tensor("identf", [128, 128], F32, kind="ExternalInput").ap()
    ones_in = nc.dram_tensor("ones", [128, 128], BF16, kind="ExternalInput").ap()
    ccols_in = nc.dram_tensor("ccols", [128, len(CONST_VALS)], F32, kind="ExternalInput").ap()

    ctx = Ctx(nc)
    bfa = ctx.sb("bfa", [128, 67584], BF16)
    fpa = ctx.sb("fpa", [128, 12832], F32)
    mtab = ctx.sb("mtab_sb", [128, 2048], F32)
    wbig = [ctx.sb(f"wbg{i}", [128, 4096], BF16) for i in range(2)]
    ones_bf = ctx.sb("ones_bf", [128, 128], BF16)
    ccols = ctx.sb("ccols_sb", [128, len(CONST_VALS)], F32)
    identf = ctx.sb("identf_sb", [128, 128], F32)
    trimask = ctx.sb("tri_sb", [128, 128], F32)
    bcols = ctx.sb("bcols_sb", [128, 48], F32)
    kqvb = ctx.sb("kqvb_sb", [128, 16], F32)
    projb = ctx.sb("projb_sb", [128, 8], F32)
    slopes_row = ctx.sb("slopes_sb", [1, 16], BF16)
    ps_all = ctx.psum("ps_all", [128, 4096], F32)
    pbanks = [ps_all[:, i * 512:(i + 1) * 512] for i in range(8)]

    xhat = bfa[:, 0:16384]
    moe_fm = bfa[:, 0:16384]
    h0 = bfa[:, 16384:32768]
    h1 = bfa[:, 32768:49152]
    actbuf = bfa[:, 49152:57344]
    shh = bfa[:, 49152:65536]
    cT_bf = bfa[:, 65536:67584]
    vbias_row = bfa[:, 65536:66560]
    q_fm = bfa[:, 16384:32768]
    k_fm = bfa[:, 32768:49152]
    vtm = bfa[:, 49152:65536]
    y_fm = bfa[:, 0:16384]
    attT = bfa[:, 65536:67584]
    sq_scr = bfa[:, 16384:18432]
    bcast = bfa[:, 18432:22528]
    ab_rows = bfa[:, 22528:26624]

    schunk = fpa[:, 0:2048]
    schunk_res = fpa[:, 0:2048]
    rowbuf = fpa[:, 2048:3584]
    utmp = fpa[:, 3584:4608]
    ytmp = fpa[:, 4608:4640]
    acc = fpa[:, 4640:12832]
    pbuf = acc[:, 0:2048]
    c_all = fpa[:, 2048:2176]
    stats = fpa[:, 2176:2256]
    slope_bc = fpa[:, 2256:2272]

    with nc.Block() as block:
        emit_boot(ctx, block)

        @block.gpsimd
        def _(g):
            g.dma_start(out=ones_bf[:], in_=ones_in).then_inc(ctx.sio.h, 16)
            g.dma_start(out=ccols[:], in_=ccols_in).then_inc(ctx.sio.h, 16)
            g.dma_start(out=identf[:], in_=identf_in).then_inc(ctx.sio.h, 16)
            g.dma_start(out=trimask[:], in_=tri_in).then_inc(ctx.sio.h, 16)
            g.dma_start(out=bcols[:], in_=bcols_in).then_inc(ctx.sio.h, 16)
            g.dma_start(out=kqvb[:], in_=kqvb_in).then_inc(ctx.sio.h, 16)
            g.dma_start(out=projb[:], in_=projb_in).then_inc(ctx.sio.h, 16)
            g.dma_start(out=slopes_row[:], in_=slopes_in).then_inc(ctx.sio.h, 16)
            g.dma_start(out=mtab[:], in_=mtab_in).then_inc(ctx.sio.h, 16)
        ctx.sio.v += 144
        pro_io = ctx.sio.v
        for en in ("tensor", "vector", "scalar", "sync"):
            getattr(block, en)(lambda e: e.wait_ge(ctx.sio.h, pro_io))

        @block.sync
        def _(sy):
            sy.dma_start(out=stream_out, in_=stream_in).then_inc(ctx.sio.h, 16)
        ctx.sio.v += 16
        fence(ctx, block)

        def do_ffn():
            emit_stream_ln(ctx, block, stream_out, schunk, xhat, sq_scr, rowbuf,
                           ab_rows, bcast, pbanks, ones_bf, ccols)
            fence(ctx, block)
            emit_l2_ffn(ctx, block, xhat, h0, h1, actbuf, shh, moe_fm, acc,
                        lin0, bcols[:, 0:8], swi1, bcols[:, 8:16], bcols[:, 16:24],
                        gatewT, [exp13.ap()[e] for e in range(8)],
                        [exp2.ap()[e] for e in range(8)], sh13, sh2, lin1,
                        bcols[:, 24:32], swi2, bcols[:, 32:40], bcols[:, 40:48],
                        c_all, cT_bf, stats, wbig, pbanks, ps_all, identf,
                        ones_bf, ccols, utmp, stream_out, schunk_res)
            fence(ctx, block)

        if "ffn1" in parts:
            do_ffn()
        if "attn" in parts:
            emit_stream_ln(ctx, block, stream_out, schunk, xhat, sq_scr, rowbuf,
                           ab_rows, bcast, pbanks, ones_bf, ccols)
            fence(ctx, block)

            @block.gpsimd
            def _(g):
                g.dma_start(out=vbias_row[0:1, :], in_=vb_in).then_inc(ctx.sio.h, 16)
            ctx.sio.v += 16
            fence(ctx, block)
            emit_l2_attn(ctx, block, xhat, kqvw, kqvb, vbias_row, slopes_row,
                         mtab, trimask, identf, q_fm, k_fm, vtm, slope_bc, pbuf,
                         attT, ytmp, y_fm, ps_all, wbig, ones_bf, stream_out,
                         schunk_res, projw, projb)
            fence(ctx, block)
        if "ffn2" in parts:
            do_ffn()

        stv = ctx.st.v
        @block.sync
        def _(sy):
            sy.wait_ge(ctx.st.h, stv)

    ctx.es.close()
    nc.gpsimd.sem_clear(ctx.clear_rng)
    return nc


def _interleave128(wa, wb):
    C, F = wa.shape
    out = np.empty((C, 2 * F), np.float32)
    for i in range(F // 128):
        out[:, (2 * i) * 128:(2 * i) * 128 + 128] = wa[:, i * 128:(i + 1) * 128]
        out[:, (2 * i + 1) * 128:(2 * i + 1) * 128 + 128] = wb[:, i * 128:(i + 1) * 128]
    return np.ascontiguousarray(out)


def prep_layer2_inputs(ip):
    out = {}
    out["lin0"] = np.ascontiguousarray(ip["L2_ffn_lin0_w"])
    out["swi1"] = _interleave128(ip["L2_ffn_swi1_w1"], ip["L2_ffn_swi1_w2"])
    out["gatewT"] = np.ascontiguousarray(ip["L2_ffn_gate_w"].T)
    out["exp13"] = np.stack([_interleave128(ip["L2_ffn_exp_w1"][e], ip["L2_ffn_exp_w3"][e])
                             for e in range(8)], axis=0)
    out["exp2"] = np.ascontiguousarray(ip["L2_ffn_exp_w2"])
    out["sh13"] = _interleave128(ip["L2_ffn_sh_w1"], ip["L2_ffn_sh_w3"])
    out["sh2"] = np.ascontiguousarray(ip["L2_ffn_sh_w2"])
    out["lin1"] = np.ascontiguousarray(ip["L2_ffn_lin1_w"])
    out["swi2"] = _interleave128(ip["L2_ffn_swi2_w1"], ip["L2_ffn_swi2_w2"])
    bcols = np.zeros((128, 48), np.float32)
    bcols[:, 0:8] = ip["L2_ffn_lin0_b"].reshape(8, 128).T
    bcols[:, 8:16] = ip["L2_ffn_swi1_b1"].reshape(8, 128).T
    bcols[:, 16:24] = ip["L2_ffn_swi1_b2"].reshape(8, 128).T
    bcols[:, 24:32] = ip["L2_ffn_lin1_b"].reshape(8, 128).T
    bcols[:, 32:40] = ip["L2_ffn_swi2_b1"].reshape(8, 128).T
    bcols[:, 40:48] = ip["L2_ffn_swi2_b2"].reshape(8, 128).T
    out["bcols"] = bcols
    out["kqvw"] = np.ascontiguousarray(ip["L2_attn_W_kqv"])
    kqvb = np.zeros((128, 16), np.float32)
    kqvb[:, 0:8] = ip["L2_attn_b_kqv"][0:1024].reshape(8, 128).T
    kqvb[:, 8:16] = ip["L2_attn_b_kqv"][1024:2048].reshape(8, 128).T
    out["kqvb"] = kqvb
    out["vbias"] = np.ascontiguousarray(ip["L2_attn_b_kqv"][2048:3072].reshape(1, 1024))
    out["projw"] = np.ascontiguousarray(ip["L2_attn_W_proj"])
    out["projb"] = np.ascontiguousarray(ip["L2_attn_b_proj"].reshape(8, 128).T)
    out["slopes"] = np.ascontiguousarray(ip["L2_attn_slopes"].reshape(1, 16))
    p = np.arange(128)
    jj = np.arange(2048)
    out["mtab"] = np.ascontiguousarray((jj[None, :] + 127.0 - 2047.0 - p[:, None]).astype(np.float32))
    return out



MAGIC_HALF = 6291456.0


def emit_quant(ctx, block, v, hsv, src, qa, bstat, W):
    nb = W // 16
    s3 = lambda ap: ap.rearrange("p (b k) -> p b k", k=16)
    bmax = bstat[:, 0:nb]
    brec = bstat[:, nb:2 * nb]
    hsv(v.tensor_scalar(qa[:, 0:W], src, -1.0, None, ALU.mult))
    hsv(v.tensor_max(qa[:, 0:W], qa[:, 0:W], src))
    hsv(v.reduce_max(bmax.rearrange("p (b o) -> p b o", o=1), s3(qa[:, 0:W]), axis=AX.X))
    hsv(v.tensor_scalar(bmax, bmax, 1.0 / 6.0, 1e-12, ALU.mult, ALU.max))
    hsv(v.reciprocal(brec, bmax))
    bm3 = bass.AP(brec.tensor, brec.offset, [brec.ap[0], [1, nb], [0, 16]])
    hsv(v.tensor_tensor(s3(qa[:, 0:W]), s3(qa[:, 0:W]), bm3, ALU.mult))
    qa2 = qa[:, W:2 * W]
    hsv(v.tensor_scalar(qa2, qa[:, 0:W], 2.0, MAGIC_HALF, ALU.min, ALU.add))
    hsv(v.tensor_scalar_add(qa2, qa2, -MAGIC_HALF))
    hsv(v.tensor_scalar(qa[:, 2 * W:3 * W], qa[:, 0:W], 2.5, None, ALU.is_gt))
    hsv(v.tensor_add(qa2, qa2, qa[:, 2 * W:3 * W]))
    hsv(v.tensor_scalar(qa[:, 2 * W:3 * W], qa[:, 0:W], 3.5, None, ALU.is_gt))
    hsv(v.tensor_add(qa2, qa2, qa[:, 2 * W:3 * W]))
    hsv(v.tensor_scalar(qa[:, 2 * W:3 * W], qa[:, 0:W], 5.0, 2.0, ALU.is_gt, ALU.mult))
    hsv(v.tensor_add(qa2, qa2, qa[:, 2 * W:3 * W]))
    hsv(v.tensor_scalar(qa[:, 2 * W:3 * W], src, 0.0, None, ALU.is_gt))
    hsv(v.tensor_scalar(qa[:, 2 * W:3 * W], qa[:, 2 * W:3 * W], 2.0, -1.0, ALU.mult, ALU.add))
    hsv(v.tensor_mul(qa2, qa2, qa[:, 2 * W:3 * W]))
    bs3 = bass.AP(bmax.tensor, bmax.offset, [bmax.ap[0], [1, nb], [0, 16]])
    hsv(v.tensor_tensor(s3(src), s3(qa2), bs3, ALU.mult))


def emit_l3_xquant(ctx, block, xfm, nchunks, fptile, qa, bstat, identf, tr_ps, Wc=1024, stride=T, ntt=16, tmbuf=None):
    for tt in range(ntt):
        @block.vector
        def _(v, tt=tt, prevs=ctx.sdr.v):
            vb = ctx.vch
            def hsv(inst):
                vb.v += 1
                inst.then_inc(vb.h, 1)
                v.wait_ge(vb.h, vb.v)
            if tt > 0:
                v.wait_ge(ctx.sdr.h, prevs)
            for k in range(nchunks):
                hsv(v.tensor_copy(fptile[:, k * 128:(k + 1) * 128],
                                  xfm[:, k * stride + tt * 128: k * stride + tt * 128 + 128]))
            ctx.vdr.v += 1
            v.drain().then_inc(ctx.vdr.h, 1)
        up_rdy = ctx.vdr.v

        ntr = nchunks
        for grp in range(0, ntr, 4):
            n = min(4, ntr - grp)
            @block.tensor
            def _(t, grp=grp, n=n, up_rdy=up_rdy, evp=ctx.vdr.v):
                t.wait_ge(ctx.vdr.h, up_rdy)
                if grp > 0:
                    t.wait_ge(ctx.vdr.h, evp)
                for j in range(n):
                    t.transpose(tr_ps[:, j * 128:(j + 1) * 128],
                                fptile[:, (grp + j) * 128:(grp + j + 1) * 128],
                                identf[:])
                ctx.tdr.v += 1
                t.drain().then_inc(ctx.tdr.h, 1)
            trd = ctx.tdr.v

            @block.vector
            def _(v, grp=grp, n=n, trd=trd):
                v.wait_ge(ctx.tdr.h, trd)
                v.tensor_copy(tmbuf[:, grp * 128:(grp + n) * 128], tr_ps[:, 0:n * 128])
                ctx.vdr.v += 1
                v.drain().then_inc(ctx.vdr.h, 1)

        @block.vector
        def _(v, tt=tt):
            vb = ctx.vch
            def hsv(inst):
                vb.v += 1
                inst.then_inc(vb.h, 1)
                v.wait_ge(vb.h, vb.v)
            emit_quant(ctx, block, v, hsv, tmbuf[:, 0:Wc], qa, bstat, Wc)
            ctx.vdr.v += 1
            v.drain().then_inc(ctx.vdr.h, 1)
        q_rdy = ctx.vdr.v

        for grp in range(0, ntr, 4):
            n = min(4, ntr - grp)
            @block.tensor
            def _(t, grp=grp, n=n, q_rdy=q_rdy, evp=ctx.sdr.v):
                t.wait_ge(ctx.vdr.h, q_rdy)
                if grp > 0:
                    t.wait_ge(ctx.sdr.h, evp)
                for j in range(n):
                    t.transpose(tr_ps[:, j * 128:(j + 1) * 128],
                                tmbuf[:, (grp + j) * 128:(grp + j + 1) * 128],
                                identf[:])
                ctx.tdr.v += 1
                t.drain().then_inc(ctx.tdr.h, 1)
            trd = ctx.tdr.v

            @block.scalar
            def _(s, tt=tt, grp=grp, n=n, trd=trd):
                s.wait_ge(ctx.tdr.h, trd)
                for j in range(n):
                    k = grp + j
                    s.activation(xfm[:, k * stride + tt * 128: k * stride + tt * 128 + 128],
                                 tr_ps[:, j * 128:(j + 1) * 128], AF.Copy)
                ctx.sdr.v += 1
                s.drain().then_inc(ctx.sdr.h, 1)


def emit_fp4_wquant_tile(ctx, block, w_nat_ap, f0, Cdim, wld, qa, bstat, identf,
                         tr_ps, slot, slot_off):
    sio0 = ctx.sio.v

    @block.sync
    def _(sy):
        sy.dma_start(out=wld[:, 0:Cdim], in_=w_nat_ap[f0:f0 + 128, :]).then_inc(ctx.sio.h, 16)
    ctx.sio.v += 16

    @block.vector
    def _(v):
        vb = ctx.vch
        def hsv(inst):
            vb.v += 1
            inst.then_inc(vb.h, 1)
            v.wait_ge(vb.h, vb.v)
        v.wait_ge(ctx.sio.h, sio0 + 16)
        emit_quant(ctx, block, v, hsv, wld[:, 0:Cdim], qa, bstat, Cdim)
        ctx.vdr.v += 1
        v.drain().then_inc(ctx.vdr.h, 1)
    q_rdy = ctx.vdr.v

    ntr = Cdim // 128
    for grp in range(0, ntr, 4):
        n = min(4, ntr - grp)
        @block.tensor
        def _(t, grp=grp, n=n, q_rdy=q_rdy, evp=ctx.sdr.v):
            t.wait_ge(ctx.vdr.h, q_rdy)
            if grp > 0:
                t.wait_ge(ctx.sdr.h, evp)
            for j in range(n):
                t.transpose(tr_ps[:, j * 128:(j + 1) * 128],
                            wld[:, (grp + j) * 128:(grp + j + 1) * 128], identf[:])
            ctx.tdr.v += 1
            t.drain().then_inc(ctx.tdr.h, 1)
        trd = ctx.tdr.v

        @block.scalar
        def _(s, grp=grp, n=n, trd=trd):
            s.wait_ge(ctx.tdr.h, trd)
            for j in range(n):
                s.activation(slot[:, slot_off + (grp + j) * 128: slot_off + (grp + j + 1) * 128],
                             tr_ps[:, j * 128:(j + 1) * 128], AF.Copy)
            ctx.sdr.v += 1
            s.drain().then_inc(ctx.sdr.h, 1)


def emit_l3_ffn(ctx, block, xq, gate_nat, up_nat, down_nat, hbuf, out_fm,
                wld, qa, bstat, utmp, slots_gu, slots_d, pbanks, ps_all,
                identf, stream_hbm, schunk_res, fptile):
    tr_ps = ps_all[:, 2048:2560]
    Tc = 1024
    for tch in range(2):
        t0 = tch * Tc
        for fb in range(22):
            slot = slots_gu[0]
            fence(ctx, block, engines=("tensor", "vector", "scalar", "sync"))
            emit_fp4_wquant_tile(ctx, block, gate_nat, fb * 128, 1024, wld, qa,
                                 bstat, identf, tr_ps, slot, 0)
            fence(ctx, block, engines=("tensor", "vector", "scalar", "sync"))
            emit_fp4_wquant_tile(ctx, block, up_nat, fb * 128, 1024, wld, qa,
                                 bstat, identf, tr_ps, slot, 1024)
            fence(ctx, block, engines=("tensor", "vector", "scalar", "sync"))

            @block.tensor
            def _(t, fb=fb, slot=slot, t0=t0):
                for half in range(2):
                    for ti in range(2):
                        psb = pbanks[half * 2 + ti]
                        for cc in range(8):
                            t.matmul(psb[:, 0:512],
                                     slot[:, half * 1024 + cc * 128: half * 1024 + cc * 128 + 128],
                                     xq[:, cc * T + t0 + ti * 512: cc * T + t0 + ti * 512 + 512],
                                     start=(cc == 0), stop=(cc == 7))
                ctx.tdr.v += 1
                t.drain().then_inc(ctx.tdr.h, 1)
            mm_rdy = ctx.tdr.v

            @block.scalar
            def _(s, mm_rdy=mm_rdy):
                s.wait_ge(ctx.tdr.h, mm_rdy)
                for ti in range(2):
                    s.activation(utmp[:, ti * 512:(ti + 1) * 512], pbanks[ti][:, 0:512],
                                 AF.Silu)
                ctx.sdr.v += 1
                s.drain().then_inc(ctx.sdr.h, 1)
            si_rdy = ctx.sdr.v

            @block.vector
            def _(v, fb=fb, si_rdy=si_rdy):
                v.wait_ge(ctx.sdr.h, si_rdy)
                for ti in range(2):
                    v.scalar_tensor_tensor(hbuf[:, fb * Tc + ti * 512: fb * Tc + ti * 512 + 512],
                                           pbanks[2 + ti][:, 0:512], 1.0,
                                           utmp[:, ti * 512:(ti + 1) * 512],
                                           ALU.mult, ALU.mult)
                ctx.vdr.v += 1
                v.drain().then_inc(ctx.vdr.h, 1)
        fence(ctx, block)

        emit_l3_xquant(ctx, block, hbuf, 22, fptile, qa, bstat, identf, tr_ps,
                       Wc=2816, stride=Tc, ntt=8, tmbuf=wld)
        fence(ctx, block)

        for fb in range(8):
            slot = slots_d[0]
            fence(ctx, block, engines=("tensor", "vector", "scalar", "sync"))
            emit_fp4_wquant_tile(ctx, block, down_nat, fb * 128, 2816, wld, qa,
                                 bstat, identf, tr_ps, slot, 0)
            fence(ctx, block, engines=("tensor", "vector", "scalar", "sync"))

            @block.tensor
            def _(t, fb=fb, slot=slot, t0=t0):
                for ti in range(2):
                    psb = pbanks[ti]
                    for cc in range(22):
                        t.matmul(psb[:, 0:512],
                                 slot[:, cc * 128: cc * 128 + 128],
                                 hbuf[:, cc * Tc + ti * 512: cc * Tc + ti * 512 + 512],
                                 start=(cc == 0), stop=(cc == 21))
                ctx.tdr.v += 1
                t.drain().then_inc(ctx.tdr.h, 1)
            mm_rdy = ctx.tdr.v

            @block.scalar
            def _(s, fb=fb, t0=t0, mm_rdy=mm_rdy):
                s.wait_ge(ctx.tdr.h, mm_rdy)
                for ti in range(2):
                    s.activation(out_fm[:, fb * T + t0 + ti * 512: fb * T + t0 + ti * 512 + 512],
                                 pbanks[ti][:, 0:512], AF.Copy)
                ctx.sdr.v += 1
                s.drain().then_inc(ctx.sdr.h, 1)
        fence(ctx, block)
    emit_plain_residual(ctx, block, out_fm, stream_hbm, schunk_res, factor=0.5)
    fence(ctx, block)


def emit_l3_attn(ctx, block, xhat, wdqT_ap, wdkvT_ap, wuq_c, wuk_c, wuv_c,
                 wdq_sb, wuqT_sb, woT_ap, woT_sb, keff_sb, veff_sb, ckv_fm,
                 ckv_tm, q2_fm, q_fm, qlat, ctxs, pbuf, attT, ytmp, fptile,
                 y_fm, ps_all, wbig, identf, trimask, stream_hbm, schunk_res):
    banks = [ps_all[:, i * 512:(i + 1) * 512] for i in range(8)]
    tr_ps = banks[4]
    SC = 0.125

    @block.tensor
    def _(t):
        for cc in range(8):
            t.matmul(banks[0][:, 0:128], wuq_c[:, cc * 128:(cc + 1) * 128],
                     wuk_c[:, cc * 128:(cc + 1) * 128],
                     start=(cc == 0), stop=(cc == 7))
        ctx.tdr.v += 1
        t.drain().then_inc(ctx.tdr.h, 1)
    a_rdy = ctx.tdr.v

    @block.vector
    def _(v, a_rdy=a_rdy):
        v.wait_ge(ctx.tdr.h, a_rdy)
        v.tensor_copy(ctxs[:, 0:128], banks[0][:, 0:128])
        ctx.vdr.v += 1
        v.drain().then_inc(ctx.vdr.h, 1)
    a_ev = ctx.vdr.v

    kb = ctx.tdr.v
    vb0 = ctx.vdr.v

    @block.tensor
    def _(t, a_ev=a_ev, vb0=vb0):
        t.wait_ge(ctx.vdr.h, a_ev)
        for cc in range(8):
            if cc >= 2:
                t.wait_ge(ctx.vdr.h, vb0 + (cc - 2) + 1)
            t.matmul(banks[1 + cc % 2][:, 0:128],
                     wdq_sb[:, cc * 128:(cc + 1) * 128], ctxs[:, 0:128],
                     start=True, stop=True)
            ctx.tdr.v += 1
            t.drain().then_inc(ctx.tdr.h, 1)

    @block.vector
    def _(v, kb=kb):
        for cc in range(8):
            v.wait_ge(ctx.tdr.h, kb + cc + 1)
            v.tensor_copy(keff_sb[:, cc * 128:(cc + 1) * 128], banks[1 + cc % 2][:, 0:128])
            ctx.vdr.v += 1
            v.drain().then_inc(ctx.vdr.h, 1)
    fence(ctx, block)

    def evac_veff(s, fb, ti, ps):
        s.activation(veff_sb[:, fb * 512 + ti * 0: fb * 512 + 512] if False else
                     veff_sb[:, fb * 128: fb * 128 + 128], ps, AF.Copy)
    wb0 = ctx.wdma.v

    @block.gpsimd
    def _(g):
        g.dma_start(out=woT_sb[:, 0:8192].rearrange("c (k m) -> c k m", m=1024),
                    in_=woT_ap.rearrange("(k c) m -> c k m", c=128),
                    ).then_inc(ctx.wdma.h, 16)
    ctx.wdma.v = wb0 + 16

    vmm0 = ctx.tdr.v
    vev0 = ctx.vdr.v

    @block.tensor
    def _(t, wb0=wb0, vev0=vev0):
        t.wait_ge(ctx.wdma.h, wb0 + 16)
        for gi in range(2):
            if gi > 0:
                t.wait_ge(ctx.vdr.h, vev0 + gi)
            for cc in range(8):
                t.matmul(banks[2][:, 0:512], wuv_c[:, cc * 128:(cc + 1) * 128],
                         woT_sb[:, cc * 1024 + gi * 512: cc * 1024 + gi * 512 + 512],
                         start=(cc == 0), stop=(cc == 7))
            ctx.tdr.v += 1
            t.drain().then_inc(ctx.tdr.h, 1)

    @block.vector
    def _(v, vmm0=vmm0):
        for gi in range(2):
            v.wait_ge(ctx.tdr.h, vmm0 + gi + 1)
            v.tensor_copy(veff_sb[:, gi * 512:(gi + 1) * 512], banks[2][:, 0:512])
            ctx.vdr.v += 1
            v.drain().then_inc(ctx.vdr.h, 1)
    fence(ctx, block)

    def evac_ckv(s, fb, ti, ps):
        s.activation(ckv_fm[:, ti * 512:(ti + 1) * 512], ps, AF.Copy)
    emit_linear_fm2(ctx, block, 1024, 128, wdkvT_ap,
                    lambda cc: xhat[:, cc * T:(cc + 1) * T], evac_ckv,
                    wbig, banks, Tloc=T, fb_per_dma=1)
    fence(ctx, block)

    def evac_q2(s, fb, ti, ps):
        s.activation(q2_fm[:, ti * 512:(ti + 1) * 512], ps, AF.Copy)
    emit_linear_fm2(ctx, block, 1024, 128, wdqT_ap,
                    lambda cc: xhat[:, cc * T:(cc + 1) * T], evac_q2,
                    wbig, banks, Tloc=T, fb_per_dma=1)
    fence(ctx, block)

    for grp in range(4):
        @block.vector
        def _(v, grp=grp, prevs=ctx.vdr.v):
            v.tensor_copy(fptile[:, 0:512], ckv_fm[:, grp * 512:(grp + 1) * 512])
            ctx.vdr.v += 1
            v.drain().then_inc(ctx.vdr.h, 1)
        up_rdy = ctx.vdr.v

        @block.tensor
        def _(t, grp=grp, up_rdy=up_rdy):
            t.wait_ge(ctx.vdr.h, up_rdy)
            for j in range(4):
                t.transpose(tr_ps[:, j * 128:(j + 1) * 128],
                            fptile[:, j * 128:(j + 1) * 128], identf[:])
            ctx.tdr.v += 1
            t.drain().then_inc(ctx.tdr.h, 1)
        trd = ctx.tdr.v

        @block.vector
        def _(v, grp=grp, trd=trd):
            v.wait_ge(ctx.tdr.h, trd)
            v.tensor_copy(ckv_tm[:, grp * 512:(grp + 1) * 512], tr_ps[:, 0:512])
            ctx.vdr.v += 1
            v.drain().then_inc(ctx.vdr.h, 1)
    fence(ctx, block)

    qmm = ctx.tdr.v
    qse = ctx.sdr.v

    @block.tensor
    def _(t, qse=qse):
        n = 0
        for fc in range(8):
            for ti in range(4):
                psb = banks[n % 2]
                if n >= 2:
                    t.wait_ge(ctx.sdr.h, qse + (n - 2) + 1)
                t.matmul(psb[:, 0:512], wuqT_sb[:, fc * 128:(fc + 1) * 128],
                         q2_fm[:, ti * 512:(ti + 1) * 512], start=True, stop=True)
                ctx.tdr.v += 1
                t.drain().then_inc(ctx.tdr.h, 1)
                n += 1

    @block.scalar
    def _(s, qmm=qmm):
        n = 0
        for fc in range(8):
            for ti in range(4):
                s.wait_ge(ctx.tdr.h, qmm + n + 1)
                s.activation(q_fm[:, fc * T + ti * 512: fc * T + ti * 512 + 512],
                             banks[n % 2][:, 0:512], AF.Copy)
                n += 1
                ctx.sdr.v += 1
                s.drain().then_inc(ctx.sdr.h, 1)
    fence(ctx, block)

    sc_ps = ps_all[:, 0:2048]
    prev_yevac = ctx.sdr.v
    for h in range(16):
        b = (h % 2) * 64
        hc = h // 2
        qlat_mm = ctx.tdr.v
        qlse = ctx.sdr.v

        @block.tensor
        def _(t, b=b, hc=hc, prev=prev_yevac, qlse=qlse):
            t.wait_ge(ctx.sdr.h, prev)
            for ti in range(4):
                if ti >= 2:
                    t.wait_ge(ctx.sdr.h, qlse + (ti - 2) + 1)
                t.matmul(banks[4 + ti % 2][:, 0:512],
                         keff_sb[b:b + 64, hc * 128:(hc + 1) * 128],
                         q_fm[b:b + 64, hc * T + ti * 512: hc * T + ti * 512 + 512],
                         start=True, stop=True)
                ctx.tdr.v += 1
                t.drain().then_inc(ctx.tdr.h, 1)

        @block.scalar
        def _(s, qlat_mm=qlat_mm):
            for ti in range(4):
                s.wait_ge(ctx.tdr.h, qlat_mm + ti + 1)
                s.activation(qlat[:, ti * 512:(ti + 1) * 512],
                             banks[4 + ti % 2][:, 0:512], AF.Copy)
                ctx.sdr.v += 1
                s.drain().then_inc(ctx.sdr.h, 1)
        qlat_rdy = ctx.sdr.v

        for qt in range(16):
            q0 = qt * 128
            klen = q0 + 128
            nkc = (klen + 511) // 512
            nkt = klen // 128

            @block.tensor
            def _(t, q0=q0, klen=klen, nkc=nkc, prev=prev_yevac, qlat_rdy=qlat_rdy):
                t.wait_ge(ctx.sdr.h, max(prev, qlat_rdy))
                for kc in range(nkc):
                    k0 = kc * 512
                    kw = min(512, klen - k0)
                    t.matmul(sc_ps[:, k0:k0 + kw], qlat[:, q0:q0 + 128],
                             ckv_fm[:, k0:k0 + kw], start=True, stop=True)
                ctx.tdr.v += 1
                t.drain().then_inc(ctx.tdr.h, 1)
            sc_rdy = ctx.tdr.v

            @block.vector
            def _(v, q0=q0, klen=klen, sc_rdy=sc_rdy):
                vb = ctx.vch
                def hs(inst):
                    vb.v += 1
                    inst.then_inc(vb.h, 1)
                    v.wait_ge(vb.h, vb.v)
                v.wait_ge(ctx.tdr.h, sc_rdy)
                hs(v.tensor_add(sc_ps[:, q0:q0 + 128], sc_ps[:, q0:q0 + 128], trimask[:]))
                hs(v.reduce_max(ytmp[:, 0:1], sc_ps[:, 0:klen], axis=AX.X))
                hs(v.tensor_scalar_mul(ytmp[:, 1:2], ytmp[:, 0:1], -SC))
                ctx.vdr.v += 1
                v.drain().then_inc(ctx.vdr.h, 1)
            mask_rdy = ctx.vdr.v

            @block.scalar
            def _(s, klen=klen, nkc=nkc, mask_rdy=mask_rdy):
                s.wait_ge(ctx.vdr.h, mask_rdy)
                for kc in range(nkc):
                    k0 = kc * 512
                    kw = min(512, klen - k0)
                    s.activation(pbuf[:, k0:k0 + kw], sc_ps[:, k0:k0 + kw], AF.Exp,
                                 bias=ytmp[:, 1:2], scale=SC,
                                 accum_out=ytmp[:, 2 + kc:3 + kc])
                ctx.sdr.v += 1
                s.drain().then_inc(ctx.sdr.h, 1)
            exp_rdy = ctx.sdr.v

            @block.vector
            def _(v, klen=klen, nkc=nkc, exp_rdy=exp_rdy):
                vb = ctx.vch
                def hs(inst):
                    vb.v += 1
                    inst.then_inc(vb.h, 1)
                    v.wait_ge(vb.h, vb.v)
                v.wait_ge(ctx.sdr.h, exp_rdy)
                hs(v.reduce_sum(ytmp[:, 8:9], ytmp[:, 2:2 + nkc], axis=AX.X))
                hs(v.reciprocal(ytmp[:, 9:10], ytmp[:, 8:9]))
                for kc in range(nkc):
                    k0 = kc * 512
                    kw = min(512, klen - k0)
                    hs(v.tensor_scalar(pbuf[:, k0:k0 + kw], pbuf[:, k0:k0 + kw],
                                       ytmp[:, 9:10], None, ALU.mult))
                ctx.vdr.v += 1
                v.drain().then_inc(ctx.vdr.h, 1)
            mods_rdy = ctx.vdr.v

            ngrp = (nkt + 3) // 4
            for grp in range(ngrp):
                n = min(4, nkt - grp * 4)

                @block.tensor
                def _(t, grp=grp, n=n, mods_rdy=mods_rdy, evprev=ctx.vdr.v):
                    t.wait_ge(ctx.vdr.h, mods_rdy)
                    if grp > 0:
                        t.wait_ge(ctx.vdr.h, evprev)
                    for j in range(n):
                        t.transpose(tr_ps[:, j * 128:(j + 1) * 128],
                                    pbuf[:, (grp * 4 + j) * 128:(grp * 4 + j + 1) * 128],
                                    identf[:])
                    ctx.tdr.v += 1
                    t.drain().then_inc(ctx.tdr.h, 1)
                trd = ctx.tdr.v

                @block.vector
                def _(v, grp=grp, n=n, trd=trd):
                    v.wait_ge(ctx.tdr.h, trd)
                    v.tensor_copy(attT[:, grp * 4 * 128: grp * 4 * 128 + n * 128],
                                  tr_ps[:, 0:n * 128])
                    ctx.vdr.v += 1
                    v.drain().then_inc(ctx.vdr.h, 1)
            attT_rdy = ctx.vdr.v

            @block.tensor
            def _(t, nkt=nkt, attT_rdy=attT_rdy):
                t.wait_ge(ctx.vdr.h, attT_rdy)
                for kt in range(nkt):
                    t.matmul(banks[5][:, 0:128], ckv_tm[:, kt * 128:(kt + 1) * 128],
                             attT[:, kt * 128:(kt + 1) * 128],
                             start=(kt == 0), stop=(kt == nkt - 1))
                ctx.tdr.v += 1
                t.drain().then_inc(ctx.tdr.h, 1)
            ctx_rdy = ctx.tdr.v

            @block.vector
            def _(v, ctx_rdy=ctx_rdy):
                v.wait_ge(ctx.tdr.h, ctx_rdy)
                v.tensor_copy(ctxs[:, 128:256], banks[5][:, 0:128])
                ctx.vdr.v += 1
                v.drain().then_inc(ctx.vdr.h, 1)
            ctxs_rdy = ctx.vdr.v

            @block.tensor
            def _(t, h=h, ctxs_rdy=ctxs_rdy):
                t.wait_ge(ctx.vdr.h, ctxs_rdy)
                t.matmul(banks[6][0:64, 0:128], veff_sb[:, h * 64:(h + 1) * 64],
                         ctxs[:, 128:256], start=True, stop=True)
                ctx.tdr.v += 1
                t.drain().then_inc(ctx.tdr.h, 1)
            y_rdy = ctx.tdr.v

            @block.scalar
            def _(s, h=h, q0=q0, y_rdy=y_rdy):
                s.wait_ge(ctx.tdr.h, y_rdy)
                b2 = (h % 2) * 64
                fc = h // 2
                s.activation(y_fm[b2:b2 + 64, fc * T + q0: fc * T + q0 + 128],
                             banks[6][0:64, 0:128], AF.Copy)
                ctx.sdr.v += 1
                s.drain().then_inc(ctx.sdr.h, 1)
            prev_yevac = ctx.sdr.v

    fence(ctx, block)
    emit_plain_residual(ctx, block, y_fm, stream_hbm, schunk_res, factor=1.0)
    fence(ctx, block)


def build_layer3_program(parts=("ffn1", "attn", "ffn2")):
    nc = bass.Bass("TRN2", target_bir_lowering=False)
    stream_in = nc.dram_tensor("stream_in", [D, T], F32, kind="ExternalInput").ap()
    stream_out = nc.dram_tensor("stream_out", [D, T], F32, kind="ExternalOutput").ap()
    gate_nat = nc.dram_tensor("gate_nat", [2816, 1024], F32, kind="ExternalInput").ap()
    up_nat = nc.dram_tensor("up_nat", [2816, 1024], F32, kind="ExternalInput").ap()
    down_nat = nc.dram_tensor("down_nat", [1024, 2816], F32, kind="ExternalInput").ap()
    wdqT = nc.dram_tensor("wdqT", [1024, 128], F32, kind="ExternalInput").ap()
    wdkvT = nc.dram_tensor("wdkvT", [1024, 128], F32, kind="ExternalInput").ap()
    wuqc_in = nc.dram_tensor("wuqc", [128, 1024], F32, kind="ExternalInput").ap()
    wukc_in = nc.dram_tensor("wukc", [128, 1024], F32, kind="ExternalInput").ap()
    wuvc_in = nc.dram_tensor("wuvc", [128, 1024], F32, kind="ExternalInput").ap()
    wdq_in = nc.dram_tensor("wdq", [128, 1024], F32, kind="ExternalInput").ap()
    wuqT_in = nc.dram_tensor("wuqT", [128, 1024], F32, kind="ExternalInput").ap()
    woT = nc.dram_tensor("woT", [1024, 1024], F32, kind="ExternalInput").ap()
    tri_in = nc.dram_tensor("tri", [128, 128], F32, kind="ExternalInput").ap()
    identf_in = nc.dram_tensor("identf", [128, 128], F32, kind="ExternalInput").ap()
    ones_in = nc.dram_tensor("ones", [128, 128], BF16, kind="ExternalInput").ap()
    ccols_in = nc.dram_tensor("ccols", [128, len(CONST_VALS)], F32, kind="ExternalInput").ap()

    ctx = Ctx(nc)
    bfa = ctx.sb("bfa", [128, 66816], BF16)
    fpa = ctx.sb("fpa", [128, 16480], F32)
    wbig = [ctx.sb(f"wbg{i}", [128, 1024], BF16) for i in range(2)]
    ones_bf = ctx.sb("ones_bf", [128, 128], BF16)
    ccols = ctx.sb("ccols_sb", [128, len(CONST_VALS)], F32)
    identf = ctx.sb("identf_sb", [128, 128], F32)
    trimask = ctx.sb("tri_sb", [128, 128], F32)
    ps_all = ctx.psum("ps_all", [128, 4096], F32)
    pbanks = [ps_all[:, i * 512:(i + 1) * 512] for i in range(8)]

    xhat = bfa[:, 0:16384]
    hbuf = bfa[:, 16384:38912]
    out_fm = bfa[:, 38912:55296]
    slots_gu = [bfa[:, 55296:57344]]
    slots_d = [bfa[:, 57344:60160]]
    q_fm = bfa[:, 16384:32768]
    y_fm = bfa[:, 32768:49152]
    ckv_fm = bfa[:, 49152:51200]
    ckv_tm = bfa[:, 51200:53248]
    q2_fm = bfa[:, 53248:55296]
    qlat = bfa[:, 55296:57344]
    attT = bfa[:, 57344:59392]
    keff_sb = bfa[:, 59392:60416]
    veff_sb = bfa[:, 60416:61440]
    ctxs = bfa[:, 61440:61696]
    wuq_c = bfa[:, 61696:62720]
    wuk_c = bfa[:, 62720:63744]
    wuv_c = bfa[:, 63744:64768]
    wdq_sb = bfa[:, 64768:65792]
    wuqT_sb = bfa[:, 65792:66816]
    sq_scr = bfa[:, 16384:18432]
    bcast = bfa[:, 18432:22528]
    ab_rows = bfa[:, 22528:26624]
    qa = fpa[:, 0:8448]
    pbuf = fpa[:, 0:2048]
    wld = fpa[:, 8448:11264]
    fptile = fpa[:, 11264:14080]
    bstat = fpa[:, 14080:14432]
    schunk = fpa[:, 14432:16480]
    schunk_res = fpa[:, 14432:16480]
    rowbuf = fpa[:, 0:1536]
    utmp = fpa[:, 11264:12288]
    ytmp = fpa[:, 14080:14112]

    with nc.Block() as block:
        emit_boot(ctx, block)

        @block.gpsimd
        def _(g):
            g.dma_start(out=ones_bf[:], in_=ones_in).then_inc(ctx.sio.h, 16)
            g.dma_start(out=ccols[:], in_=ccols_in).then_inc(ctx.sio.h, 16)
            g.dma_start(out=identf[:], in_=identf_in).then_inc(ctx.sio.h, 16)
            g.dma_start(out=trimask[:], in_=tri_in).then_inc(ctx.sio.h, 16)
        ctx.sio.v += 64
        pro_io = ctx.sio.v
        for en in ("tensor", "vector", "scalar", "sync"):
            getattr(block, en)(lambda e: e.wait_ge(ctx.sio.h, pro_io))

        @block.sync
        def _(sy):
            sy.dma_start(out=stream_out, in_=stream_in).then_inc(ctx.sio.h, 16)
        ctx.sio.v += 16
        fence(ctx, block)

        def do_ffn():
            emit_stream_ln(ctx, block, stream_out, schunk, xhat, sq_scr, rowbuf,
                           ab_rows, bcast, pbanks, ones_bf, ccols)
            fence(ctx, block)
            emit_l3_xquant(ctx, block, xhat, 8, fptile, qa, bstat, identf,
                           ps_all[:, 2048:2560], Wc=1024, stride=T, ntt=16, tmbuf=wld)
            fence(ctx, block)
            emit_l3_ffn(ctx, block, xhat, gate_nat, up_nat, down_nat, hbuf,
                        out_fm, wld, qa, bstat, utmp, slots_gu, slots_d,
                        pbanks, ps_all, identf, stream_out, schunk_res, fptile)
            fence(ctx, block)

        if "ffn1" in parts:
            do_ffn()
        if "attn" in parts:
            emit_stream_ln(ctx, block, stream_out, schunk, xhat, sq_scr, rowbuf,
                           ab_rows, bcast, pbanks, ones_bf, ccols)
            fence(ctx, block)

            @block.gpsimd
            def _(g):
                g.dma_start(out=wuq_c[:], in_=wuqc_in).then_inc(ctx.sio.h, 16)
                g.dma_start(out=wuk_c[:], in_=wukc_in).then_inc(ctx.sio.h, 16)
                g.dma_start(out=wuv_c[:], in_=wuvc_in).then_inc(ctx.sio.h, 16)
                g.dma_start(out=wdq_sb[:], in_=wdq_in).then_inc(ctx.sio.h, 16)
                g.dma_start(out=wuqT_sb[:], in_=wuqT_in).then_inc(ctx.sio.h, 16)
            ctx.sio.v += 80
            fence(ctx, block)
            emit_l3_attn(ctx, block, xhat, wdqT, wdkvT, wuq_c, wuk_c, wuv_c,
                         wdq_sb, wuqT_sb, woT, bfa[:, 16384:24576], keff_sb,
                         veff_sb, ckv_fm, ckv_tm, q2_fm, q_fm, qlat, ctxs,
                         pbuf, attT, ytmp, fptile, y_fm, ps_all, wbig, identf,
                         trimask, stream_out, schunk_res)
            fence(ctx, block)
        if "ffn2" in parts:
            do_ffn()

        stv = ctx.st.v
        @block.sync
        def _(sy):
            sy.wait_ge(ctx.st.h, stv)

    ctx.es.close()
    nc.gpsimd.sem_clear(ctx.clear_rng)
    return nc


def prep_layer3_inputs(ip):
    out = {}
    out["gate_nat"] = np.ascontiguousarray(ip["L3_ffn_w_gate"])
    out["up_nat"] = np.ascontiguousarray(ip["L3_ffn_w_up"])
    out["down_nat"] = np.ascontiguousarray(ip["L3_ffn_w_down"])
    out["wdqT"] = np.ascontiguousarray(ip["L3_attn_W_dq"].T)
    out["wdkvT"] = np.ascontiguousarray(ip["L3_attn_W_dkv"].T)
    chunkcat = lambda w: np.ascontiguousarray(
        np.concatenate([w[i * 128:(i + 1) * 128, :] for i in range(8)], axis=1))
    out["wuqc"] = chunkcat(ip["L3_attn_W_uq"])
    out["wukc"] = chunkcat(ip["L3_attn_W_uk"])
    out["wuvc"] = chunkcat(ip["L3_attn_W_uv"])
    out["wdq"] = np.ascontiguousarray(ip["L3_attn_W_dq"])
    out["wuqT"] = np.ascontiguousarray(ip["L3_attn_W_uq"].T)
    out["woT"] = np.ascontiguousarray(ip["L3_attn_W_o"].T)
    return out


# version 03 (final)
# speedup vs baseline: 1.1883x; 1.1883x over previous
import math
import contextlib
import numpy as np
import ml_dtypes
import concourse.bass as bass
import concourse.mybir as mybir
from concourse.bass_utils import run_bass_kernel_spmd

F32 = mybir.dt.float32
BF16 = mybir.dt.bfloat16
AF = mybir.ActivationFunctionType
ALU = mybir.AluOpType
AX = mybir.AxisListType

NCORES = 8
T = 2048
D = 1024
NT = T // 512
ND = D // 128
BF = ml_dtypes.bfloat16

CONST_VALS = [0.0, 1.0, -1.0, 0.5, 7.0, -7.0, 1e-5,
              float(np.float32(np.finfo(np.float32).eps)), 1e-12, 1.0 / 1024.0]
CCOL = {v: i for i, v in enumerate(CONST_VALS)}


def host_constants():
    c = {}
    c["c_ident"] = np.ascontiguousarray(np.eye(128, dtype=np.float32).astype(BF))
    c["c_ones"] = np.ascontiguousarray(np.ones((128, 128), dtype=np.float32).astype(BF))
    i = np.arange(128)
    c["c_trimask"] = np.where(i[None, :] <= i[:, None], 0.0, -1e30).astype(np.float32)
    c["c_identf"] = np.eye(128, dtype=np.float32)
    c["c_cols"] = np.tile(np.array(CONST_VALS, dtype=np.float32)[None, :], (128, 1))
    return c


def body_sem_clear_range(nc):
    kr = bass.get_kernel_semaphore_range()
    skip = 2
    if nc._bir_kernel_barrier_sem is not None:
        skip += 1
    skip += len(nc._monotonic_sems)
    return range(kr.start + skip, kr.stop)


class Sem:
    def __init__(self, ctx, name):
        self.h = ctx.es.enter_context(ctx.nc.semaphore(name))
        self.v = 0


class Ctx:

    def __init__(self, nc):
        self.nc = nc
        self.es = contextlib.ExitStack()
        self.clear_rng = body_sem_clear_range(nc)
        self.boot = Sem(self, "boot")
        self.wdma = Sem(self, "wdma")
        self.sio = Sem(self, "sio")
        self.tdr = Sem(self, "tdr")
        self.vdr = Sem(self, "vdr")
        self.sdr = Sem(self, "sdr")
        self.gdr = Sem(self, "gdr")
        self.vch = Sem(self, "vch")
        self.sch = Sem(self, "sch")
        self.evk = Sem(self, "evk")
        self.st = Sem(self, "st")
        self.aux = Sem(self, "aux")
        self.aux2 = Sem(self, "aux2")

    def sb(self, name, shape, dtype):
        return self.es.enter_context(self.nc.sbuf_tensor(name, shape, dtype))

    def psum(self, name, shape, dtype=F32):
        return self.es.enter_context(self.nc.psum_tensor(name, shape, dtype))


def emit_boot(ctx, block):
    @block.gpsimd
    def _(g):
        g.sem_clear(ctx.clear_rng)
        g.nop().then_inc(ctx.boot.h, 1)
    ctx.boot.v = 1
    block.tensor(lambda e: e.wait_ge(ctx.boot.h, 1))
    block.vector(lambda e: e.wait_ge(ctx.boot.h, 1))
    block.scalar(lambda e: e.wait_ge(ctx.boot.h, 1))
    block.sync(lambda e: e.wait_ge(ctx.boot.h, 1))



def emit_linear_fm(ctx, block, C, F, w_ap, src_fn, evac_fn, wslots, pbanks,
                   Tloc=T, fb_per_dma=2, extra_tensor_wait=None):
    ncc = C // 128
    nfb = F // 128
    ntc = (Tloc + 511) // 512
    tcs = min(512, Tloc)
    nslot = len(wslots)
    ngrp = (nfb + fb_per_dma - 1) // fb_per_dma
    assert len(pbanks) >= 2 * ntc

    wbase = ctx.wdma.v
    tbase = ctx.tdr.v
    ebase = ctx.evk.v

    @block.gpsimd
    def _(g):
        for gi in range(ngrp):
            f0 = gi * fb_per_dma * 128
            fw = min(fb_per_dma * 128, F - f0)
            slot = wslots[gi % nslot]
            if gi >= nslot:
                g.wait_ge(ctx.tdr.h, tbase + (gi - nslot + 1))
            g.dma_start(
                out=slot[:, 0:ncc * fw].rearrange("c (k m) -> c k m", m=fw),
                in_=w_ap[:, f0:f0 + fw].rearrange("(k c) m -> c k m", c=128),
            ).then_inc(ctx.wdma.h, 16)
    ctx.wdma.v = wbase + 16 * ngrp

    @block.tensor
    def _(t):
        if extra_tensor_wait is not None:
            extra_tensor_wait(t)
        for gi in range(ngrp):
            t.wait_ge(ctx.wdma.h, wbase + 16 * (gi + 1))
            f0 = gi * fb_per_dma * 128
            fw = min(fb_per_dma * 128, F - f0)
            slot = wslots[gi % nslot]
            for fb in range(gi * fb_per_dma, min((gi + 1) * fb_per_dma, nfb)):
                foff = fb * 128 - f0
                pb0 = (fb % 2) * ntc
                if fb >= 2:
                    t.wait_ge(ctx.evk.h, ebase + (fb - 2) + 1)
                for ti in range(ntc):
                    psb = pbanks[pb0 + ti]
                    for cc in range(ncc):
                        t.matmul(psb[:, 0:tcs],
                                 slot[:, cc * fw + foff: cc * fw + foff + 128],
                                 src_fn(cc)[:, ti * tcs:(ti + 1) * tcs],
                                 start=(cc == 0), stop=(cc == ncc - 1))
            ctx.tdr.v += 1
            t.drain().then_inc(ctx.tdr.h, 1)

    @block.scalar
    def _(s):
        for gi in range(ngrp):
            s.wait_ge(ctx.tdr.h, tbase + gi + 1)
            for fb in range(gi * fb_per_dma, min((gi + 1) * fb_per_dma, nfb)):
                pb0 = (fb % 2) * ntc
                for ti in range(ntc):
                    evac_fn(s, fb, ti, pbanks[pb0 + ti][:, 0:tcs])
                ctx.evk.v += 1
                s.drain().then_inc(ctx.evk.h, 1)



def emit_norm(ctx, block, stream, xhat, sq_scratch, rowbuf, ab_rows, bcast,
              pbanks, ones_bf, ccols, kind="ln", gamma_cols=None, src_bf=None):
    is_ln = kind == "ln"

    if is_ln:
        @block.vector
        def _(v):
            for k in range(ND):
                v.tensor_copy(xhat[:, k * T:(k + 1) * T], stream[:, k * T:(k + 1) * T])
            ctx.vdr.v += 1
            v.drain().then_inc(ctx.vdr.h, 1)
        xsrc = xhat
    else:
        xsrc = src_bf
    x_rdy = ctx.vdr.v

    @block.scalar
    def _(s):
        if is_ln:
            s.wait_ge(ctx.vdr.h, x_rdy)
        for k in range(ND):
            s.activation(sq_scratch[:, k * T:(k + 1) * T], xsrc[:, k * T:(k + 1) * T], AF.Square)
        ctx.sdr.v += 1
        s.drain().then_inc(ctx.sdr.h, 1)
    sq_rdy = ctx.sdr.v

    @block.tensor
    def _(t):
        t.wait_ge(ctx.sdr.h, sq_rdy)
        srcs = ((0, xsrc), (1, sq_scratch)) if is_ln else ((1, sq_scratch),)
        for which, buf in srcs:
            for ti in range(NT):
                pb = pbanks[which * NT + ti]
                for k in range(ND):
                    t.matmul(pb[0:1, 0:512], ones_bf[:, 0:1],
                             buf[:, k * T + ti * 512: k * T + ti * 512 + 512],
                             start=(k == 0), stop=(k == ND - 1))
        ctx.tdr.v += 1
        t.drain().then_inc(ctx.tdr.h, 1)
    sums_rdy = ctx.tdr.v

    @block.vector
    def _(v):
        vb = ctx.vch
        def hs(inst):
            vb.v += 1
            inst.then_inc(vb.h, 1)
            v.wait_ge(vb.h, vb.v)
        v.wait_ge(ctx.tdr.h, sums_rdy)
        for ti in range(NT):
            if is_ln:
                hs(v.tensor_copy(rowbuf[0:1, ti * 512:(ti + 1) * 512], pbanks[ti][0:1, 0:512]))
            hs(v.tensor_copy(rowbuf[0:1, T + ti * 512: T + (ti + 1) * 512],
                             pbanks[NT + ti][0:1, 0:512]))
        if is_ln:
            hs(v.tensor_scalar_mul(rowbuf[0:1, 2 * T:3 * T], rowbuf[0:1, 0:T], 1.0 / 1024.0))
            hs(v.tensor_scalar_mul(rowbuf[0:1, 3 * T:4 * T], rowbuf[0:1, T:2 * T], 1.0 / 1024.0))
            hs(v.tensor_mul(rowbuf[0:1, 0:T], rowbuf[0:1, 2 * T:3 * T], rowbuf[0:1, 2 * T:3 * T]))
            hs(v.tensor_sub(rowbuf[0:1, 3 * T:4 * T], rowbuf[0:1, 3 * T:4 * T], rowbuf[0:1, 0:T]))
        else:
            hs(v.tensor_scalar_mul(rowbuf[0:1, 3 * T:4 * T], rowbuf[0:1, T:2 * T], 1.0 / 1024.0))
        ctx.vdr.v += 1
        v.drain().then_inc(ctx.vdr.h, 1)
    var_rdy = ctx.vdr.v

    eps_col = ccols[:, CCOL[1e-5]:CCOL[1e-5] + 1] if is_ln else \
        ccols[:, CCOL[float(np.float32(np.finfo(np.float32).eps))]:
              CCOL[float(np.float32(np.finfo(np.float32).eps))] + 1]

    @block.scalar
    def _(s):
        sc = ctx.sch
        def hs(inst):
            sc.v += 1
            inst.then_inc(sc.h, 1)
            s.wait_ge(sc.h, sc.v)
        s.wait_ge(ctx.vdr.h, var_rdy)
        hs(s.activation(rowbuf[0:1, T:2 * T], rowbuf[0:1, 3 * T:4 * T], AF.Ln,
                        bias=eps_col[0:1, :]))
        hs(s.activation(rowbuf[0:1, 3 * T:4 * T], rowbuf[0:1, T:2 * T], AF.Exp, scale=-0.5))
        hs(s.activation(ab_rows[0:1, 0:T], rowbuf[0:1, 3 * T:4 * T], AF.Copy))
        ctx.sdr.v += 1
        s.drain().then_inc(ctx.sdr.h, 1)
    rstd_rdy = ctx.sdr.v

    if is_ln:
        @block.vector
        def _(v):
            v.wait_ge(ctx.sdr.h, rstd_rdy)
            v.tensor_mul(ab_rows[0:1, T:2 * T], rowbuf[0:1, 2 * T:3 * T],
                         rowbuf[0:1, 3 * T:4 * T])
            ctx.vdr.v += 1
            v.drain().then_inc(ctx.vdr.h, 1)
    ab_rdy = ctx.vdr.v if is_ln else None

    @block.tensor
    def _(t):
        t.wait_ge(ctx.sdr.h, rstd_rdy)
        if is_ln:
            t.wait_ge(ctx.vdr.h, ab_rdy)
        nrow = 2 if is_ln else 1
        for which in range(nrow):
            for ti in range(NT):
                pb = pbanks[which * NT + ti]
                t.matmul(pb[:, 0:512], ones_bf[0:1, 0:128],
                         ab_rows[0:1, which * T + ti * 512: which * T + ti * 512 + 512],
                         start=True, stop=True)
        ctx.tdr.v += 1
        t.drain().then_inc(ctx.tdr.h, 1)
    bc_rdy = ctx.tdr.v

    @block.vector
    def _(v):
        v.wait_ge(ctx.tdr.h, bc_rdy)
        nrow = 2 if is_ln else 1
        for which in range(nrow):
            for ti in range(NT):
                v.tensor_copy(bcast[:, which * T + ti * 512: which * T + ti * 512 + 512],
                              pbanks[which * NT + ti][:, 0:512])
        ctx.vdr.v += 1
        v.drain().then_inc(ctx.vdr.h, 1)

        v.wait_ge(ctx.vdr.h, ctx.vdr.v)
        for k in range(ND):
            xk = xhat[:, k * T:(k + 1) * T]
            sk = xsrc[:, k * T:(k + 1) * T]
            v.tensor_mul(xk, sk, bcast[:, 0:T])
            if is_ln:
                v.tensor_sub(xk, xk, bcast[:, T:2 * T])
            else:
                v.tensor_scalar_mul(xk, xk, gamma_cols[:, k:k + 1])
        ctx.vdr.v += 1
        v.drain().then_inc(ctx.vdr.h, 1)
    return ctx.vdr.v



def fence(ctx, block, engines=("tensor", "vector", "scalar", "gpsimd", "sync")):
    tv, vv, sv, ev = ctx.tdr.v, ctx.vdr.v, ctx.sdr.v, ctx.evk.v
    gv, iov, stv = ctx.gdr.v, ctx.sio.v, ctx.st.v

    def mk(e):
        if tv: e.wait_ge(ctx.tdr.h, tv)
        if vv: e.wait_ge(ctx.vdr.h, vv)
        if sv: e.wait_ge(ctx.sdr.h, sv)
        if ev: e.wait_ge(ctx.evk.h, ev)
        if gv: e.wait_ge(ctx.gdr.h, gv)
        if iov: e.wait_ge(ctx.sio.h, iov)
        if stv: e.wait_ge(ctx.st.h, stv)
    for name in engines:
        getattr(block, name)(mk)



def emit_stream_ln(ctx, block, stream_hbm, schunk, xhat, sq_scratch, rowbuf,
                   ab_rows, bcast, pbanks, ones_bf, ccols):
    sio_base = ctx.sio.v
    vdr_base = ctx.vdr.v

    @block.sync
    def _(sy):
        for k in range(ND):
            if k >= 1:
                sy.wait_ge(ctx.vdr.h, vdr_base + k)
            sy.dma_start(out=schunk[:, 0:T],
                         in_=stream_hbm[k * 128:(k + 1) * 128, :]).then_inc(ctx.sio.h, 16)
    ctx.sio.v = sio_base + 16 * ND

    @block.vector
    def _(v):
        for k in range(ND):
            v.wait_ge(ctx.sio.h, sio_base + 16 * (k + 1))
            v.tensor_copy(xhat[:, k * T:(k + 1) * T], schunk[:, 0:T])
            ctx.vdr.v += 1
            v.drain().then_inc(ctx.vdr.h, 1)

    sdr_base = ctx.sdr.v
    @block.scalar
    def _(s):
        for k in range(ND):
            s.wait_ge(ctx.vdr.h, vdr_base + (k + 1))
            s.activation(sq_scratch[:, 0:T], xhat[:, k * T:(k + 1) * T], AF.Square)
            ctx.sdr.v += 1
            s.drain().then_inc(ctx.sdr.h, 1)

    @block.tensor
    def _(t):
        for k in range(ND):
            t.wait_ge(ctx.vdr.h, vdr_base + (k + 1))
            for ti in range(NT):
                t.matmul(pbanks[ti][0:1, 0:512], ones_bf[:, 0:1],
                         xhat[:, k * T + ti * 512: k * T + ti * 512 + 512],
                         start=(k == 0), stop=(k == ND - 1))
            t.wait_ge(ctx.sdr.h, sdr_base + (k + 1))
            for ti in range(NT):
                t.matmul(pbanks[NT + ti][0:1, 0:512], ones_bf[:, 0:1],
                         sq_scratch[:, ti * 512: ti * 512 + 512],
                         start=(k == 0), stop=(k == ND - 1))
        ctx.tdr.v += 1
        t.drain().then_inc(ctx.tdr.h, 1)
    sums_rdy = ctx.tdr.v

    _emit_ln_tail(ctx, block, xhat, rowbuf, ab_rows, bcast, pbanks, ones_bf, ccols,
                  sums_rdy, is_ln=True, xsrc=xhat)
    return ctx.vdr.v


def _emit_ln_tail(ctx, block, xhat, rowbuf, ab_rows, bcast, pbanks, ones_bf, ccols,
                  sums_rdy, is_ln=True, xsrc=None, gamma_cols=None):
    eps_val = 1e-5 if is_ln else float(np.float32(np.finfo(np.float32).eps))
    s0 = rowbuf[0:1, 0:512]
    s1 = rowbuf[0:1, 512:1024]
    t2 = rowbuf[0:1, 1024:1536]

    for ti in range(NT):
        @block.vector
        def _(v, ti=ti):
            vb = ctx.vch
            def hs(inst):
                vb.v += 1
                inst.then_inc(vb.h, 1)
                v.wait_ge(vb.h, vb.v)
            if ti == 0:
                v.wait_ge(ctx.tdr.h, sums_rdy)
            if is_ln:
                hs(v.tensor_copy(s0, pbanks[ti][0:1, 0:512]))
            hs(v.tensor_copy(s1, pbanks[NT + ti][0:1, 0:512]))
            if is_ln:
                hs(v.tensor_scalar_mul(s0, s0, 1.0 / 1024.0))
                hs(v.tensor_mul(t2, s0, s0))
                hs(v.tensor_scalar_mul(s1, s1, 1.0 / 1024.0))
                hs(v.tensor_sub(s1, s1, t2))
            else:
                hs(v.tensor_scalar_mul(s1, s1, 1.0 / 1024.0))
            ctx.vdr.v += 1
            v.drain().then_inc(ctx.vdr.h, 1)
        var_rdy = ctx.vdr.v

        @block.scalar
        def _(s, ti=ti, var_rdy=var_rdy):
            sc = ctx.sch
            def hs(inst):
                sc.v += 1
                inst.then_inc(sc.h, 1)
                s.wait_ge(sc.h, sc.v)
            s.wait_ge(ctx.vdr.h, var_rdy)
            hs(s.activation(s1, s1, AF.Ln, bias=ccols[0:1, CCOL[eps_val]:CCOL[eps_val] + 1]))
            hs(s.activation(s1, s1, AF.Exp, scale=-0.5))
            hs(s.activation(ab_rows[0:1, ti * 512:(ti + 1) * 512], s1, AF.Copy))
            ctx.sdr.v += 1
            s.drain().then_inc(ctx.sdr.h, 1)
        rstd_rdy = ctx.sdr.v

        @block.vector
        def _(v, ti=ti, rstd_rdy=rstd_rdy):
            v.wait_ge(ctx.sdr.h, rstd_rdy)
            if is_ln:
                v.tensor_mul(ab_rows[0:1, T + ti * 512: T + (ti + 1) * 512], s0, s1)
            ctx.vdr.v += 1
            v.drain().then_inc(ctx.vdr.h, 1)
    ab_rdy = ctx.vdr.v

    @block.tensor
    def _(t):
        t.wait_ge(ctx.vdr.h, ab_rdy)
        nrow = 2 if is_ln else 1
        for which in range(nrow):
            for ti in range(NT):
                t.matmul(pbanks[which * NT + ti][:, 0:512], ones_bf[0:1, 0:128],
                         ab_rows[0:1, which * T + ti * 512: which * T + ti * 512 + 512],
                         start=True, stop=True)
        ctx.tdr.v += 1
        t.drain().then_inc(ctx.tdr.h, 1)
    bc_rdy = ctx.tdr.v

    @block.vector
    def _(v):
        v.wait_ge(ctx.tdr.h, bc_rdy)
        nrow = 2 if is_ln else 1
        for which in range(nrow):
            for ti in range(NT):
                v.tensor_copy(bcast[:, which * T + ti * 512: which * T + ti * 512 + 512],
                              pbanks[which * NT + ti][:, 0:512])
        ctx.vdr.v += 1
        v.drain().then_inc(ctx.vdr.h, 1)
        v.wait_ge(ctx.vdr.h, ctx.vdr.v)
        for k in range(ND):
            xk = xhat[:, k * T:(k + 1) * T]
            sk = xsrc[:, k * T:(k + 1) * T]
            v.tensor_mul(xk, sk, bcast[:, 0:T])
            if is_ln:
                v.tensor_sub(xk, xk, bcast[:, T:2 * T])
            else:
                v.tensor_scalar_mul(xk, xk, gamma_cols[:, k:k + 1])
        ctx.vdr.v += 1
        v.drain().then_inc(ctx.vdr.h, 1)



def emit_l0_ffn(ctx, block, xhat, wu_ap, h1_ap, h2_ap, w3_ap, scale_cols, u_tmp,
                bufA, bufB, wslots_small, wslots_big, pbanks, stream_hbm, schunk_res):
    Tc = 1024
    for tchunk in range(2):
        t0 = tchunk * Tc

        def src_x(cc, t0=t0):
            return xhat[:, cc * T + t0: cc * T + t0 + Tc]

        fence(ctx, block)
        _emit_u_stage(ctx, block, wu_ap, src_x, scale_cols, u_tmp, bufA,
                      wslots_small, pbanks, Tc)

        fence(ctx, block)
        def evac_h1(s, fb, ti, ps):
            s.activation(bufB[:, fb * Tc + ti * 512: fb * Tc + ti * 512 + 512],
                         ps, AF.Gelu, scale=scale_cols[:, 64 + fb:64 + fb + 1])
        emit_linear_fm(ctx, block, 4096, 4096, h1_ap,
                       lambda cc: bufA[:, cc * Tc:(cc + 1) * Tc],
                       evac_h1, wslots_big, pbanks[:4], Tloc=Tc, fb_per_dma=1)

        fence(ctx, block)
        def evac_h2(s, fb, ti, ps):
            s.activation(bufA[:, fb * Tc + ti * 512: fb * Tc + ti * 512 + 512],
                         ps, AF.Gelu, scale=scale_cols[:, 96 + fb:96 + fb + 1])
        emit_linear_fm(ctx, block, 4096, 4096, h2_ap,
                       lambda cc: bufB[:, cc * Tc:(cc + 1) * Tc],
                       evac_h2, wslots_big, pbanks[:4], Tloc=Tc, fb_per_dma=1)

        fence(ctx, block)
        _emit_w3_residual(ctx, block, w3_ap, bufA, scale_cols, wslots_big, pbanks,
                          stream_hbm, schunk_res, Tc, t0)
    fence(ctx, block)


def _emit_u_stage(ctx, block, wu_ap, src_x, scale_cols, u_tmp, out_u, wslots, pbanks, Tc):
    ncc, nfb = 8, 64
    ntc = Tc // 512
    nslot = len(wslots)
    ngrp = nfb // 2
    wbase, tbase, ebase, sbase = ctx.wdma.v, ctx.tdr.v, ctx.evk.v, ctx.sdr.v

    @block.gpsimd
    def _(g):
        for gi in range(ngrp):
            f0 = gi * 256
            slot = wslots[gi % nslot]
            if gi >= nslot:
                g.wait_ge(ctx.tdr.h, tbase + (gi - nslot + 1))
            g.dma_start(out=slot[:, 0:ncc * 256].rearrange("c (k m) -> c k m", m=256),
                        in_=wu_ap[:, f0:f0 + 256].rearrange("(k c) m -> c k m", c=128),
                        ).then_inc(ctx.wdma.h, 16)
    ctx.wdma.v = wbase + 16 * ngrp

    @block.tensor
    def _(t):
        for gi in range(ngrp):
            t.wait_ge(ctx.wdma.h, wbase + 16 * (gi + 1))
            slot = wslots[gi % nslot]
            if gi >= 1:
                t.wait_ge(ctx.evk.h, ebase + gi)
            for half in range(2):
                foff = half * 128
                for ti in range(ntc):
                    psb = pbanks[half * ntc + ti]
                    for cc in range(ncc):
                        t.matmul(psb[:, 0:512],
                                 slot[:, cc * 256 + foff: cc * 256 + foff + 128],
                                 src_x(cc)[:, ti * 512:(ti + 1) * 512],
                                 start=(cc == 0), stop=(cc == ncc - 1))
            ctx.tdr.v += 1
            t.drain().then_inc(ctx.tdr.h, 1)

    @block.scalar
    def _(s):
        for gi in range(ngrp):
            s.wait_ge(ctx.tdr.h, tbase + gi + 1)
            for ti in range(ntc):
                s.activation(u_tmp[:, ti * 512:(ti + 1) * 512], pbanks[ti][:, 0:512],
                             AF.Silu, scale=scale_cols[:, gi:gi + 1])
            ctx.sdr.v += 1
            s.drain().then_inc(ctx.sdr.h, 1)

    @block.vector
    def _(v):
        for gi in range(ngrp):
            v.wait_ge(ctx.sdr.h, sbase + gi + 1)
            for ti in range(ntc):
                v.scalar_tensor_tensor(
                    out_u[:, gi * Tc + ti * 512: gi * Tc + ti * 512 + 512],
                    pbanks[ntc + ti][:, 0:512], scale_cols[:, 32 + gi:32 + gi + 1],
                    u_tmp[:, ti * 512:(ti + 1) * 512], ALU.mult, ALU.mult)
            ctx.evk.v += 1
            v.drain().then_inc(ctx.evk.h, 1)


def _emit_w3_residual(ctx, block, w3_ap, src_buf, scale_cols, wslots, pbanks,
                      stream_hbm, schunk_res, Tc, t0):
    ncc, nfb = 32, 8
    ntc = Tc // 512
    nslot = len(wslots)
    fbpd = 1
    ngrp = nfb // fbpd
    wbase, tbase, ebase = ctx.wdma.v, ctx.tdr.v, ctx.evk.v
    sio_base = ctx.sio.v
    st_base = ctx.st.v

    @block.sync
    def _(sy):
        for fb in range(nfb):
            if fb >= 2:
                sy.wait_ge(ctx.st.h, st_base + 16 * (fb - 1))
            sy.dma_start(out=schunk_res[:, (fb % 2) * Tc:(fb % 2) * Tc + Tc],
                         in_=stream_hbm[fb * 128:(fb + 1) * 128, t0:t0 + Tc],
                         ).then_inc(ctx.sio.h, 16)
    ctx.sio.v = sio_base + 16 * nfb

    @block.gpsimd
    def _(g):
        for gi in range(ngrp):
            f0 = gi * fbpd * 128
            fw = fbpd * 128
            slot = wslots[gi % nslot]
            if gi >= nslot:
                g.wait_ge(ctx.tdr.h, tbase + (gi - nslot + 1))
            g.dma_start(out=slot[:, 0:ncc * fw].rearrange("c (k m) -> c k m", m=fw),
                        in_=w3_ap[:, f0:f0 + fw].rearrange("(k c) m -> c k m", c=128),
                        ).then_inc(ctx.wdma.h, 16)
    ctx.wdma.v = wbase + 16 * ngrp

    @block.tensor
    def _(t):
        for gi in range(ngrp):
            t.wait_ge(ctx.wdma.h, wbase + 16 * (gi + 1))
            slot = wslots[gi % nslot]
            for fb in range(gi * fbpd, (gi + 1) * fbpd):
                foff = (fb - gi * fbpd) * 128
                pb0 = (fb % 2) * ntc
                if fb >= 2:
                    t.wait_ge(ctx.evk.h, ebase + (fb - 2) + 1)
                for ti in range(ntc):
                    psb = pbanks[pb0 + ti]
                    for cc in range(ncc):
                        t.matmul(psb[:, 0:512],
                                 slot[:, cc * (fbpd * 128) + foff: cc * (fbpd * 128) + foff + 128],
                                 src_buf[:, cc * Tc + ti * 512: cc * Tc + ti * 512 + 512],
                                 start=(cc == 0), stop=(cc == ncc - 1))
            ctx.tdr.v += 1
            t.drain().then_inc(ctx.tdr.h, 1)

    @block.vector
    def _(v):
        for fb in range(nfb):
            gi = fb // fbpd
            v.wait_ge(ctx.tdr.h, tbase + gi + 1)
            v.wait_ge(ctx.sio.h, sio_base + 16 * (fb + 1))
            pb0 = (fb % 2) * ntc
            sc = schunk_res[:, (fb % 2) * Tc:(fb % 2) * Tc + Tc]
            for ti in range(ntc):
                v.scalar_tensor_tensor(
                    sc[:, ti * 512:(ti + 1) * 512], pbanks[pb0 + ti][:, 0:512],
                    scale_cols[:, 128 + fb:128 + fb + 1],
                    sc[:, ti * 512:(ti + 1) * 512], ALU.mult, ALU.add)
            ctx.evk.v += 1
            v.drain().then_inc(ctx.evk.h, 1)

    @block.scalar
    def _(s):
        for fb in range(nfb):
            s.wait_ge(ctx.evk.h, ebase + fb + 1)
            s.dma_start(out=stream_hbm[fb * 128:(fb + 1) * 128, t0:t0 + Tc],
                        in_=schunk_res[:, (fb % 2) * Tc:(fb % 2) * Tc + Tc],
                        ).then_inc(ctx.st.h, 16)
    ctx.st.v = st_base + 16 * nfb



def emit_l0_attn(ctx, block, xhat, qkv_w_ap, qkv_b_cols, wlat, lat_b_cols,
                 rope_cs, trimask, ident_f32, qfk, krope, vo, sec_stage,
                 lat_s1, lat_s2, attT, pbuf, ytmp, fptmp, y_tm, y_fm, ps_all,
                 wslots, stream_hbm, schunk_res, cproj_w_ap, cproj_b_cols, l2ob_col):
    SC = 1.0 / math.sqrt(32.0)
    banks = [ps_all[:, i * 512:(i + 1) * 512] for i in range(8)]

    for sec in range(6):
        f0 = sec * 512
        kind = ("q", "q", "k", "k", "v", "v")[sec]
        fence(ctx, block)

        def evac_qkv(s, fb, ti, ps, sec=sec):
            gfb = sec * 4 + fb
            s.activation(sec_stage[:, fb * T + ti * 512: fb * T + ti * 512 + 512],
                         ps, AF.Identity, bias=qkv_b_cols[:, gfb:gfb + 1])
        emit_linear_fm(ctx, block, 1024, 512, qkv_w_ap[:, f0:f0 + 512],
                       lambda cc: xhat[:, cc * T:(cc + 1) * T],
                       evac_qkv, wslots, banks, Tloc=T, fb_per_dma=2)
        fence(ctx, block)

        for fb in range(4):
            pt = (sec % 2) * 4 + fb
            stage = sec_stage[:, fb * T:(fb + 1) * T]
            _emit_lat_pair(ctx, block, kind, pt, stage, wlat, lat_b_cols,
                           rope_cs, qfk, krope, vo, lat_s1, lat_s2, banks)
            fence(ctx, block, engines=("tensor", "vector", "scalar"))
        fence(ctx, block)

    sc_ps = ps_all[:, 0:2048]
    tr_ps = banks[4]
    y_ps = banks[5][:, 0:64]
    fence(ctx, block)
    prev_yevac = ctx.sdr.v

    for h in range(16):
        for qt in range(16):
            q0 = qt * 128
            klen = q0 + 128
            nkc = (klen + 511) // 512
            nkt = klen // 128
            b = (h % 2) * 64
            qt_tile = qfk[:, (h // 2) * T:(h // 2) * T + T]
            kt_tile = krope[:, (h // 2) * T:(h // 2) * T + T]

            @block.tensor
            def _(t, qt_tile=qt_tile, kt_tile=kt_tile, b=b, q0=q0, klen=klen,
                  nkc=nkc, prev=prev_yevac):
                t.wait_ge(ctx.sdr.h, prev)
                for kc in range(nkc):
                    k0 = kc * 512
                    kw = min(512, klen - k0)
                    t.matmul(sc_ps[:, k0:k0 + kw], qt_tile[b:b + 32, q0:q0 + 128],
                             kt_tile[b:b + 32, k0:k0 + kw], start=True, stop=True)
                ctx.tdr.v += 1
                t.drain().then_inc(ctx.tdr.h, 1)
            sc_rdy = ctx.tdr.v

            @block.vector
            def _(v, q0=q0, klen=klen, sc_rdy=sc_rdy):
                vb = ctx.vch
                def hs(inst):
                    vb.v += 1
                    inst.then_inc(vb.h, 1)
                    v.wait_ge(vb.h, vb.v)
                v.wait_ge(ctx.tdr.h, sc_rdy)
                hs(v.tensor_add(sc_ps[:, q0:q0 + 128], sc_ps[:, q0:q0 + 128], trimask[:]))
                hs(v.reduce_max(ytmp[:, 0:1], sc_ps[:, 0:klen], axis=AX.X))
                hs(v.tensor_scalar_mul(ytmp[:, 1:2], ytmp[:, 0:1], -SC))
                ctx.vdr.v += 1
                v.drain().then_inc(ctx.vdr.h, 1)
            mask_rdy = ctx.vdr.v

            @block.scalar
            def _(s, klen=klen, nkc=nkc, mask_rdy=mask_rdy):
                s.wait_ge(ctx.vdr.h, mask_rdy)
                for kc in range(nkc):
                    k0 = kc * 512
                    kw = min(512, klen - k0)
                    s.activation(pbuf[:, k0:k0 + kw], sc_ps[:, k0:k0 + kw], AF.Exp,
                                 bias=ytmp[:, 1:2], scale=SC,
                                 accum_out=ytmp[:, 2 + kc:3 + kc])
                ctx.sdr.v += 1
                s.drain().then_inc(ctx.sdr.h, 1)
            exp_rdy = ctx.sdr.v

            @block.vector
            def _(v, nkc=nkc, exp_rdy=exp_rdy):
                vb = ctx.vch
                def hs(inst):
                    vb.v += 1
                    inst.then_inc(vb.h, 1)
                    v.wait_ge(vb.h, vb.v)
                v.wait_ge(ctx.sdr.h, exp_rdy)
                hs(v.reduce_sum(ytmp[:, 8:9], ytmp[:, 2:2 + nkc], axis=AX.X))
                hs(v.reciprocal(ytmp[:, 9:10], ytmp[:, 8:9]))
                ctx.vdr.v += 1
                v.drain().then_inc(ctx.vdr.h, 1)
            sum_rdy = ctx.vdr.v

            ngrp = (nkt + 3) // 4
            for grp in range(ngrp):
                n = min(4, nkt - grp * 4)

                @block.tensor
                def _(t, grp=grp, n=n, exp_rdy=exp_rdy, evprev=ctx.vdr.v):
                    t.wait_ge(ctx.sdr.h, exp_rdy)
                    if grp > 0:
                        t.wait_ge(ctx.vdr.h, evprev)
                    for j in range(n):
                        t.transpose(tr_ps[:, j * 128:(j + 1) * 128],
                                    pbuf[:, (grp * 4 + j) * 128:(grp * 4 + j + 1) * 128],
                                    ident_f32[:])
                    ctx.tdr.v += 1
                    t.drain().then_inc(ctx.tdr.h, 1)
                trd = ctx.tdr.v

                @block.vector
                def _(v, grp=grp, n=n, trd=trd):
                    v.wait_ge(ctx.tdr.h, trd)
                    v.tensor_copy(attT[:, grp * 4 * 128: grp * 4 * 128 + n * 128],
                                  tr_ps[:, 0:n * 128])
                    ctx.vdr.v += 1
                    v.drain().then_inc(ctx.vdr.h, 1)
            attT_rdy = ctx.vdr.v

            @block.tensor
            def _(t, h=h, nkt=nkt, attT_rdy=attT_rdy):
                t.wait_ge(ctx.vdr.h, attT_rdy)
                for kt in range(nkt):
                    t.matmul(y_ps, attT[:, kt * 128:(kt + 1) * 128],
                             vo[:, h * 1024 + kt * 64: h * 1024 + kt * 64 + 64],
                             start=(kt == 0), stop=(kt == nkt - 1))
                ctx.tdr.v += 1
                t.drain().then_inc(ctx.tdr.h, 1)
            pv_rdy = ctx.tdr.v

            @block.scalar
            def _(s, h=h, qt=qt, pv_rdy=pv_rdy, sum_rdy=sum_rdy):
                s.wait_ge(ctx.tdr.h, pv_rdy)
                s.wait_ge(ctx.vdr.h, sum_rdy)
                s.activation(y_tm[:, qt * 1024 + h * 64: qt * 1024 + h * 64 + 64],
                             y_ps, AF.Copy, scale=ytmp[:, 9:10])
                ctx.sdr.v += 1
                s.drain().then_inc(ctx.sdr.h, 1)
            prev_yevac = ctx.sdr.v

    fence(ctx, block)
    for qt in range(16):
        @block.vector
        def _(v, qt=qt, prev=ctx.sdr.v):
            if qt > 0:
                v.wait_ge(ctx.sdr.h, prev)
            v.tensor_copy(fptmp[:, 0:1024], y_tm[:, qt * 1024:(qt + 1) * 1024])
            ctx.vdr.v += 1
            v.drain().then_inc(ctx.vdr.h, 1)
        up_rdy = ctx.vdr.v
        for half in range(2):
            @block.tensor
            def _(t, half=half, up_rdy=up_rdy, prevs=ctx.sdr.v):
                t.wait_ge(ctx.vdr.h, up_rdy)
                if half == 1:
                    t.wait_ge(ctx.sdr.h, prevs)
                for j in range(4):
                    t.transpose(tr_ps[:, j * 128:(j + 1) * 128],
                                fptmp[:, (half * 4 + j) * 128:(half * 4 + j + 1) * 128],
                                ident_f32[:])
                ctx.tdr.v += 1
                t.drain().then_inc(ctx.tdr.h, 1)
            trd = ctx.tdr.v

            @block.scalar
            def _(s, qt=qt, half=half, trd=trd):
                s.wait_ge(ctx.tdr.h, trd)
                for j in range(4):
                    fc = half * 4 + j
                    s.activation(y_fm[:, fc * T + qt * 128: fc * T + qt * 128 + 128],
                                 tr_ps[:, j * 128:(j + 1) * 128], AF.Identity,
                                 bias=l2ob_col[:, 0:1])
                ctx.sdr.v += 1
                s.drain().then_inc(ctx.sdr.h, 1)

    fence(ctx, block)
    _emit_out_residual(ctx, block, cproj_w_ap, y_fm, wslots, banks,
                       stream_hbm, schunk_res, C=1024, bias_cols=cproj_b_cols,
                       scale_cols=None, t0=0, Tc=T)
    fence(ctx, block)


def _emit_lat_pair(ctx, block, kind, pt, stage, wlat, lat_b_cols, rope_cs,
                   qfk, krope, vo, lat_s1, lat_s2, banks):
    proj_col = {"q": 0, "k": 32, "v": 32}[kind]
    bias_col = {"q": 0, "k": 1, "v": 1}[kind]

    @block.tensor
    def _(t, prev_ev=ctx.sdr.v):
        t.wait_ge(ctx.sdr.h, prev_ev)
        for ti in range(NT):
            for b in (0, 64):
                t.matmul(banks[ti][b:b + 32, 0:512],
                         wlat[b:b + 64, proj_col:proj_col + 32],
                         stage[b:b + 64, ti * 512:(ti + 1) * 512],
                         start=True, stop=True)
        ctx.tdr.v += 1
        t.drain().then_inc(ctx.tdr.h, 1)
    lat_rdy = ctx.tdr.v

    @block.scalar
    def _(s, lat_rdy=lat_rdy):
        s.wait_ge(ctx.tdr.h, lat_rdy)
        for ti in range(NT):
            s.activation(lat_s1[:, ti * 512:(ti + 1) * 512], banks[ti][:, 0:512],
                         AF.Identity, bias=lat_b_cols[:, bias_col:bias_col + 1])
        ctx.sdr.v += 1
        s.drain().then_inc(ctx.sdr.h, 1)
    ev1 = ctx.sdr.v

    if kind in ("q", "k"):
        @block.tensor
        def _(t, ev1=ev1):
            t.wait_ge(ctx.sdr.h, ev1)
            for ti in range(NT):
                for b in (0, 64):
                    t.matmul(banks[4 + ti][b:b + 32, 0:512],
                             wlat[b:b + 32, 96:128],
                             lat_s1[b:b + 32, ti * 512:(ti + 1) * 512],
                             start=True, stop=True)
            ctx.tdr.v += 1
            t.drain().then_inc(ctx.tdr.h, 1)
        rot_rdy = ctx.tdr.v

        @block.vector
        def _(v, rot_rdy=rot_rdy):
            vb = ctx.vch
            def hs(inst):
                vb.v += 1
                inst.then_inc(vb.h, 1)
                v.wait_ge(vb.h, vb.v)
            v.wait_ge(ctx.tdr.h, rot_rdy)
            for ti in range(NT):
                sl = slice(ti * 512, (ti + 1) * 512)
                hs(v.tensor_mul(lat_s2[:, sl], lat_s1[:, sl], rope_cs[:, 0:T][:, sl]))
                hs(v.tensor_mul(lat_s1[:, sl], banks[4 + ti][:, 0:512], rope_cs[:, T:2 * T][:, sl]))
                hs(v.tensor_add(lat_s2[:, sl], lat_s2[:, sl], lat_s1[:, sl]))
            ctx.vdr.v += 1
            v.drain().then_inc(ctx.vdr.h, 1)
        rope_rdy = ctx.vdr.v

        if kind == "k":
            @block.vector
            def _(v, pt=pt, rope_rdy=rope_rdy):
                v.wait_ge(ctx.vdr.h, rope_rdy)
                v.tensor_copy(krope[:, pt * T:(pt + 1) * T], lat_s2[:, 0:T])
                ctx.vdr.v += 1
                v.drain().then_inc(ctx.vdr.h, 1)
        else:
            @block.tensor
            def _(t, rope_rdy=rope_rdy):
                t.wait_ge(ctx.vdr.h, rope_rdy)
                for ti in range(NT):
                    for b in (0, 64):
                        t.matmul(banks[ti][b:b + 32, 0:512],
                                 wlat[b:b + 32, 64:96],
                                 lat_s2[b:b + 32, ti * 512:(ti + 1) * 512],
                                 start=True, stop=True)
                ctx.tdr.v += 1
                t.drain().then_inc(ctx.tdr.h, 1)
            wqk_rdy = ctx.tdr.v

            @block.scalar
            def _(s, pt=pt, wqk_rdy=wqk_rdy):
                s.wait_ge(ctx.tdr.h, wqk_rdy)
                for ti in range(NT):
                    s.activation(qfk[:, pt * T + ti * 512: pt * T + ti * 512 + 512],
                                 banks[ti][:, 0:512], AF.Identity,
                                 bias=lat_b_cols[:, 2:3])
                ctx.sdr.v += 1
                s.drain().then_inc(ctx.sdr.h, 1)
    else:
        @block.tensor
        def _(t, ev1=ev1):
            t.wait_ge(ctx.sdr.h, ev1)
            for hh in range(2):
                b = hh * 64
                for tt in range(16):
                    bank = banks[4 + 2 * hh + tt // 8]
                    t.matmul(bank[:, (tt % 8) * 64:(tt % 8) * 64 + 64],
                             lat_s1[b:b + 32, tt * 128:(tt + 1) * 128],
                             wlat[b:b + 32, 128:192],
                             start=True, stop=True)
            ctx.tdr.v += 1
            t.drain().then_inc(ctx.tdr.h, 1)
        vo_rdy = ctx.tdr.v

        @block.vector
        def _(v, pt=pt, vo_rdy=vo_rdy):
            v.wait_ge(ctx.tdr.h, vo_rdy)
            for hh in range(2):
                h = pt * 2 + hh
                v.tensor_copy(vo[:, h * 1024: h * 1024 + 512], banks[4 + 2 * hh][:, 0:512])
                v.tensor_copy(vo[:, h * 1024 + 512: h * 1024 + 1024],
                              banks[5 + 2 * hh][:, 0:512])
            ctx.vdr.v += 1
            v.drain().then_inc(ctx.vdr.h, 1)


def _emit_out_residual(ctx, block, w_ap, src_fm, wslots, banks, stream_hbm,
                       schunk_res, C, bias_cols=None, scale_cols=None, t0=0, Tc=T,
                       factor_col=None):
    ncc, nfb = C // 128, 8
    ntc = Tc // 512
    nslot = len(wslots)
    ngrp = nfb
    wbase, tbase, ebase = ctx.wdma.v, ctx.tdr.v, ctx.evk.v
    sio_base = ctx.sio.v
    st_base = ctx.st.v

    dbuf = Tc <= 1024

    @block.sync
    def _(sy):
        for fb in range(nfb):
            if dbuf and fb >= 2:
                sy.wait_ge(ctx.st.h, st_base + 16 * (fb - 1))
            if not dbuf and fb >= 1:
                sy.wait_ge(ctx.st.h, st_base + 16 * fb)
            sy.dma_start(out=schunk_res[:, (fb % 2) * Tc:(fb % 2) * Tc + Tc]
                         if dbuf else schunk_res[:, 0:Tc],
                         in_=stream_hbm[fb * 128:(fb + 1) * 128, t0:t0 + Tc],
                         ).then_inc(ctx.sio.h, 16)
    ctx.sio.v = sio_base + 16 * nfb

    @block.gpsimd
    def _(g):
        for gi in range(ngrp):
            f0 = gi * 128
            slot = wslots[gi % nslot]
            if gi >= nslot:
                g.wait_ge(ctx.tdr.h, tbase + (gi - nslot + 1))
            g.dma_start(out=slot[:, 0:ncc * 128].rearrange("c (k m) -> c k m", m=128),
                        in_=w_ap[:, f0:f0 + 128].rearrange("(k c) m -> c k m", c=128),
                        ).then_inc(ctx.wdma.h, 16)
    ctx.wdma.v = wbase + 16 * ngrp

    @block.tensor
    def _(t):
        for fb in range(nfb):
            t.wait_ge(ctx.wdma.h, wbase + 16 * (fb + 1))
            slot = wslots[fb % nslot]
            pb0 = (fb % 2) * ntc
            if fb >= 2:
                t.wait_ge(ctx.evk.h, ebase + (fb - 2) + 1)
            for ti in range(ntc):
                psb = banks[pb0 + ti]
                for cc in range(ncc):
                    t.matmul(psb[:, 0:512],
                             slot[:, cc * 128: cc * 128 + 128],
                             src_fm[:, cc * Tc + ti * 512: cc * Tc + ti * 512 + 512]
                             if Tc != T else src_fm[:, cc * T + t0 + ti * 512: cc * T + t0 + ti * 512 + 512],
                             start=(cc == 0), stop=(cc == ncc - 1))
            ctx.tdr.v += 1
            t.drain().then_inc(ctx.tdr.h, 1)

    @block.vector
    def _(v):
        for fb in range(nfb):
            v.wait_ge(ctx.tdr.h, tbase + fb + 1)
            v.wait_ge(ctx.sio.h, sio_base + 16 * (fb + 1))
            pb0 = (fb % 2) * ntc
            sc = schunk_res[:, (fb % 2) * Tc:(fb % 2) * Tc + Tc] if dbuf \
                else schunk_res[:, 0:Tc]
            for ti in range(ntc):
                if factor_col is not None:
                    v.scalar_tensor_tensor(sc[:, ti * 512:(ti + 1) * 512],
                                           banks[pb0 + ti][:, 0:512],
                                           factor_col[:, fb:fb + 1],
                                           sc[:, ti * 512:(ti + 1) * 512],
                                           ALU.mult, ALU.add)
                else:
                    v.scalar_tensor_tensor(sc[:, ti * 512:(ti + 1) * 512],
                                           banks[pb0 + ti][:, 0:512], 1.0,
                                           sc[:, ti * 512:(ti + 1) * 512],
                                           ALU.mult, ALU.add)
            if bias_cols is not None:
                v.tensor_scalar_add(sc, sc, bias_cols[:, fb:fb + 1])
            ctx.evk.v += 1
            v.drain().then_inc(ctx.evk.h, 1)

    @block.scalar
    def _(s):
        for fb in range(nfb):
            s.wait_ge(ctx.evk.h, ebase + fb + 1)
            s.dma_start(out=stream_hbm[fb * 128:(fb + 1) * 128, t0:t0 + Tc],
                        in_=schunk_res[:, (fb % 2) * Tc:(fb % 2) * Tc + Tc]
                        if dbuf else schunk_res[:, 0:Tc],
                        ).then_inc(ctx.st.h, 16)
    ctx.st.v = st_base + 16 * nfb



def build_layer0_program(parts=("ffn1", "attn", "ffn2"), fused=None, stream_io=None, prefix="", do_copy=True):
    if fused is None:
        nc = bass.Bass("TRN2", target_bir_lowering=False)
    else:
        nc = fused[0]
    if stream_io is None:
        stream_in = nc.dram_tensor(prefix + "stream_in", [D, T], F32, kind="ExternalInput").ap()
        stream_out = nc.dram_tensor(prefix + "stream_out", [D, T], F32, kind="ExternalOutput").ap()
    else:
        stream_in, stream_out = stream_io
    wu = nc.dram_tensor(prefix + "wu", [1024, 8192], F32, kind="ExternalInput").ap()
    h1 = nc.dram_tensor(prefix + "h1", [4096, 4096], F32, kind="ExternalInput").ap()
    h2 = nc.dram_tensor(prefix + "h2", [4096, 4096], F32, kind="ExternalInput").ap()
    w3 = nc.dram_tensor(prefix + "w3", [4096, 1024], F32, kind="ExternalInput").ap()
    scl_in = nc.dram_tensor(prefix + "scl", [128, 144], F32, kind="ExternalInput").ap()
    qkvw = nc.dram_tensor(prefix + "qkvw", [1024, 3072], F32, kind="ExternalInput").ap()
    cprojw = nc.dram_tensor(prefix + "cprojw", [1024, 1024], F32, kind="ExternalInput").ap()
    qkvb_in = nc.dram_tensor(prefix + "qkvb", [128, 24], F32, kind="ExternalInput").ap()
    wlat_in = nc.dram_tensor(prefix + "wlat", [128, 192], F32, kind="ExternalInput").ap()
    latb_in = nc.dram_tensor(prefix + "latb", [128, 4], F32, kind="ExternalInput").ap()
    cpb_in = nc.dram_tensor(prefix + "cpb", [128, 8], F32, kind="ExternalInput").ap()
    rope_in = nc.dram_tensor(prefix + "rope", [128, 2 * T], F32, kind="ExternalInput").ap()
    tri_in = nc.dram_tensor(prefix + "tri", [128, 128], F32, kind="ExternalInput").ap()
    identf_in = nc.dram_tensor(prefix + "identf", [128, 128], F32, kind="ExternalInput").ap()
    ones_in = nc.dram_tensor(prefix + "ones", [128, 128], BF16, kind="ExternalInput").ap()
    ccols_in = nc.dram_tensor(prefix + "ccols", [128, len(CONST_VALS)], F32, kind="ExternalInput").ap()

    if fused is None:
        ctx = Ctx(nc)
    else:
        ctx = fused[1]
    les = contextlib.ExitStack()
    def _sb(n, sh, dt):
        return les.enter_context(nc.sbuf_tensor(prefix + n, sh, dt))
    def _ps(n, sh, dt=F32):
        return les.enter_context(nc.psum_tensor(prefix + n, sh, dt))
    bfa = _sb("bfa", [128, 81920], BF16)
    fpa = _sb("fpa", [128, 4640], F32)
    wsmall = [_sb(f"wsm{i}", [128, 2048], BF16) for i in range(2)]
    wbig = [_sb(f"wbg{i}", [128, 4096], BF16) for i in range(2)]
    ones_bf = _sb("ones_bf", [128, 128], BF16)
    ccols = _sb("ccols_sb", [128, len(CONST_VALS)], F32)
    scl = _sb("scl_sb", [128, 144], F32)
    qkvb = _sb("qkvb_sb", [128, 24], F32)
    wlat = _sb("wlat_sb", [128, 192], BF16)
    latb = _sb("latb_sb", [128, 4], F32)
    cpb = _sb("cpb_sb", [128, 8], F32)
    trimask = _sb("tri_sb", [128, 128], F32)
    identf = _sb("identf_sb", [128, 128], F32)
    ps_all = _ps("ps_all", [128, 4096], F32)
    pbanks = [ps_all[:, i * 512:(i + 1) * 512] for i in range(8)]

    xhat = bfa[:, 0:16384]
    bufA = bfa[:, 16384:49152]
    bufB = bfa[:, 49152:81920]
    sq_scr = bufA[:, 0:2048]
    bcast = bufA[:, 2048:6144]
    ab_rows = bufA[:, 6144:10240]
    qfk = bfa[:, 16384:32768]
    krope = bfa[:, 32768:49152]
    vo = bfa[:, 49152:65536]
    sec_stage = bfa[:, 65536:73728]
    lat_s1 = bfa[:, 73728:75776]
    lat_s2 = bfa[:, 75776:77824]
    attT = bfa[:, 65536:67584]
    rope_cs = bfa[:, 77824:81920]
    y_tm = bfa[:, 0:16384]
    y_fm = bfa[:, 16384:32768]
    schunk = fpa[:, 0:2048]
    rowbuf = fpa[:, 2048:3584]
    u_tmp = fpa[:, 3584:4608]
    fptmp = fpa[:, 3584:4608]
    schunk_res = fpa[:, 0:2048]
    pbuf = fpa[:, 0:2048]
    ytmp = fpa[:, 4608:4640]

    _bcm = nc.Block() if fused is None else contextlib.nullcontext(fused[2])
    with _bcm as block:
        if fused is None:
            emit_boot(ctx, block)

        @block.gpsimd
        def _(g):
            g.dma_start(out=ones_bf[:], in_=ones_in).then_inc(ctx.sio.h, 16)
            g.dma_start(out=ccols[:], in_=ccols_in).then_inc(ctx.sio.h, 16)
            g.dma_start(out=scl[:], in_=scl_in).then_inc(ctx.sio.h, 16)
            g.dma_start(out=qkvb[:], in_=qkvb_in).then_inc(ctx.sio.h, 16)
            g.dma_start(out=wlat[:], in_=wlat_in).then_inc(ctx.sio.h, 16)
            g.dma_start(out=latb[:], in_=latb_in).then_inc(ctx.sio.h, 16)
            g.dma_start(out=cpb[:], in_=cpb_in).then_inc(ctx.sio.h, 16)
            g.dma_start(out=trimask[:], in_=tri_in).then_inc(ctx.sio.h, 16)
            g.dma_start(out=identf[:], in_=identf_in).then_inc(ctx.sio.h, 16)
        ctx.sio.v += 144
        pro_io = ctx.sio.v

        @block.vector
        def _(v):
            v.wait_ge(ctx.sio.h, pro_io)
            v.tensor_scalar_mul(scl[:, 128:136], scl[:, 128:136], 0.5)
            ctx.vdr.v += 1
            v.drain().then_inc(ctx.vdr.h, 1)
        fence(ctx, block)

        if do_copy:
            @block.sync
            def _(sy):
                sy.dma_start(out=stream_out, in_=stream_in).then_inc(ctx.sio.h, 16)
            ctx.sio.v += 16
        fence(ctx, block)

        if "ffn1" in parts:
            emit_stream_ln(ctx, block, stream_out, schunk, xhat, sq_scr, rowbuf,
                           ab_rows, bcast, pbanks, ones_bf, ccols)
            fence(ctx, block)
            emit_l0_ffn(ctx, block, xhat, wu, h1, h2, w3, scl, u_tmp,
                        bufA, bufB, wsmall, wbig, pbanks, stream_out, schunk_res)

        if "attn" in parts:
            emit_stream_ln(ctx, block, stream_out, schunk, xhat, sq_scr, rowbuf,
                           ab_rows, bcast, pbanks, ones_bf, ccols)
            fence(ctx, block)

            @block.gpsimd
            def _(g):
                g.dma_start(out=rope_cs[:], in_=rope_in).then_inc(ctx.sio.h, 16)
            ctx.sio.v += 16
            fence(ctx, block)
            emit_l0_attn(ctx, block, xhat, qkvw, qkvb, wlat, latb, rope_cs,
                         trimask, identf, qfk, krope, vo, sec_stage, lat_s1,
                         lat_s2, attT, pbuf, ytmp, fptmp, y_tm, y_fm, ps_all,
                         wsmall, stream_out, schunk_res, cprojw, cpb, latb[:, 3:4])

        if "ffn2" in parts:
            emit_stream_ln(ctx, block, stream_out, schunk, xhat, sq_scr, rowbuf,
                           ab_rows, bcast, pbanks, ones_bf, ccols)
            fence(ctx, block)
            emit_l0_ffn(ctx, block, xhat, wu, h1, h2, w3, scl, u_tmp,
                        bufA, bufB, wsmall, wbig, pbanks, stream_out, schunk_res)

        stv = ctx.st.v
        @block.sync
        def _(sy):
            sy.wait_ge(ctx.st.h, stv)

    les.close()
    if fused is None:
        ctx.es.close()
        nc.gpsimd.sem_clear(ctx.clear_rng)
    return nc


def prep_layer0_inputs(ip):
    out = {}
    w1 = ip["L0_ffn_w1_w"]; w2 = ip["L0_ffn_w2_w"]
    wu = np.empty((1024, 8192), np.float32)
    w1t = np.ascontiguousarray(w1.T); w2t = np.ascontiguousarray(w2.T)
    for i in range(32):
        wu[:, (2 * i) * 128:(2 * i) * 128 + 128] = w1t[:, i * 128:(i + 1) * 128]
        wu[:, (2 * i + 1) * 128:(2 * i + 1) * 128 + 128] = w2t[:, i * 128:(i + 1) * 128]
    out["wu"] = np.ascontiguousarray(wu)
    out["h1"] = np.ascontiguousarray(ip["L0_ffn_h1_w"].T)
    out["h2"] = np.ascontiguousarray(ip["L0_ffn_h2_w"].T)
    out["w3"] = np.ascontiguousarray(ip["L0_ffn_w3_w"].T)
    scl = np.zeros((128, 144), np.float32)
    scl[:, 0:32] = ip["L0_ffn_w1_s"].reshape(32, 128).T
    scl[:, 32:64] = ip["L0_ffn_w2_s"].reshape(32, 128).T
    scl[:, 64:96] = ip["L0_ffn_h1_s"].reshape(32, 128).T
    scl[:, 96:128] = ip["L0_ffn_h2_s"].reshape(32, 128).T
    scl[:, 128:136] = ip["L0_ffn_w3_s"].reshape(8, 128).T
    out["scl"] = np.ascontiguousarray(scl)

    out["qkvw"] = np.ascontiguousarray(ip["L0_attn_c_attn_w"].T)
    out["cprojw"] = np.ascontiguousarray(ip["L0_attn_cproj_w"].T)
    out["qkvb"] = np.ascontiguousarray(ip["L0_attn_c_attn_b"].reshape(24, 128).T)
    wlat = np.zeros((128, 192), np.float32)
    q2lT = ip["L0_attn_q2l_w"].T
    kv2lT = ip["L0_attn_kv2l_w"].T
    wqkT = ip["L0_attn_wqk_w"].T
    l2oT = ip["L0_attn_l2o_w"].T
    P = np.zeros((32, 32), np.float32)
    for i in range(16):
        P[2 * i, 2 * i + 1] = -1.0
        P[2 * i + 1, 2 * i] = 1.0
    PT = P.T
    for b in (0, 64):
        wlat[b:b + 64, 0:32] = q2lT
        wlat[b:b + 64, 32:64] = kv2lT
        wlat[b:b + 32, 64:96] = wqkT
        wlat[b:b + 32, 96:128] = PT
        wlat[b:b + 32, 128:192] = l2oT
    out["wlat"] = wlat
    latb = np.zeros((128, 4), np.float32)
    for b in (0, 64):
        latb[b:b + 32, 0] = ip["L0_attn_q2l_b"]
        latb[b:b + 32, 1] = ip["L0_attn_kv2l_b"]
        latb[b:b + 32, 2] = ip["L0_attn_wqk_b"]
    latb[0:64, 3] = ip["L0_attn_l2o_b"]
    latb[64:128, 3] = ip["L0_attn_l2o_b"]
    out["latb"] = latb
    out["cpb"] = np.ascontiguousarray(ip["L0_attn_cproj_b"].reshape(8, 128).T)
    L = 32
    inv_freq = (1.0 / 10000.0 ** (np.arange(0, L, 2, dtype=np.float32) / L)).astype(np.float32)
    tpos = np.arange(T, dtype=np.float32)
    freqs = np.outer(tpos, inv_freq).astype(np.float32)
    cos = np.repeat(np.cos(freqs), 2, axis=-1)[:, :L].astype(np.float32)
    sin = np.repeat(np.sin(freqs), 2, axis=-1)[:, :L].astype(np.float32)
    rope = np.zeros((128, 2 * T), np.float32)
    for b in (0, 64):
        rope[b:b + 32, 0:T] = cos.T
        rope[b:b + 32, T:2 * T] = sin.T
    out["rope"] = rope
    return out


_NLAYERS = 4
_L3PARTS = ("ffn1", "attn", "ffn2")
_L1PARTS = ("ffn1", "attn", "ffn2")
_L2PARTS = ("ffn1", "attn", "ffn2")


_FUSED = True


def kernel(**inputs):
    if _FUSED:
        return kernel_fused(inputs)
    return _kernel_multi(inputs)


def _kernel_multi(inputs):
    consts = host_constants()
    x = np.ascontiguousarray(inputs["L0_attn_x"])
    streams = [np.ascontiguousarray(x[b].T) for b in range(NCORES)]
    base = {"ones": consts["c_ones"], "ccols": consts["c_cols"],
            "tri": consts["c_trimask"], "identf": consts["c_identf"]}

    l0 = prep_layer0_inputs(inputs)
    nc = build_layer0_program(parts=("ffn1", "attn", "ffn2"))
    in_maps = [dict(base, stream_in=streams[b], **l0) for b in range(NCORES)]
    res = run_bass_kernel_spmd(nc, in_maps, core_ids=list(range(NCORES)))
    streams = [np.ascontiguousarray(res.results[b]["stream_out"]) for b in range(NCORES)]

    if _NLAYERS >= 2:
        l1 = prep_layer1_inputs(inputs)
        l1.update({k: base[k] for k in ("ones", "ccols", "identf", "tri")})
        nc = build_layer1_program(parts=_L1PARTS)
        in_maps = [dict(l1, stream_in=streams[b]) for b in range(NCORES)]
        res = run_bass_kernel_spmd(nc, in_maps, core_ids=list(range(NCORES)))
        streams = [np.ascontiguousarray(res.results[b]["stream_out"]) for b in range(NCORES)]

    if _NLAYERS >= 3:
        l2 = prep_layer2_inputs(inputs)
        l2.update({k: base[k] for k in ("ones", "ccols", "identf", "tri")})
        nc = build_layer2_program(parts=_L2PARTS)
        in_maps = [dict(l2, stream_in=streams[b]) for b in range(NCORES)]
        res = run_bass_kernel_spmd(nc, in_maps, core_ids=list(range(NCORES)))
        streams = [np.ascontiguousarray(res.results[b]["stream_out"]) for b in range(NCORES)]

    if _NLAYERS >= 4:
        l3 = prep_layer3_inputs(inputs)
        l3.update({k: base[k] for k in ("ones", "ccols", "identf", "tri")})
        nc = build_layer3_program(parts=_L3PARTS)
        in_maps = [dict(l3, stream_in=streams[b]) for b in range(NCORES)]
        res = run_bass_kernel_spmd(nc, in_maps, core_ids=list(range(NCORES)))
        streams = [np.ascontiguousarray(res.results[b]["stream_out"]) for b in range(NCORES)]

    out = np.stack([np.ascontiguousarray(s.T) for s in streams], axis=0)
    return out.astype(np.float32)



def emit_linear_fm2(ctx, block, C, F, w_ap, src_fn, evac_fn, wslots, pbanks,
                    Tloc=T, fb_per_dma=2, evac_engine="scalar"):
    ncc = C // 128
    nfb = F // 128
    ntc = (Tloc + 511) // 512
    tcs = min(512, Tloc)
    nslot = len(wslots)
    ngrp = (nfb + fb_per_dma - 1) // fb_per_dma
    wbase, tbase, ebase = ctx.wdma.v, ctx.tdr.v, ctx.evk.v

    @block.gpsimd
    def _(g):
        for gi in range(ngrp):
            f0 = gi * fb_per_dma * 128
            fw = min(fb_per_dma * 128, F - f0)
            slot = wslots[gi % nslot]
            if gi >= nslot:
                g.wait_ge(ctx.tdr.h, tbase + (gi - nslot + 1))
            g.dma_start(out=slot[:, 0:ncc * fw].rearrange("c (k m) -> c k m", m=fw),
                        in_=w_ap[:, f0:f0 + fw].rearrange("(k c) m -> c k m", c=128),
                        ).then_inc(ctx.wdma.h, 16)
    ctx.wdma.v = wbase + 16 * ngrp

    @block.tensor
    def _(t):
        for gi in range(ngrp):
            t.wait_ge(ctx.wdma.h, wbase + 16 * (gi + 1))
            f0 = gi * fb_per_dma * 128
            fw = min(fb_per_dma * 128, F - f0)
            slot = wslots[gi % nslot]
            for fb in range(gi * fb_per_dma, min((gi + 1) * fb_per_dma, nfb)):
                foff = fb * 128 - f0
                pb0 = (fb % 2) * ntc
                if fb >= 2:
                    t.wait_ge(ctx.evk.h, ebase + (fb - 2) + 1)
                for ti in range(ntc):
                    for cc in range(ncc):
                        t.matmul(pbanks[pb0 + ti][:, 0:tcs],
                                 slot[:, cc * fw + foff: cc * fw + foff + 128],
                                 src_fn(cc)[:, ti * tcs:(ti + 1) * tcs],
                                 start=(cc == 0), stop=(cc == ncc - 1))
            ctx.tdr.v += 1
            t.drain().then_inc(ctx.tdr.h, 1)

    def evac_body(e):
        for gi in range(ngrp):
            e.wait_ge(ctx.tdr.h, tbase + gi + 1)
            for fb in range(gi * fb_per_dma, min((gi + 1) * fb_per_dma, nfb)):
                pb0 = (fb % 2) * ntc
                for ti in range(ntc):
                    evac_fn(e, fb, ti, pbanks[pb0 + ti][:, 0:tcs])
                ctx.evk.v += 1
                e.drain().then_inc(ctx.evk.h, 1)
    (block.scalar if evac_engine == "scalar" else block.vector)(evac_body)


def emit_linear_tm(ctx, block, C, F, w_ap, src_fn, evac_fn, wslots, pbanks,
                   Tloc=T, extra_mm=None):
    ncc = C // 128
    nfc = (F + 511) // 512
    ntt = Tloc // 128
    nslot = len(wslots)
    wbase, tbase, ebase = ctx.wdma.v, ctx.tdr.v, ctx.evk.v

    @block.gpsimd
    def _(g):
        for fc in range(nfc):
            f0 = fc * 512
            fw = min(512, F - f0)
            slot = wslots[fc % nslot]
            if fc >= nslot:
                g.wait_ge(ctx.tdr.h, tbase + (fc - nslot) * ntt + ntt)
            g.dma_start(out=slot[:, 0:ncc * fw].rearrange("c (k m) -> c k m", m=fw),
                        in_=w_ap[:, f0:f0 + fw].rearrange("(k c) m -> c k m", c=128),
                        ).then_inc(ctx.wdma.h, 16)
    ctx.wdma.v = wbase + 16 * nfc

    @block.tensor
    def _(t):
        for fc in range(nfc):
            t.wait_ge(ctx.wdma.h, wbase + 16 * (fc + 1))
            f0 = fc * 512
            fw = min(512, F - f0)
            slot = wslots[fc % nslot]
            for tt in range(ntt):
                pb = pbanks[tt % 2]
                if fc > 0 or tt >= 2:
                    ctx_evk_target = ebase + (fc * ntt + tt - 2) + 1
                    t.wait_ge(ctx.evk.h, ctx_evk_target)
                for cc in range(ncc):
                    t.matmul(pb[:, 0:fw],
                             src_fn(cc)[:, tt * 128:(tt + 1) * 128],
                             slot[:, cc * fw: cc * fw + fw],
                             start=(cc == 0), stop=(cc == ncc - 1 and extra_mm is None))
                if extra_mm is not None:
                    extra_mm(t, tt, fc, pb[:, 0:fw])
                ctx.tdr.v += 1
                t.drain().then_inc(ctx.tdr.h, 1)

    @block.vector
    def _(v):
        for fc in range(nfc):
            fw = min(512, F - fc * 512)
            for tt in range(ntt):
                v.wait_ge(ctx.tdr.h, tbase + fc * ntt + tt + 1)
                evac_fn(v, tt, fc, pbanks[tt % 2][:, 0:fw])
                ctx.evk.v += 1
                v.drain().then_inc(ctx.evk.h, 1)



def emit_topk_gating(ctx, block, h_src_fn, gatewT_ap, gateb_row, ones_bf, c_all,
                     cT_bf, stats, wslot, bank, tr_bank, identf, topk, softmax_all):
    wbase = ctx.wdma.v

    @block.gpsimd
    def _(g):
        g.dma_start(out=wslot[:, 0:64].rearrange("c (k m) -> c k m", m=8),
                    in_=gatewT_ap.rearrange("(k c) m -> c k m", c=128),
                    ).then_inc(ctx.wdma.h, 16)
    ctx.wdma.v = wbase + 16

    @block.tensor
    def _(t):
        t.wait_ge(ctx.wdma.h, wbase + 16)
        for tt in range(16):
            for cc in range(8):
                t.matmul(bank[:, tt * 8:(tt + 1) * 8],
                         h_src_fn(cc)[:, tt * 128:(tt + 1) * 128],
                         wslot[:, cc * 8: cc * 8 + 8],
                         start=(cc == 0), stop=(cc == 7 and gateb_row is None))
            if gateb_row is not None:
                t.matmul(bank[:, tt * 8:(tt + 1) * 8], ones_bf[0:1, 0:128],
                         gateb_row[0:1, 0:8], start=False, stop=True)
        ctx.tdr.v += 1
        t.drain().then_inc(ctx.tdr.h, 1)
    logits_rdy = ctx.tdr.v

    @block.vector
    def _(v):
        vb = ctx.vch
        def hs(inst):
            vb.v += 1
            inst.then_inc(vb.h, 1)
            v.wait_ge(vb.h, vb.v)
        v.wait_ge(ctx.tdr.h, logits_rdy)
        for tt in range(16):
            lg = bank[:, tt * 8:(tt + 1) * 8]
            st4 = 4 * tt
            msk = stats[:, 64:72]
            hs(v.reduce_max(stats[:, st4:st4 + 1], lg, axis=AX.X))
            hs(v.tensor_scalar(msk, lg, stats[:, st4:st4 + 1], -1e30, ALU.is_ge, ALU.mult))
            hs(v.tensor_add(msk, msk, lg))
            for k in range(1, topk):
                hs(v.reduce_max(stats[:, st4 + 1:st4 + 2], msk, axis=AX.X))
                if k < topk - 1:
                    hs(v.tensor_scalar(stats[:, 72:80], msk, stats[:, st4 + 1:st4 + 2],
                                       -1e30, ALU.is_ge, ALU.mult))
                    hs(v.tensor_add(msk, msk, stats[:, 72:80]))
        ctx.vdr.v += 1
        v.drain().then_inc(ctx.vdr.h, 1)
    mk_rdy = ctx.vdr.v

    @block.scalar
    def _(s):
        s.wait_ge(ctx.vdr.h, mk_rdy)
        for tt in range(16):
            lg = bank[:, tt * 8:(tt + 1) * 8]
            st4 = 4 * tt
            s.activation(stats[:, st4 + 2:st4 + 3], stats[:, st4:st4 + 1],
                         AF.Identity, scale=-1.0)
        s.drain().then_inc(ctx.sdr.h, 1)
        ctx.sdr.v += 1
        s.wait_ge(ctx.sch.h, ctx.sch.v)
        for tt in range(16):
            lg = bank[:, tt * 8:(tt + 1) * 8]
            st4 = 4 * tt
            s.activation(c_all[:, tt * 8:(tt + 1) * 8], lg, AF.Exp,
                         bias=stats[:, st4 + 2:st4 + 3])
        ctx.sdr.v += 1
        s.drain().then_inc(ctx.sdr.h, 1)
    exp_rdy = ctx.sdr.v

    @block.vector
    def _(v):
        vb = ctx.vch
        def hs(inst):
            vb.v += 1
            inst.then_inc(vb.h, 1)
            v.wait_ge(vb.h, vb.v)
        v.wait_ge(ctx.sdr.h, exp_rdy)
        for tt in range(16):
            lg = bank[:, tt * 8:(tt + 1) * 8]
            st4 = 4 * tt
            ca = c_all[:, tt * 8:(tt + 1) * 8]
            msk = stats[:, 64:72]
            hs(v.tensor_scalar(msk, lg, stats[:, st4 + 1:st4 + 2], None, ALU.is_ge))
            hs(v.tensor_mul(ca, ca, msk))
            hs(v.reduce_sum(stats[:, st4 + 3:st4 + 4], ca, axis=AX.X))
            hs(v.reciprocal(stats[:, st4 + 2:st4 + 3], stats[:, st4 + 3:st4 + 4]))
            hs(v.tensor_scalar(ca, ca, stats[:, st4 + 2:st4 + 3], None, ALU.mult))
        ctx.vdr.v += 1
        v.drain().then_inc(ctx.vdr.h, 1)
    c_rdy = ctx.vdr.v

    @block.tensor
    def _(t):
        t.wait_ge(ctx.vdr.h, c_rdy)
        for tt in range(16):
            t.transpose(tr_bank[:, (tt % 4) * 128:(tt % 4) * 128 + 128][0:8, :],
                        c_all[:, tt * 8:(tt + 1) * 8], identf[:])
            if tt % 4 == 3:
                ctx.tdr.v += 1
                t.drain().then_inc(ctx.tdr.h, 1)
                t.wait_ge(ctx.vdr.h, ctx.vdr.v + (tt // 4) + 1)
    trb = ctx.tdr.v - 4

    @block.vector
    def _(v):
        for grp in range(4):
            v.wait_ge(ctx.tdr.h, trb + grp + 1)
            v.tensor_copy(cT_bf[0:8, grp * 512:(grp + 1) * 512],
                          tr_bank[0:8, 0:512])
            ctx.vdr.v += 1
            v.drain().then_inc(ctx.vdr.h, 1)


def emit_l1_ffn(ctx, block, xhat, h, normw_cols, gatewT_ap, gateb_row, mlp1_ap,
                b1_cols, mlp2_ap, b2row_bf, c_all, cT_bf, stats, t1buf, actbuf,
                acc, wsmall, wbig, pbanks, ps_all, identf, ones_bf, ccols,
                rowbuf, ab_rows, bcast, sq_scr, stream_hbm, schunk_res, fptmp):
    sdr0 = ctx.sdr.v
    tdr0 = ctx.tdr.v
    @block.scalar
    def _(s):
        for k in range(ND):
            if k >= 1:
                s.wait_ge(ctx.tdr.h, tdr0 + k)
            s.activation(sq_scr[:, 0:T], xhat[:, k * T:(k + 1) * T], AF.Square)
            ctx.sdr.v += 1
            s.drain().then_inc(ctx.sdr.h, 1)

    @block.tensor
    def _(t):
        for k in range(ND):
            t.wait_ge(ctx.sdr.h, sdr0 + k + 1)
            for ti in range(NT):
                t.matmul(pbanks[NT + ti][0:1, 0:512], ones_bf[:, 0:1],
                         sq_scr[:, ti * 512: ti * 512 + 512],
                         start=(k == 0), stop=(k == ND - 1))
            ctx.tdr.v += 1
            t.drain().then_inc(ctx.tdr.h, 1)
    sums_rdy = ctx.tdr.v
    _emit_ln_tail(ctx, block, h, rowbuf, ab_rows, bcast, pbanks, ones_bf, ccols,
                  sums_rdy, is_ln=False, xsrc=xhat, gamma_cols=normw_cols)
    fence(ctx, block)

    emit_topk_gating(ctx, block, lambda cc: h[:, cc * T:(cc + 1) * T], gatewT_ap,
                     gateb_row, ones_bf, c_all, cT_bf, stats, wsmall[0],
                     pbanks[0], pbanks[1], identf, topk=2, softmax_all=False)
    fence(ctx, block)

    Tc = 1024
    for tchunk in range(2):
        t0 = tchunk * Tc
        tb0 = ctx.tdr.v
        ev0 = ctx.evk.v

        @block.tensor
        def _(t, tchunk=tchunk, tb0=tb0, ev0=ev0):
            n = 0
            for tt in range(8):
                gt = tchunk * 8 + tt
                for fc in range(2):
                    pb = pbanks[4 + n % 2]
                    if n >= 2:
                        t.wait_ge(ctx.evk.h, ev0 + (n - 2) + 1)
                    t.matmul(pb[:, 0:512],
                             cT_bf[0:8, gt * 128:(gt + 1) * 128],
                             b2row_bf[0:8, fc * 512:(fc + 1) * 512],
                             start=True, stop=True)
                    ctx.tdr.v += 1
                    t.drain().then_inc(ctx.tdr.h, 1)
                    n += 1

        @block.vector
        def _(v, tchunk=tchunk, tb0=tb0):
            n = 0
            for tt in range(8):
                for fc in range(2):
                    pb = pbanks[4 + n % 2]
                    n += 1
                    v.wait_ge(ctx.tdr.h, tb0 + n)
                    v.tensor_copy(acc[:, tt * 1024 + fc * 512: tt * 1024 + fc * 512 + 512],
                                  pb[:, 0:512])
                    ctx.evk.v += 1
                    v.drain().then_inc(ctx.evk.h, 1)
        fence(ctx, block)
        for e in range(8):
            fence(ctx, block, engines=("tensor", "vector", "scalar", "gpsimd"))

            def evac_t1(v, fb, ti, ps, e=e):
                dst = t1buf[:, fb * Tc + ti * 512: fb * Tc + ti * 512 + 512]
                bc = b1_cols[:, e * 16 + fb: e * 16 + fb + 1]
                if fb < 8:
                    v.tensor_scalar(dst, ps, bc, 7.0, ALU.add, ALU.min)
                else:
                    ctx.vch.v += 1
                    v.tensor_scalar(dst, ps, bc, 7.0, ALU.add, ALU.min
                                    ).then_inc(ctx.vch.h, 1)
                    v.wait_ge(ctx.vch.h, ctx.vch.v)
                    v.tensor_scalar(dst, dst, -7.0, None, ALU.max)
            emit_linear_fm2(ctx, block, 1024, 2048,
                            mlp1_ap[e], lambda cc: xh_src(h, cc, t0, Tc),
                            evac_t1, wbig, pbanks[:4], Tloc=Tc, fb_per_dma=2,
                            evac_engine="vector")
            fence(ctx, block, engines=("tensor", "vector", "scalar", "gpsimd"))

            sd0 = ctx.sdr.v
            vd0 = ctx.vdr.v
            @block.scalar
            def _(s, e=e, vd0=vd0):
                for fb in range(8):
                    if fb >= 1:
                        s.wait_ge(ctx.vdr.h, vd0 + fb)
                    s.activation(fptmp[:, 0:Tc], t1buf[:, fb * Tc:(fb + 1) * Tc],
                                 AF.Sigmoid, scale=1.702)
                    ctx.sdr.v += 1
                    s.drain().then_inc(ctx.sdr.h, 1)

            @block.vector
            def _(v, sd0=sd0, e=e):
                for fb in range(8):
                    v.wait_ge(ctx.sdr.h, sd0 + fb + 1)
                    v.tensor_mul(fptmp[:, 0:Tc], fptmp[:, 0:Tc], t1buf[:, fb * Tc:(fb + 1) * Tc])
                    v.scalar_tensor_tensor(actbuf[:, fb * Tc:(fb + 1) * Tc],
                                           t1buf[:, (8 + fb) * Tc:(9 + fb) * Tc], 1.0,
                                           fptmp[:, 0:Tc], ALU.add, ALU.mult)
                    ctx.vdr.v += 1
                    v.drain().then_inc(ctx.vdr.h, 1)
            fence(ctx, block, engines=("tensor", "vector", "scalar"))

            def evac_y(v, tt, fc, ps, e=e, tchunk=tchunk):
                gt = tchunk * 8 + tt
                dst = acc[:, tt * 1024 + fc * 512: tt * 1024 + fc * 512 + 512]
                v.scalar_tensor_tensor(dst, ps, c_all[:, gt * 8 + e: gt * 8 + e + 1],
                                       dst, ALU.mult, ALU.add)
            emit_linear_tm(ctx, block, 1024, 1024, mlp2_ap[e],
                           lambda cc: actbuf[:, cc * Tc:(cc + 1) * Tc],
                           evac_y, wbig, pbanks[4:6], Tloc=Tc)
        fence(ctx, block)

        _emit_l1_residual(ctx, block, xhat, acc, stream_hbm, schunk_res, identf,
                          ps_all, t0, Tc)
        fence(ctx, block)


def _nv(ctx):
    return ctx.vdr.v


def xh_src(h, cc, t0, Tc):
    return h[:, cc * T + t0: cc * T + t0 + Tc]


def _emit_l1_residual(ctx, block, xhat, acc, stream_hbm, schunk_res, identf,
                      ps_all, t0, Tc, xfac=0.5, afac=0.5, acc_is_tm=True):
    tr_bank = ps_all[:, 2048:2560]
    sio_base = ctx.sio.v
    st_base = ctx.st.v
    ntt = Tc // 128

    @block.sync
    def _(sy):
        for fb in range(ND):
            if fb >= 1:
                sy.wait_ge(ctx.st.h, st_base + 16 * fb)
            sy.dma_start(out=schunk_res[:, 0:Tc],
                         in_=stream_hbm[fb * 128:(fb + 1) * 128, t0:t0 + Tc],
                         ).then_inc(ctx.sio.h, 16)
    ctx.sio.v = sio_base + 16 * ND

    for fb in range(ND):
        @block.tensor
        def _(t, fb=fb, evtarget=ctx.evk.v, ev_at=ctx.evk.v):
            if fb > 0:
                t.wait_ge(ctx.evk.h, evtarget)
            for tt in range(ntt):
                t.transpose(tr_bank[:, (tt % 4) * 128:(tt % 4) * 128 + 128],
                            acc[:, tt * 1024 + fb * 128: tt * 1024 + fb * 128 + 128],
                            identf[:])
                if tt % 4 == 3:
                    ctx.tdr.v += 1
                    t.drain().then_inc(ctx.tdr.h, 1)
                    if tt < ntt - 1:
                        t.wait_ge(ctx.evk.h, ev_at + tt // 4 + 1)
        trb = ctx.tdr.v - 2

        @block.vector
        def _(v, fb=fb, trb=trb):
            v.wait_ge(ctx.sio.h, sio_base + 16 * (fb + 1))
            v.scalar_tensor_tensor(schunk_res[:, 0:Tc],
                                   xhat[:, fb * T + t0: fb * T + t0 + Tc], xfac,
                                   schunk_res[:, 0:Tc], ALU.mult, ALU.add)
            for g2 in range(2):
                v.wait_ge(ctx.tdr.h, trb + g2 + 1)
                v.scalar_tensor_tensor(schunk_res[:, g2 * 512:(g2 + 1) * 512],
                                       tr_bank[:, 0:512], afac,
                                       schunk_res[:, g2 * 512:(g2 + 1) * 512],
                                       ALU.mult, ALU.add)
                ctx.evk.v += 1
                v.drain().then_inc(ctx.evk.h, 1)

        @block.scalar
        def _(s, fb=fb, evt=None):
            s.wait_ge(ctx.evk.h, ctx.evk.v if evt is None else evt)
            s.dma_start(out=stream_hbm[fb * 128:(fb + 1) * 128, t0:t0 + Tc],
                        in_=schunk_res[:, 0:Tc]).then_inc(ctx.st.h, 16)
        ctx.st.v += 16


def emit_l1_attn(ctx, block, xhat, qkw_ap, qkb_cols, vw_ap, gains_row_bf,
                 rowrecip, tri01, trimask, identf, q_fm, k_fm, vtm, gains_bc,
                 pbuf, attT, ytmp, fptmp, y_tm, y_fm, ps_all, wsmall, wbig,
                 ones_bf, stream_hbm, schunk_res, cprojw_ap):
    SC = 0.125
    banks = [ps_all[:, i * 512:(i + 1) * 512] for i in range(8)]

    @block.tensor
    def _(t, prev=ctx.tdr.v):
        t.matmul(banks[0][:, 0:48], ones_bf[0:1, 0:128], gains_row_bf[0:1, 0:48],
                 start=True, stop=True)
        ctx.tdr.v += 1
        t.drain().then_inc(ctx.tdr.h, 1)
    gb = ctx.tdr.v

    @block.vector
    def _(v, gb=gb):
        v.wait_ge(ctx.tdr.h, gb)
        v.tensor_copy(gains_bc[:], banks[0][:, 0:48])
        ctx.vdr.v += 1
        v.drain().then_inc(ctx.vdr.h, 1)
    fence(ctx, block)

    def evac_q(s, fb, ti, ps):
        s.activation(q_fm[:, fb * T + ti * 512: fb * T + ti * 512 + 512], ps,
                     AF.Identity, bias=qkb_cols[:, fb:fb + 1])
    emit_linear_fm2(ctx, block, 1024, 1024, qkw_ap[:, 0:1024],
                    lambda cc: xhat[:, cc * T:(cc + 1) * T], evac_q,
                    wsmall, banks, Tloc=T, fb_per_dma=2)
    fence(ctx, block)

    def evac_k(s, fb, ti, ps):
        s.activation(k_fm[:, fb * T + ti * 512: fb * T + ti * 512 + 512], ps,
                     AF.Identity, bias=qkb_cols[:, 8 + fb:9 + fb])
    emit_linear_fm2(ctx, block, 1024, 1024, qkw_ap[:, 1024:2048],
                    lambda cc: xhat[:, cc * T:(cc + 1) * T], evac_k,
                    wsmall, banks, Tloc=T, fb_per_dma=2)
    fence(ctx, block)

    def evac_v(v, tt, fc, ps):
        v.tensor_copy(vtm[:, tt * 1024 + fc * 512: tt * 1024 + fc * 512 + 512], ps)
    emit_linear_tm(ctx, block, 1024, 1024, vw_ap,
                   lambda cc: xhat[:, cc * T:(cc + 1) * T], evac_v,
                   wbig, banks[4:6], Tloc=T)
    fence(ctx, block)

    sc_ps = ps_all[:, 0:2048]
    tr_ps = banks[4]
    y_ps = banks[5][0:64, 0:128]
    prev_yevac = ctx.sdr.v
    for h in range(16):
        for qt in range(16):
            q0 = qt * 128
            klen = q0 + 128
            nkc = (klen + 511) // 512
            nkt = klen // 128
            b = (h % 2) * 64
            qtile = q_fm[:, (h // 2) * T:(h // 2) * T + T]
            ktile = k_fm[:, (h // 2) * T:(h // 2) * T + T]

            @block.tensor
            def _(t, qtile=qtile, ktile=ktile, b=b, q0=q0, klen=klen, nkc=nkc,
                  prev=prev_yevac):
                t.wait_ge(ctx.sdr.h, prev)
                for kc in range(nkc):
                    k0 = kc * 512
                    kw = min(512, klen - k0)
                    t.matmul(sc_ps[:, k0:k0 + kw], qtile[b:b + 64, q0:q0 + 128],
                             ktile[b:b + 64, k0:k0 + kw], start=True, stop=True)
                ctx.tdr.v += 1
                t.drain().then_inc(ctx.tdr.h, 1)
            sc_rdy = ctx.tdr.v

            @block.vector
            def _(v, q0=q0, klen=klen, sc_rdy=sc_rdy):
                vb = ctx.vch
                def hs(inst):
                    vb.v += 1
                    inst.then_inc(vb.h, 1)
                    v.wait_ge(vb.h, vb.v)
                v.wait_ge(ctx.tdr.h, sc_rdy)
                hs(v.tensor_add(sc_ps[:, q0:q0 + 128], sc_ps[:, q0:q0 + 128], trimask[:]))
                hs(v.reduce_max(ytmp[:, 0:1], sc_ps[:, 0:klen], axis=AX.X))
                hs(v.tensor_scalar_mul(ytmp[:, 1:2], ytmp[:, 0:1], -SC))
                ctx.vdr.v += 1
                v.drain().then_inc(ctx.vdr.h, 1)
            mask_rdy = ctx.vdr.v

            @block.scalar
            def _(s, klen=klen, nkc=nkc, mask_rdy=mask_rdy):
                s.wait_ge(ctx.vdr.h, mask_rdy)
                for kc in range(nkc):
                    k0 = kc * 512
                    kw = min(512, klen - k0)
                    s.activation(pbuf[:, k0:k0 + kw], sc_ps[:, k0:k0 + kw], AF.Exp,
                                 bias=ytmp[:, 1:2], scale=SC,
                                 accum_out=ytmp[:, 2 + kc:3 + kc])
                ctx.sdr.v += 1
                s.drain().then_inc(ctx.sdr.h, 1)
            exp_rdy = ctx.sdr.v

            @block.vector
            def _(v, h=h, qt=qt, q0=q0, klen=klen, nkc=nkc, exp_rdy=exp_rdy):
                vb = ctx.vch
                def hs(inst):
                    vb.v += 1
                    inst.then_inc(vb.h, 1)
                    v.wait_ge(vb.h, vb.v)
                v.wait_ge(ctx.sdr.h, exp_rdy)
                hs(v.reduce_sum(ytmp[:, 8:9], ytmp[:, 2:2 + nkc], axis=AX.X))
                hs(v.reciprocal(ytmp[:, 9:10], ytmp[:, 8:9]))
                hs(v.tensor_mul(ytmp[:, 10:11], gains_bc[:, h:h + 1], ytmp[:, 9:10]))
                for kc in range(nkc):
                    k0 = kc * 512
                    kw = min(512, klen - k0)
                    hs(v.tensor_scalar(pbuf[:, k0:k0 + kw], pbuf[:, k0:k0 + kw],
                                       ytmp[:, 10:11], None, ALU.mult))
                hs(v.scalar_tensor_tensor(pbuf[:, q0:q0 + 128], identf[:],
                                          gains_bc[:, 16 + h:17 + h],
                                          pbuf[:, q0:q0 + 128], ALU.mult, ALU.add))
                hs(v.tensor_mul(ytmp[:, 11:12], gains_bc[:, 32 + h:33 + h],
                                rowrecip[:, qt:qt + 1]))
                hs(v.tensor_scalar_mul(ytmp[:, 12:13], ytmp[:, 11:12], -1.0))
                if q0 > 0:
                    hs(v.tensor_scalar(pbuf[:, 0:q0], pbuf[:, 0:q0],
                                       ytmp[:, 12:13], None, ALU.add))
                hs(v.scalar_tensor_tensor(pbuf[:, q0:q0 + 128], tri01[:],
                                          ytmp[:, 12:13], pbuf[:, q0:q0 + 128],
                                          ALU.mult, ALU.add))
                ctx.vdr.v += 1
                v.drain().then_inc(ctx.vdr.h, 1)
            mods_rdy = ctx.vdr.v

            ngrp = (nkt + 3) // 4
            for grp in range(ngrp):
                n = min(4, nkt - grp * 4)

                @block.tensor
                def _(t, grp=grp, n=n, mods_rdy=mods_rdy, evprev=ctx.vdr.v):
                    t.wait_ge(ctx.vdr.h, mods_rdy)
                    if grp > 0:
                        t.wait_ge(ctx.vdr.h, evprev)
                    for j in range(n):
                        t.transpose(tr_ps[:, j * 128:(j + 1) * 128],
                                    pbuf[:, (grp * 4 + j) * 128:(grp * 4 + j + 1) * 128],
                                    identf[:])
                    ctx.tdr.v += 1
                    t.drain().then_inc(ctx.tdr.h, 1)
                trd = ctx.tdr.v

                @block.vector
                def _(v, grp=grp, n=n, trd=trd):
                    v.wait_ge(ctx.tdr.h, trd)
                    v.tensor_copy(attT[:, grp * 4 * 128: grp * 4 * 128 + n * 128],
                                  tr_ps[:, 0:n * 128])
                    ctx.vdr.v += 1
                    v.drain().then_inc(ctx.vdr.h, 1)
            attT_rdy = ctx.vdr.v

            @block.tensor
            def _(t, h=h, nkt=nkt, attT_rdy=attT_rdy):
                t.wait_ge(ctx.vdr.h, attT_rdy)
                for kt in range(nkt):
                    t.matmul(y_ps, vtm[:, kt * 1024 + h * 64: kt * 1024 + h * 64 + 64],
                             attT[:, kt * 128:(kt + 1) * 128],
                             start=(kt == 0), stop=(kt == nkt - 1))
                ctx.tdr.v += 1
                t.drain().then_inc(ctx.tdr.h, 1)
            pv_rdy = ctx.tdr.v

            @block.scalar
            def _(s, h=h, qt=qt, q0=q0, pv_rdy=pv_rdy):
                s.wait_ge(ctx.tdr.h, pv_rdy)
                b2 = (h % 2) * 64
                fc = h // 2
                s.activation(y_fm[b2:b2 + 64, fc * T + q0: fc * T + q0 + 128],
                             y_ps, AF.Copy)
                ctx.sdr.v += 1
                s.drain().then_inc(ctx.sdr.h, 1)
            prev_yevac = ctx.sdr.v

    fence(ctx, block)
    _emit_out_residual(ctx, block, cprojw_ap, y_fm, wsmall, banks,
                       stream_hbm, schunk_res, C=1024, bias_cols=None,
                       scale_cols=None, t0=0, Tc=T)
    fence(ctx, block)


def build_layer1_program(parts=("ffn1", "attn", "ffn2"), fused=None, stream_io=None, prefix="", do_copy=True):
    if fused is None:
        nc = bass.Bass("TRN2", target_bir_lowering=False)
    else:
        nc = fused[0]
    if stream_io is None:
        stream_in = nc.dram_tensor(prefix + "stream_in", [D, T], F32, kind="ExternalInput").ap()
        stream_out = nc.dram_tensor(prefix + "stream_out", [D, T], F32, kind="ExternalOutput").ap()
    else:
        stream_in, stream_out = stream_io
    gatewT = nc.dram_tensor(prefix + "gatewT", [1024, 8], F32, kind="ExternalInput").ap()
    gateb = nc.dram_tensor(prefix + "gateb", [1, 8], F32, kind="ExternalInput").ap()
    mlp1 = nc.dram_tensor(prefix + "mlp1", [8, 1024, 2048], F32, kind="ExternalInput")
    mlp2 = nc.dram_tensor(prefix + "mlp2", [8, 1024, 1024], F32, kind="ExternalInput")
    b1c_in = nc.dram_tensor(prefix + "b1c", [128, 128], F32, kind="ExternalInput").ap()
    b2r_in = nc.dram_tensor(prefix + "b2r", [8, 1024], F32, kind="ExternalInput").ap()
    normw_in = nc.dram_tensor(prefix + "normw", [128, 8], F32, kind="ExternalInput").ap()
    qkw = nc.dram_tensor(prefix + "qkw", [1024, 2048], F32, kind="ExternalInput").ap()
    qkb_in = nc.dram_tensor(prefix + "qkb", [128, 16], F32, kind="ExternalInput").ap()
    vw = nc.dram_tensor(prefix + "vw", [1024, 1024], F32, kind="ExternalInput").ap()
    cprojw = nc.dram_tensor(prefix + "cprojw", [1024, 1024], F32, kind="ExternalInput").ap()
    gains_in = nc.dram_tensor(prefix + "gains", [1, 48], F32, kind="ExternalInput").ap()
    rr_in = nc.dram_tensor(prefix + "rowrecip", [128, 16], F32, kind="ExternalInput").ap()
    tri_in = nc.dram_tensor(prefix + "tri", [128, 128], F32, kind="ExternalInput").ap()
    tri01_in = nc.dram_tensor(prefix + "tri01", [128, 128], F32, kind="ExternalInput").ap()
    identf_in = nc.dram_tensor(prefix + "identf", [128, 128], F32, kind="ExternalInput").ap()
    ones_in = nc.dram_tensor(prefix + "ones", [128, 128], BF16, kind="ExternalInput").ap()
    ccols_in = nc.dram_tensor(prefix + "ccols", [128, len(CONST_VALS)], F32, kind="ExternalInput").ap()

    if fused is None:
        ctx = Ctx(nc)
    else:
        ctx = fused[1]
    les = contextlib.ExitStack()
    def _sb(n, sh, dt):
        return les.enter_context(nc.sbuf_tensor(prefix + n, sh, dt))
    def _ps(n, sh, dt=F32):
        return les.enter_context(nc.psum_tensor(prefix + n, sh, dt))
    bfa = _sb("bfa", [128, 68608], BF16)
    fpa = _sb("fpa", [128, 13104], F32)
    wbig = [_sb(f"wbg{i}", [128, 4096], BF16) for i in range(2)]
    ones_bf = _sb("ones_bf", [128, 128], BF16)
    ccols = _sb("ccols_sb", [128, len(CONST_VALS)], F32)
    identf = _sb("identf_sb", [128, 128], F32)
    trimask = _sb("tri_sb", [128, 128], F32)
    tri01 = _sb("tri01_sb", [128, 128], F32)
    rowrecip = _sb("rr_sb", [128, 16], F32)
    normw = _sb("normw_sb", [128, 8], F32)
    qkb = _sb("qkb_sb", [128, 16], F32)
    b1c = _sb("b1c_sb", [128, 128], F32)
    gateb_row = _sb("gateb_sb", [1, 8], BF16)
    gains_row = _sb("gains_sb", [1, 48], BF16)
    ps_all = _ps("ps_all", [128, 4096], F32)
    pbanks = [ps_all[:, i * 512:(i + 1) * 512] for i in range(8)]

    xhat = bfa[:, 0:16384]
    h = bfa[:, 16384:32768]
    sq_scr = bfa[:, 32768:34816]
    bcast = bfa[:, 34816:38912]
    ab_rows = bfa[:, 38912:43008]
    t1buf = bfa[:, 32768:49152]
    actbuf = bfa[:, 49152:57344]
    cT_bf = bfa[:, 57344:59392]
    b2row_bf = bfa[:, 67584:68608]
    q_fm = bfa[:, 16384:32768]
    k_fm = bfa[:, 32768:49152]
    vtm = bfa[:, 49152:65536]
    y_fm = bfa[:, 0:16384]
    attT = bfa[:, 65536:67584]
    schunk = fpa[:, 0:2048]
    schunk_res = fpa[:, 0:2048]
    rowbuf = fpa[:, 2048:3584]
    fptmp = fpa[:, 3584:4608]
    ytmp = fpa[:, 4608:4640]
    acc = fpa[:, 4640:12832]
    pbuf = acc[:, 0:2048]
    c_all = fpa[:, 12832:12960]
    stats = fpa[:, 12960:13040]
    gains_bc = fpa[:, 13040:13088]

    _bcm = nc.Block() if fused is None else contextlib.nullcontext(fused[2])
    with _bcm as block:
        if fused is None:
            emit_boot(ctx, block)

        @block.gpsimd
        def _(g):
            g.dma_start(out=ones_bf[:], in_=ones_in).then_inc(ctx.sio.h, 16)
            g.dma_start(out=ccols[:], in_=ccols_in).then_inc(ctx.sio.h, 16)
            g.dma_start(out=identf[:], in_=identf_in).then_inc(ctx.sio.h, 16)
            g.dma_start(out=trimask[:], in_=tri_in).then_inc(ctx.sio.h, 16)
            g.dma_start(out=tri01[:], in_=tri01_in).then_inc(ctx.sio.h, 16)
            g.dma_start(out=rowrecip[:], in_=rr_in).then_inc(ctx.sio.h, 16)
            g.dma_start(out=normw[:], in_=normw_in).then_inc(ctx.sio.h, 16)
            g.dma_start(out=qkb[:], in_=qkb_in).then_inc(ctx.sio.h, 16)
            g.dma_start(out=b1c[:], in_=b1c_in).then_inc(ctx.sio.h, 16)
            g.dma_start(out=gateb_row[:], in_=gateb).then_inc(ctx.sio.h, 16)
            g.dma_start(out=gains_row[:], in_=gains_in).then_inc(ctx.sio.h, 16)
            g.dma_start(out=b2row_bf[0:8, :], in_=b2r_in).then_inc(ctx.sio.h, 16)
        ctx.sio.v += 192
        pro_io = ctx.sio.v
        for en in ("tensor", "vector", "scalar", "sync"):
            getattr(block, en)(lambda e: e.wait_ge(ctx.sio.h, pro_io))

        if do_copy:
            @block.sync
            def _(sy):
                sy.dma_start(out=stream_out, in_=stream_in).then_inc(ctx.sio.h, 16)
            ctx.sio.v += 16
        fence(ctx, block)

        def do_ffn():
            emit_stream_ln(ctx, block, stream_out, schunk, xhat, sq_scr, rowbuf,
                           ab_rows, bcast, pbanks, ones_bf, ccols)
            fence(ctx, block)
            emit_l1_ffn(ctx, block, xhat, h, normw, gatewT, gateb_row,
                        [mlp1.ap()[e] for e in range(8)], b1c,
                        [mlp2.ap()[e] for e in range(8)], b2row_bf, c_all, cT_bf,
                        stats, t1buf, actbuf, acc, wbig, wbig, pbanks, ps_all,
                        identf, ones_bf, ccols, rowbuf, ab_rows, bcast, sq_scr,
                        stream_out, schunk_res, fptmp)
            fence(ctx, block)

        if "ffn1" in parts:
            do_ffn()
        if "attn" in parts:
            emit_stream_ln(ctx, block, stream_out, schunk, xhat, sq_scr, rowbuf,
                           ab_rows, bcast, pbanks, ones_bf, ccols)
            fence(ctx, block)
            emit_l1_attn(ctx, block, xhat, qkw, qkb, vw, gains_row, rowrecip, tri01,
                         trimask, identf, q_fm, k_fm, vtm, gains_bc, pbuf, attT,
                         ytmp, fptmp, None, y_fm, ps_all, wbig, wbig, ones_bf,
                         stream_out, schunk_res, cprojw)
            fence(ctx, block)
        if "ffn2" in parts:
            do_ffn()

        stv = ctx.st.v
        @block.sync
        def _(sy):
            sy.wait_ge(ctx.st.h, stv)

    les.close()
    if fused is None:
        ctx.es.close()
        nc.gpsimd.sem_clear(ctx.clear_rng)
    return nc


def prep_layer1_inputs(ip):
    out = {}
    out["gatewT"] = np.ascontiguousarray(ip["L1_ffn_gate_w"].T)
    out["gateb"] = np.ascontiguousarray(ip["L1_ffn_gate_b"].reshape(1, 8))
    m1 = ip["L1_ffn_mlp1_w"]
    m1d = np.empty((8, 1024, 2048), np.float32)
    b1 = ip["L1_ffn_mlp1_b"]
    b1c = np.zeros((128, 128), np.float32)
    for e in range(8):
        de = np.concatenate([m1[e, 0::2, :], m1[e, 1::2, :]], axis=0)
        m1d[e] = de.T
        bde = np.concatenate([b1[e, 0::2], b1[e, 1::2]])
        b1c[:, e * 16:(e + 1) * 16] = bde.reshape(16, 128).T
    out["mlp1"] = np.ascontiguousarray(m1d)
    out["b1c"] = b1c
    m2 = ip["L1_ffn_mlp2_w"]
    m2t = np.stack([np.ascontiguousarray(m2[e].T) for e in range(8)], axis=0)
    out["mlp2"] = np.ascontiguousarray(m2t)
    out["b2r"] = np.ascontiguousarray(ip["L1_ffn_mlp2_b"])
    out["normw"] = np.ascontiguousarray(ip["L1_ffn_norm_w"].reshape(8, 128).T)
    out["qkw"] = np.ascontiguousarray(ip["L1_attn_qk_w"])
    out["qkb"] = np.ascontiguousarray(ip["L1_attn_qk_b"].reshape(16, 128).T)
    out["vw"] = np.ascontiguousarray(ip["L1_attn_v_w"])
    out["cprojw"] = np.ascontiguousarray(ip["L1_attn_cproj_w"])
    gains = np.concatenate([ip["L1_attn_resid_gain"].reshape(16),
                            ip["L1_attn_skip_gain"].reshape(16),
                            ip["L1_attn_centre_gain"].reshape(16)]).reshape(1, 48)
    out["gains"] = np.ascontiguousarray(gains.astype(np.float32))
    rr = np.zeros((128, 16), np.float32)
    for qt in range(16):
        rr[:, qt] = 1.0 / (qt * 128 + np.arange(128) + 1.0)
    out["rowrecip"] = rr
    i = np.arange(128)
    out["tri01"] = (i[None, :] <= i[:, None]).astype(np.float32)
    return out



def emit_swiglu_pair(ctx, block, w_ap, b1col, b2col, src_fn, out_fn, utmp,
                     wslots, pbanks, Tc, F2):
    ncc = 8
    nfb = 2 * F2 // 128
    ntc = Tc // 512
    nslot = len(wslots)
    ngrp = nfb // 2
    wbase, tbase, ebase, sbase = ctx.wdma.v, ctx.tdr.v, ctx.evk.v, ctx.sdr.v

    @block.gpsimd
    def _(g):
        for gi in range(ngrp):
            f0 = gi * 256
            slot = wslots[gi % nslot]
            if gi >= nslot:
                g.wait_ge(ctx.tdr.h, tbase + (gi - nslot + 1))
            g.dma_start(out=slot[:, 0:ncc * 256].rearrange("c (k m) -> c k m", m=256),
                        in_=w_ap[:, f0:f0 + 256].rearrange("(k c) m -> c k m", c=128),
                        ).then_inc(ctx.wdma.h, 16)
    ctx.wdma.v = wbase + 16 * ngrp

    @block.tensor
    def _(t):
        for gi in range(ngrp):
            t.wait_ge(ctx.wdma.h, wbase + 16 * (gi + 1))
            slot = wslots[gi % nslot]
            if gi >= 1:
                t.wait_ge(ctx.evk.h, ebase + gi)
            for half in range(2):
                foff = half * 128
                for ti in range(ntc):
                    psb = pbanks[half * ntc + ti]
                    for cc in range(ncc):
                        t.matmul(psb[:, 0:512],
                                 slot[:, cc * 256 + foff: cc * 256 + foff + 128],
                                 src_fn(cc)[:, ti * 512:(ti + 1) * 512],
                                 start=(cc == 0), stop=(cc == ncc - 1))
            ctx.tdr.v += 1
            t.drain().then_inc(ctx.tdr.h, 1)

    @block.scalar
    def _(s):
        for gi in range(ngrp):
            s.wait_ge(ctx.tdr.h, tbase + gi + 1)
            for ti in range(ntc):
                if b1col is not None:
                    s.activation(utmp[:, ti * 512:(ti + 1) * 512], pbanks[ti][:, 0:512],
                                 AF.Silu, bias=b1col[:, gi:gi + 1])
                else:
                    s.activation(utmp[:, ti * 512:(ti + 1) * 512], pbanks[ti][:, 0:512],
                                 AF.Silu)
            ctx.sdr.v += 1
            s.drain().then_inc(ctx.sdr.h, 1)

    @block.vector
    def _(v):
        for gi in range(ngrp):
            v.wait_ge(ctx.sdr.h, sbase + gi + 1)
            for ti in range(ntc):
                dst = out_fn(gi, ti)
                if b2col is not None:
                    v.scalar_tensor_tensor(dst, pbanks[ntc + ti][:, 0:512],
                                           b2col[:, gi:gi + 1],
                                           utmp[:, ti * 512:(ti + 1) * 512],
                                           ALU.add, ALU.mult)
                else:
                    v.tensor_mul(dst, pbanks[ntc + ti][:, 0:512],
                                 utmp[:, ti * 512:(ti + 1) * 512])
            ctx.evk.v += 1
            v.drain().then_inc(ctx.evk.h, 1)


def emit_plain_residual(ctx, block, src_fm, stream_hbm, schunk_res, factor=1.0,
                        bias_cols=None):
    sio_base = ctx.sio.v
    st_base = ctx.st.v
    ev0 = ctx.evk.v

    @block.sync
    def _(sy):
        for fb in range(ND):
            if fb >= 1:
                sy.wait_ge(ctx.st.h, st_base + 16 * fb)
            sy.dma_start(out=schunk_res[:, 0:T],
                         in_=stream_hbm[fb * 128:(fb + 1) * 128, :],
                         ).then_inc(ctx.sio.h, 16)
    ctx.sio.v = sio_base + 16 * ND

    @block.vector
    def _(v):
        for fb in range(ND):
            v.wait_ge(ctx.sio.h, sio_base + 16 * (fb + 1))
            v.scalar_tensor_tensor(schunk_res[:, 0:T], src_fm[:, fb * T:(fb + 1) * T],
                                   factor, schunk_res[:, 0:T], ALU.mult, ALU.add)
            if bias_cols is not None:
                ctx.vch.v += 1
                v.nop().then_inc(ctx.vch.h, 1)
                v.wait_ge(ctx.vch.h, ctx.vch.v)
                v.tensor_scalar_add(schunk_res[:, 0:T], schunk_res[:, 0:T],
                                    bias_cols[:, fb:fb + 1])
            ctx.evk.v += 1
            v.drain().then_inc(ctx.evk.h, 1)

    @block.scalar
    def _(s):
        for fb in range(ND):
            s.wait_ge(ctx.evk.h, ev0 + fb + 1)
            s.dma_start(out=stream_hbm[fb * 128:(fb + 1) * 128, :],
                        in_=schunk_res[:, 0:T]).then_inc(ctx.st.h, 16)
    ctx.st.v = st_base + 16 * ND


_L2STAGES = 99


def emit_l2_ffn(ctx, block, xhat, h0, h1, actbuf, shh, moe_fm, acc,
                lin0_ap, lin0b, swi1_ap, swi1b1, swi1b2, gatewT_ap,
                expw13, expw2, sh13_ap, sh2_ap, lin1_ap, lin1b,
                swi2_ap, swi2b1, swi2b2, c_all, cT_bf, stats,
                wbig, pbanks, ps_all, identf, ones_bf, ccols, utmp,
                stream_hbm, schunk_res):
    banks = pbanks
    def evac_lin0(s, fb, ti, ps):
        s.activation(h0[:, fb * T + ti * 512: fb * T + ti * 512 + 512], ps,
                     AF.Identity, bias=lin0b[:, fb:fb + 1])
    emit_linear_fm2(ctx, block, 1024, 1024, lin0_ap,
                    lambda cc: xhat[:, cc * T:(cc + 1) * T], evac_lin0,
                    wbig, banks, Tloc=T, fb_per_dma=2)
    fence(ctx, block)
    if _L2STAGES < 2:
        return
    for tch in range(2):
        t0 = tch * 1024
        emit_swiglu_pair(ctx, block, swi1_ap, swi1b1, swi1b2,
                         lambda cc, t0=t0: h0[:, cc * T + t0: cc * T + t0 + 1024],
                         lambda gi, ti, t0=t0: h1[:, gi * T + t0 + ti * 512:
                                                  gi * T + t0 + ti * 512 + 512],
                         utmp, wbig, banks, 1024, 1024)
        fence(ctx, block, engines=("tensor", "vector", "scalar", "gpsimd"))
    fence(ctx, block)

    if _L2STAGES < 3:
        return
    emit_topk_gating(ctx, block, lambda cc: h1[:, cc * T:(cc + 1) * T], gatewT_ap,
                     None, ones_bf, c_all, cT_bf, stats, wbig[0],
                     banks[0], banks[1], identf, topk=4, softmax_all=True)
    fence(ctx, block)

    if _L2STAGES < 4:
        return
    Tc = 1024
    for tchunk in range(2):
        t0 = tchunk * Tc
        @block.vector
        def _(v):
            v.memset(acc[:, 0:8192], 0.0)
            ctx.vdr.v += 1
            v.drain().then_inc(ctx.vdr.h, 1)
        fence(ctx, block, engines=("tensor", "vector", "scalar"))

        for e in range(8):
            emit_swiglu_pair(ctx, block, expw13[e], None, None,
                             lambda cc, t0=t0: h1[:, cc * T + t0: cc * T + t0 + Tc],
                             lambda gi, ti: actbuf[:, gi * Tc + ti * 512:
                                                   gi * Tc + ti * 512 + 512],
                             utmp, wbig, banks, Tc, 1024)
            fence(ctx, block, engines=("tensor", "vector", "scalar", "gpsimd"))
            def evac_y(v, tt, fc, ps, e=e, tchunk=tchunk):
                gt = tchunk * 8 + tt
                dst = acc[:, tt * 1024 + fc * 512: tt * 1024 + fc * 512 + 512]
                v.scalar_tensor_tensor(dst, ps, c_all[:, gt * 8 + e: gt * 8 + e + 1],
                                       dst, ALU.mult, ALU.add)
            emit_linear_tm(ctx, block, 1024, 1024, expw2[e],
                           lambda cc: actbuf[:, cc * Tc:(cc + 1) * Tc],
                           evac_y, wbig, banks[4:6], Tloc=Tc)
            fence(ctx, block, engines=("tensor", "vector", "scalar", "gpsimd"))

        if _L2STAGES < 5:
            continue
        emit_swiglu_pair(ctx, block, sh13_ap, None, None,
                         lambda cc, t0=t0: h1[:, cc * T + t0: cc * T + t0 + Tc],
                         lambda gi, ti: shh[:, gi * Tc + ti * 512: gi * Tc + ti * 512 + 512],
                         utmp, wbig, banks, Tc, 2048)
        fence(ctx, block, engines=("tensor", "vector", "scalar", "gpsimd"))

        def evac_sh(v, tt, fc, ps):
            dst = acc[:, tt * 1024 + fc * 512: tt * 1024 + fc * 512 + 512]
            v.scalar_tensor_tensor(dst, ps, 1.0, dst, ALU.mult, ALU.add)
        for chalf in range(2):
            emit_linear_tm(ctx, block, 1024, 1024, sh2_ap[chalf * 1024:(chalf + 1) * 1024],
                           lambda cc, chalf=chalf: shh[:, (chalf * 8 + cc) * Tc:
                                                       (chalf * 8 + cc + 1) * Tc],
                           evac_sh, wbig, banks[4:6], Tloc=Tc)
            fence(ctx, block, engines=("tensor", "vector", "scalar", "gpsimd"))
        fence(ctx, block)

        if _L2STAGES < 6:
            continue
        tr_bank = ps_all[:, 2048:2560]
        for fb in range(ND):
            @block.tensor
            def _(t, fb=fb, ev_at=ctx.vdr.v):
                if fb > 0:
                    t.wait_ge(ctx.vdr.h, ev_at)
                for tt in range(8):
                    t.transpose(tr_bank[:, (tt % 4) * 128:(tt % 4) * 128 + 128],
                                acc[:, tt * 1024 + fb * 128: tt * 1024 + fb * 128 + 128],
                                identf[:])
                    if tt % 4 == 3:
                        ctx.tdr.v += 1
                        t.drain().then_inc(ctx.tdr.h, 1)
                        if tt < 7:
                            t.wait_ge(ctx.vdr.h, ev_at + tt // 4 + 1)
            trb = ctx.tdr.v - 2

            @block.vector
            def _(v, fb=fb, trb=trb, t0=t0):
                for g2 in range(2):
                    v.wait_ge(ctx.tdr.h, trb + g2 + 1)
                    v.tensor_copy(moe_fm[:, fb * T + t0 + g2 * 512:
                                         fb * T + t0 + g2 * 512 + 512],
                                  tr_bank[:, 0:512])
                    ctx.vdr.v += 1
                    v.drain().then_inc(ctx.vdr.h, 1)
        fence(ctx, block)

    if _L2STAGES < 7:
        return
    def evac_lin1(s, fb, ti, ps):
        s.activation(h0[:, fb * T + ti * 512: fb * T + ti * 512 + 512], ps,
                     AF.Identity, bias=lin1b[:, fb:fb + 1])
    emit_linear_fm2(ctx, block, 1024, 1024, lin1_ap,
                    lambda cc: moe_fm[:, cc * T:(cc + 1) * T], evac_lin1,
                    wbig, banks, Tloc=T, fb_per_dma=2)
    fence(ctx, block)
    for tch in range(2):
        t0 = tch * 1024
        emit_swiglu_pair(ctx, block, swi2_ap, swi2b1, swi2b2,
                         lambda cc, t0=t0: h0[:, cc * T + t0: cc * T + t0 + 1024],
                         lambda gi, ti, t0=t0: h1[:, gi * T + t0 + ti * 512:
                                                  gi * T + t0 + ti * 512 + 512],
                         utmp, wbig, banks, 1024, 1024)
        fence(ctx, block, engines=("tensor", "vector", "scalar", "gpsimd"))
    fence(ctx, block)
    emit_plain_residual(ctx, block, h1, stream_hbm, schunk_res, factor=0.5)
    fence(ctx, block)


def emit_l2_attn(ctx, block, xhat, kqvw_ap, kqvb_cols, vbias_row, slopes_row_bf,
                 mtab, trimask, identf, q_fm, k_fm, vtm, slope_bc, pbuf, attT,
                 ytmp, y_fm, ps_all, wbig, ones_bf, stream_hbm, schunk_res,
                 projw_ap, projb_cols):
    SC = 0.125
    banks = [ps_all[:, i * 512:(i + 1) * 512] for i in range(8)]

    @block.tensor
    def _(t):
        t.matmul(banks[0][:, 0:16], ones_bf[0:1, 0:128], slopes_row_bf[0:1, 0:16],
                 start=True, stop=True)
        ctx.tdr.v += 1
        t.drain().then_inc(ctx.tdr.h, 1)
    gb = ctx.tdr.v

    @block.vector
    def _(v, gb=gb):
        v.wait_ge(ctx.tdr.h, gb)
        v.tensor_scalar(slope_bc[:], banks[0][:, 0:16], 8.0, None, ALU.mult)
        ctx.vdr.v += 1
        v.drain().then_inc(ctx.vdr.h, 1)
    fence(ctx, block)

    def evac_k(s, fb, ti, ps):
        s.activation(k_fm[:, fb * T + ti * 512: fb * T + ti * 512 + 512], ps,
                     AF.Identity, bias=kqvb_cols[:, fb:fb + 1])
    emit_linear_fm2(ctx, block, 1024, 1024, kqvw_ap[:, 0:1024],
                    lambda cc: xhat[:, cc * T:(cc + 1) * T], evac_k,
                    wbig, banks, Tloc=T, fb_per_dma=2)
    fence(ctx, block)

    def evac_q(s, fb, ti, ps):
        s.activation(q_fm[:, fb * T + ti * 512: fb * T + ti * 512 + 512], ps,
                     AF.Identity, bias=kqvb_cols[:, 8 + fb:9 + fb])
    emit_linear_fm2(ctx, block, 1024, 1024, kqvw_ap[:, 1024:2048],
                    lambda cc: xhat[:, cc * T:(cc + 1) * T], evac_q,
                    wbig, banks, Tloc=T, fb_per_dma=2)
    fence(ctx, block)

    def extra_vb(t, tt, fc, ps):
        t.matmul(ps, ones_bf[0:1, 0:128], vbias_row[0:1, fc * 512:(fc + 1) * 512],
                 start=False, stop=True)

    def evac_v2(v, tt, fc, ps):
        v.tensor_copy(vtm[:, tt * 1024 + fc * 512: tt * 1024 + fc * 512 + 512], ps)
    emit_linear_tm(ctx, block, 1024, 1024, kqvw_ap[:, 2048:3072],
                   lambda cc: xhat[:, cc * T:(cc + 1) * T], evac_v2,
                   wbig, banks[4:6], Tloc=T, extra_mm=extra_vb)
    fence(ctx, block)

    sc_ps = ps_all[:, 0:2048]
    tr_ps = banks[4]
    y_ps = banks[5][0:64, 0:128]
    prev_yevac = ctx.sdr.v
    for h in range(16):
        for qt in range(16):
            q0 = qt * 128
            klen = q0 + 128
            nkc = (klen + 511) // 512
            nkt = klen // 128
            b = (h % 2) * 64
            qtile = q_fm[:, (h // 2) * T:(h // 2) * T + T]
            ktile = k_fm[:, (h // 2) * T:(h // 2) * T + T]

            @block.tensor
            def _(t, qtile=qtile, ktile=ktile, b=b, q0=q0, klen=klen, nkc=nkc,
                  prev=prev_yevac):
                t.wait_ge(ctx.sdr.h, prev)
                for kc in range(nkc):
                    k0 = kc * 512
                    kw = min(512, klen - k0)
                    t.matmul(sc_ps[:, k0:k0 + kw], qtile[b:b + 64, q0:q0 + 128],
                             ktile[b:b + 64, k0:k0 + kw], start=True, stop=True)
                ctx.tdr.v += 1
                t.drain().then_inc(ctx.tdr.h, 1)
            sc_rdy = ctx.tdr.v

            @block.vector
            def _(v, h=h, q0=q0, klen=klen, nkc=nkc, sc_rdy=sc_rdy):
                vb = ctx.vch
                def hs(inst):
                    vb.v += 1
                    inst.then_inc(vb.h, 1)
                    v.wait_ge(vb.h, vb.v)
                v.wait_ge(ctx.tdr.h, sc_rdy)
                for kc in range(nkc):
                    k0 = kc * 512
                    kw = min(512, klen - k0)
                    hs(v.scalar_tensor_tensor(sc_ps[:, k0:k0 + kw],
                                              mtab[:, 1920 - q0 + k0: 1920 - q0 + k0 + kw],
                                              slope_bc[:, h:h + 1],
                                              sc_ps[:, k0:k0 + kw], ALU.mult, ALU.add))
                hs(v.tensor_add(sc_ps[:, q0:q0 + 128], sc_ps[:, q0:q0 + 128], trimask[:]))
                hs(v.reduce_max(ytmp[:, 0:1], sc_ps[:, 0:klen], axis=AX.X))
                hs(v.tensor_scalar_mul(ytmp[:, 1:2], ytmp[:, 0:1], -SC))
                ctx.vdr.v += 1
                v.drain().then_inc(ctx.vdr.h, 1)
            mask_rdy = ctx.vdr.v

            @block.scalar
            def _(s, klen=klen, nkc=nkc, mask_rdy=mask_rdy):
                s.wait_ge(ctx.vdr.h, mask_rdy)
                for kc in range(nkc):
                    k0 = kc * 512
                    kw = min(512, klen - k0)
                    s.activation(pbuf[:, k0:k0 + kw], sc_ps[:, k0:k0 + kw], AF.Exp,
                                 bias=ytmp[:, 1:2], scale=SC,
                                 accum_out=ytmp[:, 2 + kc:3 + kc])
                ctx.sdr.v += 1
                s.drain().then_inc(ctx.sdr.h, 1)
            exp_rdy = ctx.sdr.v

            @block.vector
            def _(v, klen=klen, nkc=nkc, exp_rdy=exp_rdy):
                vb = ctx.vch
                def hs(inst):
                    vb.v += 1
                    inst.then_inc(vb.h, 1)
                    v.wait_ge(vb.h, vb.v)
                v.wait_ge(ctx.sdr.h, exp_rdy)
                hs(v.reduce_sum(ytmp[:, 8:9], ytmp[:, 2:2 + nkc], axis=AX.X))
                hs(v.reciprocal(ytmp[:, 9:10], ytmp[:, 8:9]))
                for kc in range(nkc):
                    k0 = kc * 512
                    kw = min(512, klen - k0)
                    hs(v.tensor_scalar(pbuf[:, k0:k0 + kw], pbuf[:, k0:k0 + kw],
                                       ytmp[:, 9:10], None, ALU.mult))
                ctx.vdr.v += 1
                v.drain().then_inc(ctx.vdr.h, 1)
            mods_rdy = ctx.vdr.v

            ngrp = (nkt + 3) // 4
            for grp in range(ngrp):
                n = min(4, nkt - grp * 4)

                @block.tensor
                def _(t, grp=grp, n=n, mods_rdy=mods_rdy, evprev=ctx.vdr.v):
                    t.wait_ge(ctx.vdr.h, mods_rdy)
                    if grp > 0:
                        t.wait_ge(ctx.vdr.h, evprev)
                    for j in range(n):
                        t.transpose(tr_ps[:, j * 128:(j + 1) * 128],
                                    pbuf[:, (grp * 4 + j) * 128:(grp * 4 + j + 1) * 128],
                                    identf[:])
                    ctx.tdr.v += 1
                    t.drain().then_inc(ctx.tdr.h, 1)
                trd = ctx.tdr.v

                @block.vector
                def _(v, grp=grp, n=n, trd=trd):
                    v.wait_ge(ctx.tdr.h, trd)
                    v.tensor_copy(attT[:, grp * 4 * 128: grp * 4 * 128 + n * 128],
                                  tr_ps[:, 0:n * 128])
                    ctx.vdr.v += 1
                    v.drain().then_inc(ctx.vdr.h, 1)
            attT_rdy = ctx.vdr.v

            @block.tensor
            def _(t, h=h, nkt=nkt, attT_rdy=attT_rdy):
                t.wait_ge(ctx.vdr.h, attT_rdy)
                for kt in range(nkt):
                    t.matmul(y_ps, vtm[:, kt * 1024 + h * 64: kt * 1024 + h * 64 + 64],
                             attT[:, kt * 128:(kt + 1) * 128],
                             start=(kt == 0), stop=(kt == nkt - 1))
                ctx.tdr.v += 1
                t.drain().then_inc(ctx.tdr.h, 1)
            pv_rdy = ctx.tdr.v

            @block.scalar
            def _(s, h=h, qt=qt, q0=q0, pv_rdy=pv_rdy):
                s.wait_ge(ctx.tdr.h, pv_rdy)
                b2 = (h % 2) * 64
                fc = h // 2
                s.activation(y_fm[b2:b2 + 64, fc * T + q0: fc * T + q0 + 128],
                             y_ps, AF.Copy)
                ctx.sdr.v += 1
                s.drain().then_inc(ctx.sdr.h, 1)
            prev_yevac = ctx.sdr.v

    fence(ctx, block)
    _emit_out_residual(ctx, block, projw_ap, y_fm, wbig, banks,
                       stream_hbm, schunk_res, C=1024, bias_cols=projb_cols,
                       scale_cols=None, t0=0, Tc=T)
    fence(ctx, block)


def build_layer2_program(parts=("ffn1", "attn", "ffn2"), fused=None, stream_io=None, prefix="", do_copy=True):
    if fused is None:
        nc = bass.Bass("TRN2", target_bir_lowering=False)
    else:
        nc = fused[0]
    if stream_io is None:
        stream_in = nc.dram_tensor(prefix + "stream_in", [D, T], F32, kind="ExternalInput").ap()
        stream_out = nc.dram_tensor(prefix + "stream_out", [D, T], F32, kind="ExternalOutput").ap()
    else:
        stream_in, stream_out = stream_io
    lin0 = nc.dram_tensor(prefix + "lin0", [1024, 1024], F32, kind="ExternalInput").ap()
    swi1 = nc.dram_tensor(prefix + "swi1", [1024, 2048], F32, kind="ExternalInput").ap()
    gatewT = nc.dram_tensor(prefix + "gatewT", [1024, 8], F32, kind="ExternalInput").ap()
    exp13 = nc.dram_tensor(prefix + "exp13", [8, 1024, 2048], F32, kind="ExternalInput")
    exp2 = nc.dram_tensor(prefix + "exp2", [8, 1024, 1024], F32, kind="ExternalInput")
    sh13 = nc.dram_tensor(prefix + "sh13", [1024, 4096], F32, kind="ExternalInput").ap()
    sh2 = nc.dram_tensor(prefix + "sh2", [2048, 1024], F32, kind="ExternalInput").ap()
    lin1 = nc.dram_tensor(prefix + "lin1", [1024, 1024], F32, kind="ExternalInput").ap()
    swi2 = nc.dram_tensor(prefix + "swi2", [1024, 2048], F32, kind="ExternalInput").ap()
    bcols_in = nc.dram_tensor(prefix + "bcols", [128, 48], F32, kind="ExternalInput").ap()
    kqvw = nc.dram_tensor(prefix + "kqvw", [1024, 3072], F32, kind="ExternalInput").ap()
    kqvb_in = nc.dram_tensor(prefix + "kqvb", [128, 16], F32, kind="ExternalInput").ap()
    vb_in = nc.dram_tensor(prefix + "vbias", [1, 1024], F32, kind="ExternalInput").ap()
    projw = nc.dram_tensor(prefix + "projw", [1024, 1024], F32, kind="ExternalInput").ap()
    projb_in = nc.dram_tensor(prefix + "projb", [128, 8], F32, kind="ExternalInput").ap()
    slopes_in = nc.dram_tensor(prefix + "slopes", [1, 16], F32, kind="ExternalInput").ap()
    mtab_in = nc.dram_tensor(prefix + "mtab", [128, 2048], F32, kind="ExternalInput").ap()
    tri_in = nc.dram_tensor(prefix + "tri", [128, 128], F32, kind="ExternalInput").ap()
    identf_in = nc.dram_tensor(prefix + "identf", [128, 128], F32, kind="ExternalInput").ap()
    ones_in = nc.dram_tensor(prefix + "ones", [128, 128], BF16, kind="ExternalInput").ap()
    ccols_in = nc.dram_tensor(prefix + "ccols", [128, len(CONST_VALS)], F32, kind="ExternalInput").ap()

    if fused is None:
        ctx = Ctx(nc)
    else:
        ctx = fused[1]
    les = contextlib.ExitStack()
    def _sb(n, sh, dt):
        return les.enter_context(nc.sbuf_tensor(prefix + n, sh, dt))
    def _ps(n, sh, dt=F32):
        return les.enter_context(nc.psum_tensor(prefix + n, sh, dt))
    bfa = _sb("bfa", [128, 67584], BF16)
    fpa = _sb("fpa", [128, 12832], F32)
    mtab = _sb("mtab_sb", [128, 2048], F32)
    wbig = [_sb(f"wbg{i}", [128, 4096], BF16) for i in range(2)]
    ones_bf = _sb("ones_bf", [128, 128], BF16)
    ccols = _sb("ccols_sb", [128, len(CONST_VALS)], F32)
    identf = _sb("identf_sb", [128, 128], F32)
    trimask = _sb("tri_sb", [128, 128], F32)
    bcols = _sb("bcols_sb", [128, 48], F32)
    kqvb = _sb("kqvb_sb", [128, 16], F32)
    projb = _sb("projb_sb", [128, 8], F32)
    slopes_row = _sb("slopes_sb", [1, 16], BF16)
    ps_all = _ps("ps_all", [128, 4096], F32)
    pbanks = [ps_all[:, i * 512:(i + 1) * 512] for i in range(8)]

    xhat = bfa[:, 0:16384]
    moe_fm = bfa[:, 0:16384]
    h0 = bfa[:, 16384:32768]
    h1 = bfa[:, 32768:49152]
    actbuf = bfa[:, 49152:57344]
    shh = bfa[:, 49152:65536]
    cT_bf = bfa[:, 65536:67584]
    vbias_row = bfa[:, 65536:66560]
    q_fm = bfa[:, 16384:32768]
    k_fm = bfa[:, 32768:49152]
    vtm = bfa[:, 49152:65536]
    y_fm = bfa[:, 0:16384]
    attT = bfa[:, 65536:67584]
    sq_scr = bfa[:, 16384:18432]
    bcast = bfa[:, 18432:22528]
    ab_rows = bfa[:, 22528:26624]

    schunk = fpa[:, 0:2048]
    schunk_res = fpa[:, 0:2048]
    rowbuf = fpa[:, 2048:3584]
    utmp = fpa[:, 3584:4608]
    ytmp = fpa[:, 4608:4640]
    acc = fpa[:, 4640:12832]
    pbuf = acc[:, 0:2048]
    c_all = fpa[:, 2048:2176]
    stats = fpa[:, 2176:2256]
    slope_bc = fpa[:, 2256:2272]

    _bcm = nc.Block() if fused is None else contextlib.nullcontext(fused[2])
    with _bcm as block:
        if fused is None:
            emit_boot(ctx, block)

        @block.gpsimd
        def _(g):
            g.dma_start(out=ones_bf[:], in_=ones_in).then_inc(ctx.sio.h, 16)
            g.dma_start(out=ccols[:], in_=ccols_in).then_inc(ctx.sio.h, 16)
            g.dma_start(out=identf[:], in_=identf_in).then_inc(ctx.sio.h, 16)
            g.dma_start(out=trimask[:], in_=tri_in).then_inc(ctx.sio.h, 16)
            g.dma_start(out=bcols[:], in_=bcols_in).then_inc(ctx.sio.h, 16)
            g.dma_start(out=kqvb[:], in_=kqvb_in).then_inc(ctx.sio.h, 16)
            g.dma_start(out=projb[:], in_=projb_in).then_inc(ctx.sio.h, 16)
            g.dma_start(out=slopes_row[:], in_=slopes_in).then_inc(ctx.sio.h, 16)
            g.dma_start(out=mtab[:], in_=mtab_in).then_inc(ctx.sio.h, 16)
        ctx.sio.v += 144
        pro_io = ctx.sio.v
        for en in ("tensor", "vector", "scalar", "sync"):
            getattr(block, en)(lambda e: e.wait_ge(ctx.sio.h, pro_io))

        if do_copy:
            @block.sync
            def _(sy):
                sy.dma_start(out=stream_out, in_=stream_in).then_inc(ctx.sio.h, 16)
            ctx.sio.v += 16
        fence(ctx, block)

        def do_ffn():
            emit_stream_ln(ctx, block, stream_out, schunk, xhat, sq_scr, rowbuf,
                           ab_rows, bcast, pbanks, ones_bf, ccols)
            fence(ctx, block)
            emit_l2_ffn(ctx, block, xhat, h0, h1, actbuf, shh, moe_fm, acc,
                        lin0, bcols[:, 0:8], swi1, bcols[:, 8:16], bcols[:, 16:24],
                        gatewT, [exp13.ap()[e] for e in range(8)],
                        [exp2.ap()[e] for e in range(8)], sh13, sh2, lin1,
                        bcols[:, 24:32], swi2, bcols[:, 32:40], bcols[:, 40:48],
                        c_all, cT_bf, stats, wbig, pbanks, ps_all, identf,
                        ones_bf, ccols, utmp, stream_out, schunk_res)
            fence(ctx, block)

        if "ffn1" in parts:
            do_ffn()
        if "attn" in parts:
            emit_stream_ln(ctx, block, stream_out, schunk, xhat, sq_scr, rowbuf,
                           ab_rows, bcast, pbanks, ones_bf, ccols)
            fence(ctx, block)

            @block.gpsimd
            def _(g):
                g.dma_start(out=vbias_row[0:1, :], in_=vb_in).then_inc(ctx.sio.h, 16)
            ctx.sio.v += 16
            fence(ctx, block)
            emit_l2_attn(ctx, block, xhat, kqvw, kqvb, vbias_row, slopes_row,
                         mtab, trimask, identf, q_fm, k_fm, vtm, slope_bc, pbuf,
                         attT, ytmp, y_fm, ps_all, wbig, ones_bf, stream_out,
                         schunk_res, projw, projb)
            fence(ctx, block)
        if "ffn2" in parts:
            do_ffn()

        stv = ctx.st.v
        @block.sync
        def _(sy):
            sy.wait_ge(ctx.st.h, stv)

    les.close()
    if fused is None:
        ctx.es.close()
        nc.gpsimd.sem_clear(ctx.clear_rng)
    return nc


def _interleave128(wa, wb):
    C, F = wa.shape
    out = np.empty((C, 2 * F), np.float32)
    for i in range(F // 128):
        out[:, (2 * i) * 128:(2 * i) * 128 + 128] = wa[:, i * 128:(i + 1) * 128]
        out[:, (2 * i + 1) * 128:(2 * i + 1) * 128 + 128] = wb[:, i * 128:(i + 1) * 128]
    return np.ascontiguousarray(out)


def prep_layer2_inputs(ip):
    out = {}
    out["lin0"] = np.ascontiguousarray(ip["L2_ffn_lin0_w"])
    out["swi1"] = _interleave128(ip["L2_ffn_swi1_w1"], ip["L2_ffn_swi1_w2"])
    out["gatewT"] = np.ascontiguousarray(ip["L2_ffn_gate_w"].T)
    out["exp13"] = np.stack([_interleave128(ip["L2_ffn_exp_w1"][e], ip["L2_ffn_exp_w3"][e])
                             for e in range(8)], axis=0)
    out["exp2"] = np.ascontiguousarray(ip["L2_ffn_exp_w2"])
    out["sh13"] = _interleave128(ip["L2_ffn_sh_w1"], ip["L2_ffn_sh_w3"])
    out["sh2"] = np.ascontiguousarray(ip["L2_ffn_sh_w2"])
    out["lin1"] = np.ascontiguousarray(ip["L2_ffn_lin1_w"])
    out["swi2"] = _interleave128(ip["L2_ffn_swi2_w1"], ip["L2_ffn_swi2_w2"])
    bcols = np.zeros((128, 48), np.float32)
    bcols[:, 0:8] = ip["L2_ffn_lin0_b"].reshape(8, 128).T
    bcols[:, 8:16] = ip["L2_ffn_swi1_b1"].reshape(8, 128).T
    bcols[:, 16:24] = ip["L2_ffn_swi1_b2"].reshape(8, 128).T
    bcols[:, 24:32] = ip["L2_ffn_lin1_b"].reshape(8, 128).T
    bcols[:, 32:40] = ip["L2_ffn_swi2_b1"].reshape(8, 128).T
    bcols[:, 40:48] = ip["L2_ffn_swi2_b2"].reshape(8, 128).T
    out["bcols"] = bcols
    out["kqvw"] = np.ascontiguousarray(ip["L2_attn_W_kqv"])
    kqvb = np.zeros((128, 16), np.float32)
    kqvb[:, 0:8] = ip["L2_attn_b_kqv"][0:1024].reshape(8, 128).T
    kqvb[:, 8:16] = ip["L2_attn_b_kqv"][1024:2048].reshape(8, 128).T
    out["kqvb"] = kqvb
    out["vbias"] = np.ascontiguousarray(ip["L2_attn_b_kqv"][2048:3072].reshape(1, 1024))
    out["projw"] = np.ascontiguousarray(ip["L2_attn_W_proj"])
    out["projb"] = np.ascontiguousarray(ip["L2_attn_b_proj"].reshape(8, 128).T)
    out["slopes"] = np.ascontiguousarray(ip["L2_attn_slopes"].reshape(1, 16))
    p = np.arange(128)
    jj = np.arange(2048)
    out["mtab"] = np.ascontiguousarray((jj[None, :] + 127.0 - 2047.0 - p[:, None]).astype(np.float32))
    return out



MAGIC_HALF = 6291456.0


def emit_quant(ctx, block, v, hsv, src, qa, bstat, W):
    nb = W // 16
    s3 = lambda ap: ap.rearrange("p (b k) -> p b k", k=16)
    bmax = bstat[:, 0:nb]
    brec = bstat[:, nb:2 * nb]
    hsv(v.tensor_scalar(qa[:, 0:W], src, -1.0, None, ALU.mult))
    hsv(v.tensor_max(qa[:, 0:W], qa[:, 0:W], src))
    hsv(v.reduce_max(bmax.rearrange("p (b o) -> p b o", o=1), s3(qa[:, 0:W]), axis=AX.X))
    hsv(v.tensor_scalar(bmax, bmax, 1.0 / 6.0, 1e-12, ALU.mult, ALU.max))
    hsv(v.reciprocal(brec, bmax))
    bm3 = bass.AP(brec.tensor, brec.offset, [brec.ap[0], [1, nb], [0, 16]])
    hsv(v.tensor_tensor(s3(qa[:, 0:W]), s3(qa[:, 0:W]), bm3, ALU.mult))
    qa2 = qa[:, W:2 * W]
    hsv(v.tensor_scalar(qa2, qa[:, 0:W], 2.0, MAGIC_HALF, ALU.min, ALU.add))
    hsv(v.tensor_scalar_add(qa2, qa2, -MAGIC_HALF))
    hsv(v.tensor_scalar(qa[:, 2 * W:3 * W], qa[:, 0:W], 2.5, None, ALU.is_gt))
    hsv(v.tensor_add(qa2, qa2, qa[:, 2 * W:3 * W]))
    hsv(v.tensor_scalar(qa[:, 2 * W:3 * W], qa[:, 0:W], 3.5, None, ALU.is_gt))
    hsv(v.tensor_add(qa2, qa2, qa[:, 2 * W:3 * W]))
    hsv(v.tensor_scalar(qa[:, 2 * W:3 * W], qa[:, 0:W], 5.0, 2.0, ALU.is_gt, ALU.mult))
    hsv(v.tensor_add(qa2, qa2, qa[:, 2 * W:3 * W]))
    hsv(v.tensor_scalar(qa[:, 2 * W:3 * W], src, 0.0, None, ALU.is_gt))
    hsv(v.tensor_scalar(qa[:, 2 * W:3 * W], qa[:, 2 * W:3 * W], 2.0, -1.0, ALU.mult, ALU.add))
    hsv(v.tensor_mul(qa2, qa2, qa[:, 2 * W:3 * W]))
    bs3 = bass.AP(bmax.tensor, bmax.offset, [bmax.ap[0], [1, nb], [0, 16]])
    hsv(v.tensor_tensor(s3(src), s3(qa2), bs3, ALU.mult))


def emit_l3_xquant(ctx, block, xfm, nchunks, fptile, qa, bstat, identf, tr_ps, Wc=1024, stride=T, ntt=16, tmbuf=None):
    for tt in range(ntt):
        @block.vector
        def _(v, tt=tt, prevs=ctx.sdr.v):
            vb = ctx.vch
            def hsv(inst):
                vb.v += 1
                inst.then_inc(vb.h, 1)
                v.wait_ge(vb.h, vb.v)
            if tt > 0:
                v.wait_ge(ctx.sdr.h, prevs)
            for k in range(nchunks):
                hsv(v.tensor_copy(fptile[:, k * 128:(k + 1) * 128],
                                  xfm[:, k * stride + tt * 128: k * stride + tt * 128 + 128]))
            ctx.vdr.v += 1
            v.drain().then_inc(ctx.vdr.h, 1)
        up_rdy = ctx.vdr.v

        ntr = nchunks
        for grp in range(0, ntr, 4):
            n = min(4, ntr - grp)
            @block.tensor
            def _(t, grp=grp, n=n, up_rdy=up_rdy, evp=ctx.vdr.v):
                t.wait_ge(ctx.vdr.h, up_rdy)
                if grp > 0:
                    t.wait_ge(ctx.vdr.h, evp)
                for j in range(n):
                    t.transpose(tr_ps[:, j * 128:(j + 1) * 128],
                                fptile[:, (grp + j) * 128:(grp + j + 1) * 128],
                                identf[:])
                ctx.tdr.v += 1
                t.drain().then_inc(ctx.tdr.h, 1)
            trd = ctx.tdr.v

            @block.vector
            def _(v, grp=grp, n=n, trd=trd):
                v.wait_ge(ctx.tdr.h, trd)
                v.tensor_copy(tmbuf[:, grp * 128:(grp + n) * 128], tr_ps[:, 0:n * 128])
                ctx.vdr.v += 1
                v.drain().then_inc(ctx.vdr.h, 1)

        @block.vector
        def _(v, tt=tt):
            vb = ctx.vch
            def hsv(inst):
                vb.v += 1
                inst.then_inc(vb.h, 1)
                v.wait_ge(vb.h, vb.v)
            emit_quant(ctx, block, v, hsv, tmbuf[:, 0:Wc], qa, bstat, Wc)
            ctx.vdr.v += 1
            v.drain().then_inc(ctx.vdr.h, 1)
        q_rdy = ctx.vdr.v

        for grp in range(0, ntr, 4):
            n = min(4, ntr - grp)
            @block.tensor
            def _(t, grp=grp, n=n, q_rdy=q_rdy, evp=ctx.sdr.v):
                t.wait_ge(ctx.vdr.h, q_rdy)
                if grp > 0:
                    t.wait_ge(ctx.sdr.h, evp)
                for j in range(n):
                    t.transpose(tr_ps[:, j * 128:(j + 1) * 128],
                                tmbuf[:, (grp + j) * 128:(grp + j + 1) * 128],
                                identf[:])
                ctx.tdr.v += 1
                t.drain().then_inc(ctx.tdr.h, 1)
            trd = ctx.tdr.v

            @block.scalar
            def _(s, tt=tt, grp=grp, n=n, trd=trd):
                s.wait_ge(ctx.tdr.h, trd)
                for j in range(n):
                    k = grp + j
                    s.activation(xfm[:, k * stride + tt * 128: k * stride + tt * 128 + 128],
                                 tr_ps[:, j * 128:(j + 1) * 128], AF.Copy)
                ctx.sdr.v += 1
                s.drain().then_inc(ctx.sdr.h, 1)


def emit_fp4_wquant_tile(ctx, block, w_nat_ap, f0, Cdim, wld, qa, bstat, identf,
                         tr_ps, slot, slot_off):
    sio0 = ctx.sio.v

    @block.sync
    def _(sy):
        sy.dma_start(out=wld[:, 0:Cdim], in_=w_nat_ap[f0:f0 + 128, :]).then_inc(ctx.sio.h, 16)
    ctx.sio.v += 16

    @block.vector
    def _(v):
        vb = ctx.vch
        def hsv(inst):
            vb.v += 1
            inst.then_inc(vb.h, 1)
            v.wait_ge(vb.h, vb.v)
        v.wait_ge(ctx.sio.h, sio0 + 16)
        emit_quant(ctx, block, v, hsv, wld[:, 0:Cdim], qa, bstat, Cdim)
        ctx.vdr.v += 1
        v.drain().then_inc(ctx.vdr.h, 1)
    q_rdy = ctx.vdr.v

    ntr = Cdim // 128
    for grp in range(0, ntr, 4):
        n = min(4, ntr - grp)
        @block.tensor
        def _(t, grp=grp, n=n, q_rdy=q_rdy, evp=ctx.sdr.v):
            t.wait_ge(ctx.vdr.h, q_rdy)
            if grp > 0:
                t.wait_ge(ctx.sdr.h, evp)
            for j in range(n):
                t.transpose(tr_ps[:, j * 128:(j + 1) * 128],
                            wld[:, (grp + j) * 128:(grp + j + 1) * 128], identf[:])
            ctx.tdr.v += 1
            t.drain().then_inc(ctx.tdr.h, 1)
        trd = ctx.tdr.v

        @block.scalar
        def _(s, grp=grp, n=n, trd=trd):
            s.wait_ge(ctx.tdr.h, trd)
            for j in range(n):
                s.activation(slot[:, slot_off + (grp + j) * 128: slot_off + (grp + j + 1) * 128],
                             tr_ps[:, j * 128:(j + 1) * 128], AF.Copy)
            ctx.sdr.v += 1
            s.drain().then_inc(ctx.sdr.h, 1)


def emit_l3_ffn(ctx, block, xq, gate_nat, up_nat, down_nat, hbuf, out_fm,
                wld, qa, bstat, utmp, slots_gu, slots_d, pbanks, ps_all,
                identf, stream_hbm, schunk_res, fptile):
    tr_ps = ps_all[:, 2048:2560]
    Tc = 1024
    for tch in range(2):
        t0 = tch * Tc
        for fb in range(22):
            slot = slots_gu[0]
            fence(ctx, block, engines=("tensor", "vector", "scalar", "sync"))
            emit_fp4_wquant_tile(ctx, block, gate_nat, fb * 128, 1024, wld, qa,
                                 bstat, identf, tr_ps, slot, 0)
            fence(ctx, block, engines=("tensor", "vector", "scalar", "sync"))
            emit_fp4_wquant_tile(ctx, block, up_nat, fb * 128, 1024, wld, qa,
                                 bstat, identf, tr_ps, slot, 1024)
            fence(ctx, block, engines=("tensor", "vector", "scalar", "sync"))

            @block.tensor
            def _(t, fb=fb, slot=slot, t0=t0):
                for half in range(2):
                    for ti in range(2):
                        psb = pbanks[half * 2 + ti]
                        for cc in range(8):
                            t.matmul(psb[:, 0:512],
                                     slot[:, half * 1024 + cc * 128: half * 1024 + cc * 128 + 128],
                                     xq[:, cc * T + t0 + ti * 512: cc * T + t0 + ti * 512 + 512],
                                     start=(cc == 0), stop=(cc == 7))
                ctx.tdr.v += 1
                t.drain().then_inc(ctx.tdr.h, 1)
            mm_rdy = ctx.tdr.v

            @block.scalar
            def _(s, mm_rdy=mm_rdy):
                s.wait_ge(ctx.tdr.h, mm_rdy)
                for ti in range(2):
                    s.activation(utmp[:, ti * 512:(ti + 1) * 512], pbanks[ti][:, 0:512],
                                 AF.Silu)
                ctx.sdr.v += 1
                s.drain().then_inc(ctx.sdr.h, 1)
            si_rdy = ctx.sdr.v

            @block.vector
            def _(v, fb=fb, si_rdy=si_rdy):
                v.wait_ge(ctx.sdr.h, si_rdy)
                for ti in range(2):
                    v.scalar_tensor_tensor(hbuf[:, fb * Tc + ti * 512: fb * Tc + ti * 512 + 512],
                                           pbanks[2 + ti][:, 0:512], 1.0,
                                           utmp[:, ti * 512:(ti + 1) * 512],
                                           ALU.mult, ALU.mult)
                ctx.vdr.v += 1
                v.drain().then_inc(ctx.vdr.h, 1)
        fence(ctx, block)

        emit_l3_xquant(ctx, block, hbuf, 22, fptile, qa, bstat, identf, tr_ps,
                       Wc=2816, stride=Tc, ntt=8, tmbuf=wld)
        fence(ctx, block)

        for fb in range(8):
            slot = slots_d[0]
            fence(ctx, block, engines=("tensor", "vector", "scalar", "sync"))
            emit_fp4_wquant_tile(ctx, block, down_nat, fb * 128, 2816, wld, qa,
                                 bstat, identf, tr_ps, slot, 0)
            fence(ctx, block, engines=("tensor", "vector", "scalar", "sync"))

            @block.tensor
            def _(t, fb=fb, slot=slot, t0=t0):
                for ti in range(2):
                    psb = pbanks[ti]
                    for cc in range(22):
                        t.matmul(psb[:, 0:512],
                                 slot[:, cc * 128: cc * 128 + 128],
                                 hbuf[:, cc * Tc + ti * 512: cc * Tc + ti * 512 + 512],
                                 start=(cc == 0), stop=(cc == 21))
                ctx.tdr.v += 1
                t.drain().then_inc(ctx.tdr.h, 1)
            mm_rdy = ctx.tdr.v

            @block.scalar
            def _(s, fb=fb, t0=t0, mm_rdy=mm_rdy):
                s.wait_ge(ctx.tdr.h, mm_rdy)
                for ti in range(2):
                    s.activation(out_fm[:, fb * T + t0 + ti * 512: fb * T + t0 + ti * 512 + 512],
                                 pbanks[ti][:, 0:512], AF.Copy)
                ctx.sdr.v += 1
                s.drain().then_inc(ctx.sdr.h, 1)
        fence(ctx, block)
    emit_plain_residual(ctx, block, out_fm, stream_hbm, schunk_res, factor=0.5)
    fence(ctx, block)


def emit_l3_attn(ctx, block, xhat, wdqT_ap, wdkvT_ap, wuq_c, wuk_c, wuv_c,
                 wdq_sb, wuqT_sb, woT_ap, woT_sb, keff_sb, veff_sb, ckv_fm,
                 ckv_tm, q2_fm, q_fm, qlat, ctxs, pbuf, attT, ytmp, fptile,
                 y_fm, ps_all, wbig, identf, trimask, stream_hbm, schunk_res):
    banks = [ps_all[:, i * 512:(i + 1) * 512] for i in range(8)]
    tr_ps = banks[4]
    SC = 0.125

    @block.tensor
    def _(t):
        for cc in range(8):
            t.matmul(banks[0][:, 0:128], wuq_c[:, cc * 128:(cc + 1) * 128],
                     wuk_c[:, cc * 128:(cc + 1) * 128],
                     start=(cc == 0), stop=(cc == 7))
        ctx.tdr.v += 1
        t.drain().then_inc(ctx.tdr.h, 1)
    a_rdy = ctx.tdr.v

    @block.vector
    def _(v, a_rdy=a_rdy):
        v.wait_ge(ctx.tdr.h, a_rdy)
        v.tensor_copy(ctxs[:, 0:128], banks[0][:, 0:128])
        ctx.vdr.v += 1
        v.drain().then_inc(ctx.vdr.h, 1)
    a_ev = ctx.vdr.v

    kb = ctx.tdr.v
    vb0 = ctx.vdr.v

    @block.tensor
    def _(t, a_ev=a_ev, vb0=vb0):
        t.wait_ge(ctx.vdr.h, a_ev)
        for cc in range(8):
            if cc >= 2:
                t.wait_ge(ctx.vdr.h, vb0 + (cc - 2) + 1)
            t.matmul(banks[1 + cc % 2][:, 0:128],
                     wdq_sb[:, cc * 128:(cc + 1) * 128], ctxs[:, 0:128],
                     start=True, stop=True)
            ctx.tdr.v += 1
            t.drain().then_inc(ctx.tdr.h, 1)

    @block.vector
    def _(v, kb=kb):
        for cc in range(8):
            v.wait_ge(ctx.tdr.h, kb + cc + 1)
            v.tensor_copy(keff_sb[:, cc * 128:(cc + 1) * 128], banks[1 + cc % 2][:, 0:128])
            ctx.vdr.v += 1
            v.drain().then_inc(ctx.vdr.h, 1)
    fence(ctx, block)

    def evac_veff(s, fb, ti, ps):
        s.activation(veff_sb[:, fb * 512 + ti * 0: fb * 512 + 512] if False else
                     veff_sb[:, fb * 128: fb * 128 + 128], ps, AF.Copy)
    wb0 = ctx.wdma.v

    @block.gpsimd
    def _(g):
        g.dma_start(out=woT_sb[:, 0:8192].rearrange("c (k m) -> c k m", m=1024),
                    in_=woT_ap.rearrange("(k c) m -> c k m", c=128),
                    ).then_inc(ctx.wdma.h, 16)
    ctx.wdma.v = wb0 + 16

    vmm0 = ctx.tdr.v
    vev0 = ctx.vdr.v

    @block.tensor
    def _(t, wb0=wb0, vev0=vev0):
        t.wait_ge(ctx.wdma.h, wb0 + 16)
        for gi in range(2):
            if gi > 0:
                t.wait_ge(ctx.vdr.h, vev0 + gi)
            for cc in range(8):
                t.matmul(banks[2][:, 0:512], wuv_c[:, cc * 128:(cc + 1) * 128],
                         woT_sb[:, cc * 1024 + gi * 512: cc * 1024 + gi * 512 + 512],
                         start=(cc == 0), stop=(cc == 7))
            ctx.tdr.v += 1
            t.drain().then_inc(ctx.tdr.h, 1)

    @block.vector
    def _(v, vmm0=vmm0):
        for gi in range(2):
            v.wait_ge(ctx.tdr.h, vmm0 + gi + 1)
            v.tensor_copy(veff_sb[:, gi * 512:(gi + 1) * 512], banks[2][:, 0:512])
            ctx.vdr.v += 1
            v.drain().then_inc(ctx.vdr.h, 1)
    fence(ctx, block)

    def evac_ckv(s, fb, ti, ps):
        s.activation(ckv_fm[:, ti * 512:(ti + 1) * 512], ps, AF.Copy)
    emit_linear_fm2(ctx, block, 1024, 128, wdkvT_ap,
                    lambda cc: xhat[:, cc * T:(cc + 1) * T], evac_ckv,
                    wbig, banks, Tloc=T, fb_per_dma=1)
    fence(ctx, block)

    def evac_q2(s, fb, ti, ps):
        s.activation(q2_fm[:, ti * 512:(ti + 1) * 512], ps, AF.Copy)
    emit_linear_fm2(ctx, block, 1024, 128, wdqT_ap,
                    lambda cc: xhat[:, cc * T:(cc + 1) * T], evac_q2,
                    wbig, banks, Tloc=T, fb_per_dma=1)
    fence(ctx, block)

    for grp in range(4):
        @block.vector
        def _(v, grp=grp, prevs=ctx.vdr.v):
            v.tensor_copy(fptile[:, 0:512], ckv_fm[:, grp * 512:(grp + 1) * 512])
            ctx.vdr.v += 1
            v.drain().then_inc(ctx.vdr.h, 1)
        up_rdy = ctx.vdr.v

        @block.tensor
        def _(t, grp=grp, up_rdy=up_rdy):
            t.wait_ge(ctx.vdr.h, up_rdy)
            for j in range(4):
                t.transpose(tr_ps[:, j * 128:(j + 1) * 128],
                            fptile[:, j * 128:(j + 1) * 128], identf[:])
            ctx.tdr.v += 1
            t.drain().then_inc(ctx.tdr.h, 1)
        trd = ctx.tdr.v

        @block.vector
        def _(v, grp=grp, trd=trd):
            v.wait_ge(ctx.tdr.h, trd)
            v.tensor_copy(ckv_tm[:, grp * 512:(grp + 1) * 512], tr_ps[:, 0:512])
            ctx.vdr.v += 1
            v.drain().then_inc(ctx.vdr.h, 1)
    fence(ctx, block)

    qmm = ctx.tdr.v
    qse = ctx.sdr.v

    @block.tensor
    def _(t, qse=qse):
        n = 0
        for fc in range(8):
            for ti in range(4):
                psb = banks[n % 2]
                if n >= 2:
                    t.wait_ge(ctx.sdr.h, qse + (n - 2) + 1)
                t.matmul(psb[:, 0:512], wuqT_sb[:, fc * 128:(fc + 1) * 128],
                         q2_fm[:, ti * 512:(ti + 1) * 512], start=True, stop=True)
                ctx.tdr.v += 1
                t.drain().then_inc(ctx.tdr.h, 1)
                n += 1

    @block.scalar
    def _(s, qmm=qmm):
        n = 0
        for fc in range(8):
            for ti in range(4):
                s.wait_ge(ctx.tdr.h, qmm + n + 1)
                s.activation(q_fm[:, fc * T + ti * 512: fc * T + ti * 512 + 512],
                             banks[n % 2][:, 0:512], AF.Copy)
                n += 1
                ctx.sdr.v += 1
                s.drain().then_inc(ctx.sdr.h, 1)
    fence(ctx, block)

    sc_ps = ps_all[:, 0:2048]
    prev_yevac = ctx.sdr.v
    for h in range(16):
        b = (h % 2) * 64
        hc = h // 2
        qlat_mm = ctx.tdr.v
        qlse = ctx.sdr.v

        @block.tensor
        def _(t, b=b, hc=hc, prev=prev_yevac, qlse=qlse):
            t.wait_ge(ctx.sdr.h, prev)
            for ti in range(4):
                if ti >= 2:
                    t.wait_ge(ctx.sdr.h, qlse + (ti - 2) + 1)
                t.matmul(banks[4 + ti % 2][:, 0:512],
                         keff_sb[b:b + 64, hc * 128:(hc + 1) * 128],
                         q_fm[b:b + 64, hc * T + ti * 512: hc * T + ti * 512 + 512],
                         start=True, stop=True)
                ctx.tdr.v += 1
                t.drain().then_inc(ctx.tdr.h, 1)

        @block.scalar
        def _(s, qlat_mm=qlat_mm):
            for ti in range(4):
                s.wait_ge(ctx.tdr.h, qlat_mm + ti + 1)
                s.activation(qlat[:, ti * 512:(ti + 1) * 512],
                             banks[4 + ti % 2][:, 0:512], AF.Copy)
                ctx.sdr.v += 1
                s.drain().then_inc(ctx.sdr.h, 1)
        qlat_rdy = ctx.sdr.v

        for qt in range(16):
            q0 = qt * 128
            klen = q0 + 128
            nkc = (klen + 511) // 512
            nkt = klen // 128

            @block.tensor
            def _(t, q0=q0, klen=klen, nkc=nkc, prev=prev_yevac, qlat_rdy=qlat_rdy):
                t.wait_ge(ctx.sdr.h, max(prev, qlat_rdy))
                for kc in range(nkc):
                    k0 = kc * 512
                    kw = min(512, klen - k0)
                    t.matmul(sc_ps[:, k0:k0 + kw], qlat[:, q0:q0 + 128],
                             ckv_fm[:, k0:k0 + kw], start=True, stop=True)
                ctx.tdr.v += 1
                t.drain().then_inc(ctx.tdr.h, 1)
            sc_rdy = ctx.tdr.v

            @block.vector
            def _(v, q0=q0, klen=klen, sc_rdy=sc_rdy):
                vb = ctx.vch
                def hs(inst):
                    vb.v += 1
                    inst.then_inc(vb.h, 1)
                    v.wait_ge(vb.h, vb.v)
                v.wait_ge(ctx.tdr.h, sc_rdy)
                hs(v.tensor_add(sc_ps[:, q0:q0 + 128], sc_ps[:, q0:q0 + 128], trimask[:]))
                hs(v.reduce_max(ytmp[:, 0:1], sc_ps[:, 0:klen], axis=AX.X))
                hs(v.tensor_scalar_mul(ytmp[:, 1:2], ytmp[:, 0:1], -SC))
                ctx.vdr.v += 1
                v.drain().then_inc(ctx.vdr.h, 1)
            mask_rdy = ctx.vdr.v

            @block.scalar
            def _(s, klen=klen, nkc=nkc, mask_rdy=mask_rdy):
                s.wait_ge(ctx.vdr.h, mask_rdy)
                for kc in range(nkc):
                    k0 = kc * 512
                    kw = min(512, klen - k0)
                    s.activation(pbuf[:, k0:k0 + kw], sc_ps[:, k0:k0 + kw], AF.Exp,
                                 bias=ytmp[:, 1:2], scale=SC,
                                 accum_out=ytmp[:, 2 + kc:3 + kc])
                ctx.sdr.v += 1
                s.drain().then_inc(ctx.sdr.h, 1)
            exp_rdy = ctx.sdr.v

            @block.vector
            def _(v, klen=klen, nkc=nkc, exp_rdy=exp_rdy):
                vb = ctx.vch
                def hs(inst):
                    vb.v += 1
                    inst.then_inc(vb.h, 1)
                    v.wait_ge(vb.h, vb.v)
                v.wait_ge(ctx.sdr.h, exp_rdy)
                hs(v.reduce_sum(ytmp[:, 8:9], ytmp[:, 2:2 + nkc], axis=AX.X))
                hs(v.reciprocal(ytmp[:, 9:10], ytmp[:, 8:9]))
                for kc in range(nkc):
                    k0 = kc * 512
                    kw = min(512, klen - k0)
                    hs(v.tensor_scalar(pbuf[:, k0:k0 + kw], pbuf[:, k0:k0 + kw],
                                       ytmp[:, 9:10], None, ALU.mult))
                ctx.vdr.v += 1
                v.drain().then_inc(ctx.vdr.h, 1)
            mods_rdy = ctx.vdr.v

            ngrp = (nkt + 3) // 4
            for grp in range(ngrp):
                n = min(4, nkt - grp * 4)

                @block.tensor
                def _(t, grp=grp, n=n, mods_rdy=mods_rdy, evprev=ctx.vdr.v):
                    t.wait_ge(ctx.vdr.h, mods_rdy)
                    if grp > 0:
                        t.wait_ge(ctx.vdr.h, evprev)
                    for j in range(n):
                        t.transpose(tr_ps[:, j * 128:(j + 1) * 128],
                                    pbuf[:, (grp * 4 + j) * 128:(grp * 4 + j + 1) * 128],
                                    identf[:])
                    ctx.tdr.v += 1
                    t.drain().then_inc(ctx.tdr.h, 1)
                trd = ctx.tdr.v

                @block.vector
                def _(v, grp=grp, n=n, trd=trd):
                    v.wait_ge(ctx.tdr.h, trd)
                    v.tensor_copy(attT[:, grp * 4 * 128: grp * 4 * 128 + n * 128],
                                  tr_ps[:, 0:n * 128])
                    ctx.vdr.v += 1
                    v.drain().then_inc(ctx.vdr.h, 1)
            attT_rdy = ctx.vdr.v

            @block.tensor
            def _(t, nkt=nkt, attT_rdy=attT_rdy):
                t.wait_ge(ctx.vdr.h, attT_rdy)
                for kt in range(nkt):
                    t.matmul(banks[5][:, 0:128], ckv_tm[:, kt * 128:(kt + 1) * 128],
                             attT[:, kt * 128:(kt + 1) * 128],
                             start=(kt == 0), stop=(kt == nkt - 1))
                ctx.tdr.v += 1
                t.drain().then_inc(ctx.tdr.h, 1)
            ctx_rdy = ctx.tdr.v

            @block.vector
            def _(v, ctx_rdy=ctx_rdy):
                v.wait_ge(ctx.tdr.h, ctx_rdy)
                v.tensor_copy(ctxs[:, 128:256], banks[5][:, 0:128])
                ctx.vdr.v += 1
                v.drain().then_inc(ctx.vdr.h, 1)
            ctxs_rdy = ctx.vdr.v

            @block.tensor
            def _(t, h=h, ctxs_rdy=ctxs_rdy):
                t.wait_ge(ctx.vdr.h, ctxs_rdy)
                t.matmul(banks[6][0:64, 0:128], veff_sb[:, h * 64:(h + 1) * 64],
                         ctxs[:, 128:256], start=True, stop=True)
                ctx.tdr.v += 1
                t.drain().then_inc(ctx.tdr.h, 1)
            y_rdy = ctx.tdr.v

            @block.scalar
            def _(s, h=h, q0=q0, y_rdy=y_rdy):
                s.wait_ge(ctx.tdr.h, y_rdy)
                b2 = (h % 2) * 64
                fc = h // 2
                s.activation(y_fm[b2:b2 + 64, fc * T + q0: fc * T + q0 + 128],
                             banks[6][0:64, 0:128], AF.Copy)
                ctx.sdr.v += 1
                s.drain().then_inc(ctx.sdr.h, 1)
            prev_yevac = ctx.sdr.v

    fence(ctx, block)
    emit_plain_residual(ctx, block, y_fm, stream_hbm, schunk_res, factor=1.0)
    fence(ctx, block)


def build_layer3_program(parts=("ffn1", "attn", "ffn2"), fused=None, stream_io=None, prefix="", do_copy=True):
    if fused is None:
        nc = bass.Bass("TRN2", target_bir_lowering=False)
    else:
        nc = fused[0]
    if stream_io is None:
        stream_in = nc.dram_tensor(prefix + "stream_in", [D, T], F32, kind="ExternalInput").ap()
        stream_out = nc.dram_tensor(prefix + "stream_out", [D, T], F32, kind="ExternalOutput").ap()
    else:
        stream_in, stream_out = stream_io
    gate_nat = nc.dram_tensor(prefix + "gate_nat", [2816, 1024], F32, kind="ExternalInput").ap()
    up_nat = nc.dram_tensor(prefix + "up_nat", [2816, 1024], F32, kind="ExternalInput").ap()
    down_nat = nc.dram_tensor(prefix + "down_nat", [1024, 2816], F32, kind="ExternalInput").ap()
    wdqT = nc.dram_tensor(prefix + "wdqT", [1024, 128], F32, kind="ExternalInput").ap()
    wdkvT = nc.dram_tensor(prefix + "wdkvT", [1024, 128], F32, kind="ExternalInput").ap()
    wuqc_in = nc.dram_tensor(prefix + "wuqc", [128, 1024], F32, kind="ExternalInput").ap()
    wukc_in = nc.dram_tensor(prefix + "wukc", [128, 1024], F32, kind="ExternalInput").ap()
    wuvc_in = nc.dram_tensor(prefix + "wuvc", [128, 1024], F32, kind="ExternalInput").ap()
    wdq_in = nc.dram_tensor(prefix + "wdq", [128, 1024], F32, kind="ExternalInput").ap()
    wuqT_in = nc.dram_tensor(prefix + "wuqT", [128, 1024], F32, kind="ExternalInput").ap()
    woT = nc.dram_tensor(prefix + "woT", [1024, 1024], F32, kind="ExternalInput").ap()
    tri_in = nc.dram_tensor(prefix + "tri", [128, 128], F32, kind="ExternalInput").ap()
    identf_in = nc.dram_tensor(prefix + "identf", [128, 128], F32, kind="ExternalInput").ap()
    ones_in = nc.dram_tensor(prefix + "ones", [128, 128], BF16, kind="ExternalInput").ap()
    ccols_in = nc.dram_tensor(prefix + "ccols", [128, len(CONST_VALS)], F32, kind="ExternalInput").ap()

    if fused is None:
        ctx = Ctx(nc)
    else:
        ctx = fused[1]
    les = contextlib.ExitStack()
    def _sb(n, sh, dt):
        return les.enter_context(nc.sbuf_tensor(prefix + n, sh, dt))
    def _ps(n, sh, dt=F32):
        return les.enter_context(nc.psum_tensor(prefix + n, sh, dt))
    bfa = _sb("bfa", [128, 66816], BF16)
    fpa = _sb("fpa", [128, 16480], F32)
    wbig = [_sb(f"wbg{i}", [128, 1024], BF16) for i in range(2)]
    ones_bf = _sb("ones_bf", [128, 128], BF16)
    ccols = _sb("ccols_sb", [128, len(CONST_VALS)], F32)
    identf = _sb("identf_sb", [128, 128], F32)
    trimask = _sb("tri_sb", [128, 128], F32)
    ps_all = _ps("ps_all", [128, 4096], F32)
    pbanks = [ps_all[:, i * 512:(i + 1) * 512] for i in range(8)]

    xhat = bfa[:, 0:16384]
    hbuf = bfa[:, 16384:38912]
    out_fm = bfa[:, 38912:55296]
    slots_gu = [bfa[:, 55296:57344]]
    slots_d = [bfa[:, 57344:60160]]
    q_fm = bfa[:, 16384:32768]
    y_fm = bfa[:, 32768:49152]
    ckv_fm = bfa[:, 49152:51200]
    ckv_tm = bfa[:, 51200:53248]
    q2_fm = bfa[:, 53248:55296]
    qlat = bfa[:, 55296:57344]
    attT = bfa[:, 57344:59392]
    keff_sb = bfa[:, 59392:60416]
    veff_sb = bfa[:, 60416:61440]
    ctxs = bfa[:, 61440:61696]
    wuq_c = bfa[:, 61696:62720]
    wuk_c = bfa[:, 62720:63744]
    wuv_c = bfa[:, 63744:64768]
    wdq_sb = bfa[:, 64768:65792]
    wuqT_sb = bfa[:, 65792:66816]
    sq_scr = bfa[:, 16384:18432]
    bcast = bfa[:, 18432:22528]
    ab_rows = bfa[:, 22528:26624]
    qa = fpa[:, 0:8448]
    pbuf = fpa[:, 0:2048]
    wld = fpa[:, 8448:11264]
    fptile = fpa[:, 11264:14080]
    bstat = fpa[:, 14080:14432]
    schunk = fpa[:, 14432:16480]
    schunk_res = fpa[:, 14432:16480]
    rowbuf = fpa[:, 0:1536]
    utmp = fpa[:, 11264:12288]
    ytmp = fpa[:, 14080:14112]

    _bcm = nc.Block() if fused is None else contextlib.nullcontext(fused[2])
    with _bcm as block:
        if fused is None:
            emit_boot(ctx, block)

        @block.gpsimd
        def _(g):
            g.dma_start(out=ones_bf[:], in_=ones_in).then_inc(ctx.sio.h, 16)
            g.dma_start(out=ccols[:], in_=ccols_in).then_inc(ctx.sio.h, 16)
            g.dma_start(out=identf[:], in_=identf_in).then_inc(ctx.sio.h, 16)
            g.dma_start(out=trimask[:], in_=tri_in).then_inc(ctx.sio.h, 16)
        ctx.sio.v += 64
        pro_io = ctx.sio.v
        for en in ("tensor", "vector", "scalar", "sync"):
            getattr(block, en)(lambda e: e.wait_ge(ctx.sio.h, pro_io))

        if do_copy:
            @block.sync
            def _(sy):
                sy.dma_start(out=stream_out, in_=stream_in).then_inc(ctx.sio.h, 16)
            ctx.sio.v += 16
        fence(ctx, block)

        def do_ffn():
            emit_stream_ln(ctx, block, stream_out, schunk, xhat, sq_scr, rowbuf,
                           ab_rows, bcast, pbanks, ones_bf, ccols)
            fence(ctx, block)
            emit_l3_xquant(ctx, block, xhat, 8, fptile, qa, bstat, identf,
                           ps_all[:, 2048:2560], Wc=1024, stride=T, ntt=16, tmbuf=wld)
            fence(ctx, block)
            emit_l3_ffn(ctx, block, xhat, gate_nat, up_nat, down_nat, hbuf,
                        out_fm, wld, qa, bstat, utmp, slots_gu, slots_d,
                        pbanks, ps_all, identf, stream_out, schunk_res, fptile)
            fence(ctx, block)

        if "ffn1" in parts:
            do_ffn()
        if "attn" in parts:
            emit_stream_ln(ctx, block, stream_out, schunk, xhat, sq_scr, rowbuf,
                           ab_rows, bcast, pbanks, ones_bf, ccols)
            fence(ctx, block)

            @block.gpsimd
            def _(g):
                g.dma_start(out=wuq_c[:], in_=wuqc_in).then_inc(ctx.sio.h, 16)
                g.dma_start(out=wuk_c[:], in_=wukc_in).then_inc(ctx.sio.h, 16)
                g.dma_start(out=wuv_c[:], in_=wuvc_in).then_inc(ctx.sio.h, 16)
                g.dma_start(out=wdq_sb[:], in_=wdq_in).then_inc(ctx.sio.h, 16)
                g.dma_start(out=wuqT_sb[:], in_=wuqT_in).then_inc(ctx.sio.h, 16)
            ctx.sio.v += 80
            fence(ctx, block)
            emit_l3_attn(ctx, block, xhat, wdqT, wdkvT, wuq_c, wuk_c, wuv_c,
                         wdq_sb, wuqT_sb, woT, bfa[:, 16384:24576], keff_sb,
                         veff_sb, ckv_fm, ckv_tm, q2_fm, q_fm, qlat, ctxs,
                         pbuf, attT, ytmp, fptile, y_fm, ps_all, wbig, identf,
                         trimask, stream_out, schunk_res)
            fence(ctx, block)
        if "ffn2" in parts:
            do_ffn()

        stv = ctx.st.v
        @block.sync
        def _(sy):
            sy.wait_ge(ctx.st.h, stv)

    les.close()
    if fused is None:
        ctx.es.close()
        nc.gpsimd.sem_clear(ctx.clear_rng)
    return nc


def prep_layer3_inputs(ip):
    out = {}
    out["gate_nat"] = np.ascontiguousarray(ip["L3_ffn_w_gate"])
    out["up_nat"] = np.ascontiguousarray(ip["L3_ffn_w_up"])
    out["down_nat"] = np.ascontiguousarray(ip["L3_ffn_w_down"])
    out["wdqT"] = np.ascontiguousarray(ip["L3_attn_W_dq"].T)
    out["wdkvT"] = np.ascontiguousarray(ip["L3_attn_W_dkv"].T)
    chunkcat = lambda w: np.ascontiguousarray(
        np.concatenate([w[i * 128:(i + 1) * 128, :] for i in range(8)], axis=1))
    out["wuqc"] = chunkcat(ip["L3_attn_W_uq"])
    out["wukc"] = chunkcat(ip["L3_attn_W_uk"])
    out["wuvc"] = chunkcat(ip["L3_attn_W_uv"])
    out["wdq"] = np.ascontiguousarray(ip["L3_attn_W_dq"])
    out["wuqT"] = np.ascontiguousarray(ip["L3_attn_W_uq"].T)
    out["woT"] = np.ascontiguousarray(ip["L3_attn_W_o"].T)
    return out


def build_fused_program():
    nc = bass.Bass("TRN2", target_bir_lowering=False)
    stream_in = nc.dram_tensor("stream_in", [D, T], F32, kind="ExternalInput").ap()
    stream_out = nc.dram_tensor("stream_out", [D, T], F32, kind="ExternalOutput").ap()
    ctx = Ctx(nc)
    with nc.Block() as block:
        emit_boot(ctx, block)
        fused = (nc, ctx, block)
        sio = (stream_in, stream_out)
        build_layer0_program(parts=("ffn1", "attn", "ffn2"), fused=fused,
                             stream_io=sio, prefix="l0_", do_copy=True)
        fence(ctx, block)
        build_layer1_program(parts=("ffn1", "attn", "ffn2"), fused=fused,
                             stream_io=sio, prefix="l1_", do_copy=False)
        fence(ctx, block)
        build_layer2_program(parts=("ffn1", "attn", "ffn2"), fused=fused,
                             stream_io=sio, prefix="l2_", do_copy=False)
        fence(ctx, block)
        build_layer3_program(parts=("ffn1", "attn", "ffn2"), fused=fused,
                             stream_io=sio, prefix="l3_", do_copy=False)
        stv = ctx.st.v

        @block.sync
        def _(sy):
            sy.wait_ge(ctx.st.h, stv)

    ctx.es.close()
    nc.gpsimd.sem_clear(ctx.clear_rng)
    return nc


def kernel_fused(inputs):
    consts = host_constants()
    x = np.ascontiguousarray(inputs["L0_attn_x"])
    streams = [np.ascontiguousarray(x[b].T) for b in range(NCORES)]
    base = {"ones": consts["c_ones"], "ccols": consts["c_cols"],
            "tri": consts["c_trimask"], "identf": consts["c_identf"]}
    preps = [("l0_", prep_layer0_inputs(inputs)), ("l1_", prep_layer1_inputs(inputs)),
             ("l2_", prep_layer2_inputs(inputs)), ("l3_", prep_layer3_inputs(inputs))]
    nc = build_fused_program()
    in_maps = []
    for b in range(NCORES):
        m = {"stream_in": streams[b]}
        for pref, d in preps:
            for k, v in d.items():
                m[pref + k] = v
            for k in ("ones", "ccols", "identf", "tri"):
                m[pref + k] = base[k]
        in_maps.append(m)
    res = run_bass_kernel_spmd(nc, in_maps, core_ids=list(range(NCORES)))
    out = np.stack([np.ascontiguousarray(res.results[b]["stream_out"].T)
                    for b in range(NCORES)], axis=0)
    return out.astype(np.float32)
